# version 12
# baseline (speedup 1.0000x reference)
import numpy as np
import ml_dtypes
from contextlib import ExitStack

import concourse.bass as bass
import concourse.mybir as mybir
from concourse.bass_utils import run_bass_kernel_spmd

F32 = mybir.dt.float32
BF16 = mybir.dt.bfloat16
AF = mybir.ActivationFunctionType
ALU = mybir.AluOpType
AX = mybir.AxisListType.X
BT = ml_dtypes.bfloat16

D = 1024
NT = 8
T = 2048
TF = 4096
CH = 512
H = 16
LN_EPS = 1e-5

CONST_SPEC = [
    ("ident_bf", [128, 128], BF16),
    ("ident2_bf", [128, 64], BF16),
    ("ident_f32", [128, 128], F32),
    ("ones_col_bf", [128, 1], BF16),
    ("ones_row_bf", [1, 128], BF16),
    ("ones_row_f32", [1, 128], F32),
    ("sel_lo", [1, 128], BF16),
    ("sel_hi", [1, 128], BF16),
]


B_CONST_SPEC = [
    ("sel9", [9, 1024], BF16),
]


def host_constants():
    c = {}
    s9 = np.zeros((9, 1024), dtype=BT)
    for e in range(8):
        s9[e + 1, e * 128:(e + 1) * 128] = 1
    c["sel9"] = s9
    c["ident_bf"] = np.eye(128, dtype=BT)
    c["ident2_bf"] = np.tile(np.eye(64, dtype=BT), (2, 1))
    c["ident_f32"] = np.eye(128, dtype=np.float32)
    c["ones_col_bf"] = np.ones((128, 1), dtype=BT)
    c["ones_row_bf"] = np.ones((1, 128), dtype=BT)
    c["ones_row_f32"] = np.ones((1, 128), dtype=np.float32)
    c["sel_lo"] = np.concatenate([np.ones((1, 64), BT), np.zeros((1, 64), BT)], 1)
    c["sel_hi"] = np.concatenate([np.zeros((1, 64), BT), np.ones((1, 64), BT)], 1)
    return c


def declare_constants(nc, names=None):
    return {name: nc.dram_tensor(name, shape, dt, kind="ExternalInput").ap()
            for name, shape, dt in CONST_SPEC if names is None or name in names}


def tile(buf, j, width, sl=None):
    if sl is None:
        return buf[:, j * width:(j + 1) * width]
    a, b = sl
    return buf[:, j * width + a: j * width + b]



def dma_wtiles(g, dst_sb, src_dram, ntile, sem=None):
    C = src_dram.shape[1]
    ins = g.dma_start(out=dst_sb.rearrange("p (j c) -> p j c", j=ntile),
                      in_=src_dram.rearrange("(j p) c -> p j c", p=128))
    if sem is not None:
        ins.then_inc(sem, 16)
    return ins

def build_A(upto=3):
    nc = bass.Bass("TRN2", target_bir_lowering=False)
    caps = declare_constants(nc)
    x_in = nc.dram_tensor("xT_full", [D, TF], F32, kind="ExternalInput").ap()
    wq_in = nc.dram_tensor("wq", [D, D], F32, kind="ExternalInput").ap()
    wk_in = nc.dram_tensor("wk", [D, D], F32, kind="ExternalInput").ap()
    wo_in = nc.dram_tensor("wo", [D, D], F32, kind="ExternalInput").ap()
    bo_in = nc.dram_tensor("bo", [128, 8], F32, kind="ExternalInput").ap()
    y_out = nc.dram_tensor("yT", [D, T], F32, kind="ExternalOutput").ap()
    emit_A(nc, caps, x_in, wq_in, wk_in, wo_in, bo_in, y_out, upto=upto)
    return nc


A_CONSTS = ("ones_col_bf", "ones_row_bf", "sel_lo", "sel_hi")


def emit_A(nc, caps, x_in, wq_in, wk_in, wo_in, bo_in, y_out, upto=3, P=""):
    v_dram = nc.dram_tensor(P + "v_scr", [TF, D], BF16).ap()

    NCF = TF // CH
    NCO = T // CH
    NKT = TF // 128
    NHP = H // 2

    kq_groups_in_chunk = [24 if ci < NCO else 16 for ci in range(NCF)]
    kq_total = sum(kq_groups_in_chunk)

    with ExitStack() as ctx:
        def sb(name, shape, dt):
            return ctx.enter_context(nc.sbuf_tensor(P + name, shape, dt))
        def ps(name, shape, dt):
            return ctx.enter_context(nc.psum_tensor(P + name, shape, dt))
        def sem(name):
            return ctx.enter_context(nc.semaphore(P + name))

        s_lw = sem("s_lw");  s_ww = sem("s_ww");  s_ld = sem("s_ld");  s_xf = sem("s_xf")
        s_sq = sem("s_sq");  s_stats = sem("s_stats"); s_mu = sem("s_mu")
        s_rs = sem("s_rs");  s_pr = sem("s_pr");  s_bc = sem("s_bc")
        s_nm = sem("s_nm");  s_kq = sem("s_kq");  s_ev = sem("s_ev")
        s_vs = sem("s_vs");  s_vh = sem("s_vh")
        s_s = sem("s_s");    s_e = sem("s_e");    s_po = sem("s_po")
        s_rcf = sem("s_rcf"); s_rb = sem("s_rb"); s_fin = sem("s_fin")
        s_wo = sem("s_wo");  s_f3 = sem("s_f3");  s_st = sem("s_st")
        sv_self = sem("sv_self"); ss_self = sem("ss_self")

        xbf = sb("xbf", [128, NT * CH], BF16)
        xln = sb("xln", [128, NT * CH], BF16)
        sq = xln
        lnsm = sb("lnsm", [1, 4 * CH], F32)
        pair_bf = sb("pair_bf", [1, 2 * CH], BF16)
        rbuf = sb("rbuf", [128, 2 * CH], BF16)
        wqk = sb("wqk", [128, 2 * NT * 1024], BF16)
        wo_sb = sb("wo_sb", [128, NT * 1024], BF16)
        bo_sb = sb("bo_sb", [128, 8], F32)
        KT = sb("KT", [128, NT * TF], BF16)
        QT = sb("QT", [128, NT * T], BF16)
        OA = wqk
        Vc = sb("Vc", [128, 2 * 1024], BF16)
        Vhp = sb("Vhp", [128, NKT * 128], BF16)
        Pt = sb("Pt", [128, 4 * CH], BF16)
        rcbf = sb("rcbf", [1, 2 * CH], BF16)
        rcb = sb("rcb", [128, CH], BF16)
        xf32 = sb("xf32", [128, NT * CH], F32)
        eps1 = sb("eps1", [1, 1], F32)
        cb = {n: sb("c_" + n, shp, dt) for n, shp, dt in CONST_SPEC if n in A_CONSTS}

        pB = ps("pB", [128, 2 * CH], F32)
        pX = ps("pX", [128, 2 * CH], F32)
        pO = ps("pO", [128, 2 * CH], F32)
        pD2 = ps("pD2", [128, 2 * CH], F32)
        pS = pO

        block = ctx.enter_context(nc.Block())

        n_lw = len(A_CONSTS) + 1


        @block.sync
        def _(g):
            for name in A_CONSTS:
                g.dma_start(out=cb[name][:, :], in_=caps[name]).then_inc(s_lw, 16)
            g.dma_start(out=bo_sb[:, :], in_=bo_in).then_inc(s_lw, 16)
            ev_c = 0
            for ci in range(NCF):
                base_ev = ev_c
                ev_c += kq_groups_in_chunk[ci]
                for tt in range(4):
                    need = base_ev + 8 + 2 * tt + 2
                    g.wait_ge(s_ev, need)
                    vs_i = 4 * ci + tt
                    if vs_i >= 1:
                        g.wait_ge(s_vs, 16 * vs_i)
                    g.dma_start(out=v_dram[ci * CH + tt * 128: ci * CH + tt * 128 + 128, :],
                                in_=Vc[:, (tt % 2) * 1024:(tt % 2) * 1024 + 1024]
                                ).then_inc(s_vs, 16)
            if upto >= 3:
                for oc in range(NCO):
                    g.wait_ge(s_f3, NT * (oc + 1))
                    if oc >= 1:
                        g.wait_ge(s_st, 16 * oc)
                    g.dma_start(out=y_out.rearrange("(j p) c -> p j c", p=128)[:, :, oc * CH:(oc + 1) * CH],
                                in_=xf32.rearrange("p (j c) -> p j c", j=NT)
                                ).then_inc(s_st, 16)
                g.wait_ge(s_st, 16 * NCO)

        @block.gpsimd
        def _(g):
            dma_wtiles(g, wqk[:, 0:NT * 1024], wq_in, NT, s_ww)
            dma_wtiles(g, wqk[:, NT * 1024:2 * NT * 1024], wk_in, NT, s_ww)
            dma_wtiles(g, wo_sb[:, :], wo_in, NT, s_ww)
            for ci in range(NCF):
                if ci >= 1:
                    g.wait_ge(s_nm, ci)
                for j in range(NT):
                    g.dma_start(out=tile(xbf, j, CH),
                                in_=x_in[128 * j:128 * (j + 1), ci * CH:(ci + 1) * CH]
                                ).then_inc(s_ld, 16)
            if upto >= 2:
                for hp in range(NHP):
                    if hp == 0:
                        g.wait_ge(s_vs, 16 * 4 * NCF)
                    else:
                        g.wait_ge(s_po, 32 * 4 * hp)
                    g.dma_start(out=Vhp.rearrange("p (kt c) -> p kt c", c=128),
                                in_=v_dram[:, hp * 128:(hp + 1) * 128]
                                .rearrange("(kt p) c -> p kt c", p=128)
                                ).then_inc(s_vh, 16)
            if upto >= 3:
                for oc in range(NCO):
                    if oc >= 1:
                        g.wait_ge(s_st, 16 * oc)
                    for j in range(NT):
                        g.dma_start(out=tile(xf32, j, CH),
                                    in_=x_in[128 * j:128 * (j + 1), oc * CH:(oc + 1) * CH]
                                    ).then_inc(s_xf, 16)
            elif upto == 2:
                g.wait_ge(s_fin, NHP * NCO)
                g.dma_start(out=y_out.rearrange("(j p) c -> p j c", p=128),
                            in_=OA.rearrange("p (j c) -> p j c", j=NT)[:, 0:NT, 0:T]
                            ).then_inc(s_xf, 16)
                g.wait_ge(s_xf, 16)
            elif upto == 1:
                g.wait_ge(s_ev, kq_total)
                g.dma_start(out=y_out.rearrange("(j p) c -> p j c", p=128),
                            in_=QT.rearrange("p (j c) -> p j c", j=NT)).then_inc(s_xf, 16)
                g.wait_ge(s_xf, 16)

        @block.tensor
        def _(t):
            t.wait_ge(s_lw, 16 * n_lw)
            t.wait_ge(s_ww, 48)
            gq = 0
            for ci in range(NCF):
                t.wait_ge(s_ld, 16 * NT * (ci + 1))
                t.wait_ge(s_sq, ci + 1)
                if ci >= 1:
                    t.wait_ge(s_mu, ci)
                for j in range(NT):
                    t.matmul(pS[0:1, 0:CH], cb["ones_col_bf"][:, :], tile(xbf, j, CH),
                             start=(j == 0), stop=(j == NT - 1))
                for j in range(NT):
                    ins = t.matmul(pS[0:1, CH:2 * CH], cb["ones_col_bf"][:, :],
                                   tile(sq, j, CH), start=(j == 0), stop=(j == NT - 1))
                ins.then_inc(s_stats, 1)
                t.wait_ge(s_pr, ci + 1)
                if ci >= 1:
                    t.wait_ge(s_nm, ci)
                t.matmul(pB[:, 0:CH], cb["ones_row_bf"][0:1, :], pair_bf[0:1, 0:CH],
                         start=True, stop=True)
                t.matmul(pB[:, CH:2 * CH], cb["ones_row_bf"][0:1, :], pair_bf[0:1, CH:2 * CH],
                         start=True, stop=True).then_inc(s_bc, 1)
                t.wait_ge(s_nm, ci + 1)
                for gi in range(kq_groups_in_chunk[ci]):
                    gq += 1
                    if gq > 2:
                        t.wait_ge(s_ev, gq - 2)
                    slot = pX[:, ((gq - 1) % 2) * CH: ((gq - 1) % 2) * CH + CH]
                    if gi < 8:
                        ot = gi
                        for j in range(NT):
                            ins = t.matmul(slot,
                                           wqk[:, NT * 1024 + j * 1024 + ot * 128:
                                               NT * 1024 + j * 1024 + ot * 128 + 128],
                                           tile(xln, j, CH),
                                           start=(j == 0), stop=(j == NT - 1))
                    elif gi < 16:
                        tt, half = (gi - 8) // 2, (gi - 8) % 2
                        for j in range(NT):
                            ins = t.matmul(slot,
                                           xln[:, j * CH + tt * 128: j * CH + tt * 128 + 128],
                                           wqk[:, NT * 1024 + j * 1024 + half * 512:
                                               NT * 1024 + j * 1024 + half * 512 + 512],
                                           start=(j == 0), stop=(j == NT - 1))
                    else:
                        ot = gi - 16
                        for j in range(NT):
                            ins = t.matmul(slot,
                                           wqk[:, j * 1024 + ot * 128: j * 1024 + ot * 128 + 128],
                                           tile(xln, j, CH),
                                           start=(j == 0), stop=(j == NT - 1))
                    ins.then_inc(s_kq, 1)
            assert gq == kq_total

            if upto < 2:
                return
            t.wait_ge(s_ev, kq_total)
            t.wait_ge(s_mu, NCF)
            NIT = NHP * NCO * NKT

            def s_mm(it2):
                hp2, r2 = divmod(it2 - 1, NCO * NKT)
                qc2, kt2 = divmod(r2, NKT)
                if it2 > 2:
                    t.wait_ge(s_e, it2 - 2)
                slot2 = pB if (it2 % 2 == 1) else pX
                kcol = hp2 * TF + kt2 * 128
                qcol = hp2 * T + qc2 * CH
                t.matmul(slot2[:, 0:CH], KT[0:64, kcol:kcol + 128],
                         QT[0:64, qcol:qcol + CH], start=True, stop=True,
                         tile_position=(0, 0))
                t.matmul(slot2[:, CH:2 * CH], KT[64:128, kcol:kcol + 128],
                         QT[64:128, qcol:qcol + CH], start=True, stop=True,
                         tile_position=(64, 0)).then_inc(s_s, 1)

            it = 0
            s_mm(1)
            for hp in range(NHP):
                t.wait_ge(s_vh, 16 * (hp + 1))
                for qc in range(NCO):
                    grp = 4 * hp + qc
                    t.wait_ge(s_fin, grp)
                    for kt in range(NKT):
                        it += 1
                        if it < NIT:
                            s_mm(it + 1)
                        pts = Pt[:, (it % 2) * 1024:(it % 2) * 1024 + 1024]
                        t.wait_ge(s_e, it)
                        t.matmul(pD2[0:1, 0:CH], cb["ones_col_bf"][:, :], pts[:, 0:CH],
                                 start=(kt == 0), stop=(kt == NKT - 1))
                        t.matmul(pD2[0:1, CH:2 * CH], cb["ones_col_bf"][:, :],
                                 pts[:, CH:2 * CH], start=(kt == 0), stop=(kt == NKT - 1))
                        t.matmul(pO[0:64, 0:CH], Vhp[:, kt * 128:kt * 128 + 64],
                                 pts[:, 0:CH], start=(kt == 0), stop=(kt == NKT - 1),
                                 tile_position=(0, 0))
                        t.matmul(pO[64:128, CH:2 * CH], Vhp[:, kt * 128 + 64:kt * 128 + 128],
                                 pts[:, CH:2 * CH], start=(kt == 0), stop=(kt == NKT - 1),
                                 tile_position=(0, 64)).then_inc(s_po, 1)
                    t.wait_ge(s_rcf, grp + 1)
                    t.matmul(pD2[:, 0:CH], cb["sel_lo"][0:1, :], rcbf[0:1, 0:CH],
                             start=True, stop=True)
                    t.matmul(pD2[:, CH:2 * CH], cb["sel_hi"][0:1, :], rcbf[0:1, CH:2 * CH],
                             start=True, stop=True).then_inc(s_rb, 1)
            assert it == NHP * NCO * NKT

            if upto < 3:
                return
            t.wait_ge(s_fin, NHP * NCO)
            for oc in range(NCO):
                for ot in range(NT):
                    g3 = NT * oc + ot
                    if g3 >= 2:
                        t.wait_ge(s_f3, g3 - 1)
                    slot = pX[:, (g3 % 2) * CH:(g3 % 2) * CH + CH]
                    for j in range(NT):
                        ins = t.matmul(slot,
                                       wo_sb[:, j * 1024 + ot * 128: j * 1024 + ot * 128 + 128],
                                       tile(OA, j, T, (oc * CH, (oc + 1) * CH)),
                                       start=(j == 0), stop=(j == NT - 1))
                    ins.then_inc(s_wo, 1)

        @block.scalar
        def _(s):
            ssn = [0]
            def sbump(ins):
                ins.then_inc(ss_self, 1); ssn[0] += 1
            def sfence():
                s.wait_ge(ss_self, ssn[0])
            s.wait_ge(s_lw, 16 * n_lw)
            s.wait_ge(s_ww, 48)
            gq = 0
            for ci in range(NCF):
                s.wait_ge(s_ld, 16 * NT * (ci + 1))
                if ci >= 1:
                    s.wait_ge(s_kq, sum(kq_groups_in_chunk[:ci]))
                for j in range(NT):
                    ins = s.activation(tile(sq, j, CH), tile(xbf, j, CH), AF.Square)
                ins.then_inc(s_sq, 1)
                s.wait_ge(s_mu, ci + 1)
                if ci >= 1:
                    s.wait_ge(s_pr, ci)
                var = lnsm[0:1, 2 * CH:3 * CH]
                rstd = lnsm[0:1, 3 * CH:4 * CH]
                sbump(s.activation(rstd, var, AF.Ln, bias=eps1[0:1, 0:1]))
                sfence()
                s.activation(rstd, rstd, AF.Exp, scale=-0.5).then_inc(s_rs, 1)
                for gi in range(kq_groups_in_chunk[ci]):
                    gq += 1
                    s.wait_ge(s_kq, gq)
                    slot = pX[:, ((gq - 1) % 2) * CH:((gq - 1) % 2) * CH + CH]
                    if gi < 8:
                        dst = KT[:, gi * TF + ci * CH: gi * TF + ci * CH + CH]
                        ins = s.activation(dst, slot, AF.Copy)
                    elif gi < 16:
                        tt, half = (gi - 8) // 2, (gi - 8) % 2
                        vs_i = 4 * ci + tt
                        if vs_i >= 2 and half == 0:
                            s.wait_ge(s_vs, 16 * (vs_i - 1))
                        dst = Vc[:, (tt % 2) * 1024 + half * 512:
                                 (tt % 2) * 1024 + half * 512 + 512]
                        ins = s.activation(dst, slot, AF.Copy)
                    else:
                        ot = gi - 16
                        dst = QT[:, ot * T + ci * CH: ot * T + ci * CH + CH]
                        ins = s.activation(dst, slot, AF.Copy, scale=float(2.0 ** -5))
                    ins.then_inc(s_ev, 1)
            if upto < 2:
                return
            it = 0
            for hp in range(NHP):
                for qc in range(NCO):
                    for kt in range(NKT):
                        it += 1
                        s.wait_ge(s_s, it)
                        if it >= 3:
                            s.wait_ge(s_po, it - 2)
                        slot = pB if (it % 2 == 1) else pX
                        s.activation(Pt[:, (it % 2) * 1024:(it % 2) * 1024 + 1024],
                                     slot[:, :], AF.Exp).then_inc(s_e, 1)

        @block.vector
        def _(v):
            svn = [0]
            def vbump(ins):
                ins.then_inc(sv_self, 1); svn[0] += 1
            def vfence():
                v.wait_ge(sv_self, svn[0])
            v.wait_ge(s_lw, 16 * n_lw)
            v.wait_ge(s_ww, 48)
            v.memset(eps1[0:1, :], LN_EPS)
            for ci in range(NCF):
                v.wait_ge(s_stats, ci + 1)
                if ci >= 1:
                    v.wait_ge(s_rs, ci)
                mu = lnsm[0:1, 0:CH]
                ex2 = lnsm[0:1, CH:2 * CH]
                var = lnsm[0:1, 2 * CH:3 * CH]
                vbump(v.tensor_scalar(mu, pS[0:1, 0:CH], 1.0 / D, None, ALU.mult))
                vbump(v.tensor_scalar(ex2, pS[0:1, CH:2 * CH], 1.0 / D, None, ALU.mult))
                vfence()
                vbump(v.tensor_tensor(var, mu, mu, ALU.mult))
                vfence()
                v.tensor_tensor(var, ex2, var, ALU.subtract).then_inc(s_mu, 1)
                v.wait_ge(s_rs, ci + 1)
                rstd = lnsm[0:1, 3 * CH:4 * CH]
                nu = lnsm[0:1, CH:2 * CH]
                vbump(v.tensor_tensor(nu, mu, rstd, ALU.mult))
                vbump(v.tensor_copy(pair_bf[0:1, 0:CH], rstd))
                vfence()
                v.tensor_copy(pair_bf[0:1, CH:2 * CH], nu).then_inc(s_pr, 1)
                v.wait_ge(s_bc, ci + 1)
                if ci >= 1:
                    v.wait_ge(s_kq, sum(kq_groups_in_chunk[:ci]))
                vbump(v.tensor_copy(rbuf[:, :], pB[:, :]))
                vfence()
                for j in range(NT):
                    vbump(v.tensor_tensor(tile(xln, j, CH), tile(xbf, j, CH),
                                          rbuf[:, 0:CH], ALU.mult))
                vfence()
                for j in range(NT):
                    o = tile(xln, j, CH)
                    ins = v.tensor_tensor(o, o, rbuf[:, CH:2 * CH], ALU.subtract)
                ins.then_inc(s_nm, 1)
            if upto < 2:
                return
            for hp in range(NHP):
                for qc in range(NCO):
                    grp = 4 * hp + qc
                    v.wait_ge(s_po, 32 * (grp + 1))
                    vbump(v.reciprocal(lnsm[0:1, 0:CH], pD2[0:1, 0:CH]))
                    vbump(v.reciprocal(lnsm[0:1, CH:2 * CH], pD2[0:1, CH:2 * CH]))
                    vfence()
                    v.tensor_copy(rcbf[0:1, :], lnsm[0:1, 0:2 * CH]).then_inc(s_rcf, 1)
                    v.wait_ge(s_rb, grp + 1)
                    vbump(v.tensor_copy(rcb[0:64, :], pD2[0:64, 0:CH]))
                    vbump(v.tensor_copy(rcb[64:128, :], pD2[64:128, CH:2 * CH]))
                    vfence()
                    dst = tile(OA, hp, T, (qc * CH, (qc + 1) * CH))
                    v.tensor_tensor(dst[0:64, :], pO[0:64, 0:CH], rcb[0:64, :], ALU.mult)
                    v.tensor_tensor(dst[64:128, :], pO[64:128, CH:2 * CH], rcb[64:128, :],
                                    ALU.mult).then_inc(s_fin, 1)
            if upto < 3:
                return
            for oc in range(NCO):
                v.wait_ge(s_xf, 16 * NT * (oc + 1))
                for ot in range(NT):
                    g3 = NT * oc + ot
                    v.wait_ge(s_wo, g3 + 1)
                    slot = pX[:, (g3 % 2) * CH:(g3 % 2) * CH + CH]
                    dst = tile(xf32, ot, CH)
                    vbump(v.tensor_tensor(dst, dst, slot, ALU.add))
                    vfence()
                    v.tensor_scalar(dst, dst, bo_sb[:, ot:ot + 1], None,
                                    ALU.add).then_inc(s_f3, 1)



B_CONSTS = ("ones_col_bf", "ones_row_bf", "ident_bf")


def build_B():
    nc = bass.Bass("TRN2", target_bir_lowering=False)
    caps = declare_constants(nc, B_CONSTS)
    x_in = nc.dram_tensor("xT", [D, T], F32, kind="ExternalInput").ap()
    w1_in = nc.dram_tensor("w1T_all", [9, D, 4096], F32, kind="ExternalInput").ap()
    w2_in = nc.dram_tensor("w2T_all", [9, 8, 4096, 128], F32, kind="ExternalInput").ap()
    b1_in = nc.dram_tensor("b1_all", [9, 128, 32], F32, kind="ExternalInput").ap()
    b2_in = nc.dram_tensor("b2_all", [9, 1024], F32, kind="ExternalInput").ap()
    rw_in = nc.dram_tensor("rwT", [D, 8], F32, kind="ExternalInput").ap()
    rb_in = nc.dram_tensor("rb", [1, 8], F32, kind="ExternalInput").ap()
    sel9_in = nc.dram_tensor("sel9", [9, 1024], BF16, kind="ExternalInput").ap()
    y_out = nc.dram_tensor("yT", [D, T], F32, kind="ExternalOutput").ap()
    emit_B(nc, caps, x_in, w1_in, w2_in, b1_in, b2_in, rw_in, rb_in, sel9_in, y_out)
    return nc


def emit_B(nc, caps, x_in, w1_in, w2_in, b1_in, b2_in, rw_in, rb_in, sel9_in, y_out,
           P="B_"):
    gdram = nc.dram_tensor(P + "g_scr", [9, T], BF16).ap()

    NCH = T // CH
    NTT = T // 128
    NE = 9
    NQ8 = 8
    N_EV2 = 2 * NE * 8 * 2
    N_GTF = NTT * 8

    with ExitStack() as ctx:
        def sb(name, shape, dt):
            return ctx.enter_context(nc.sbuf_tensor(P + name, shape, dt))
        def ps(name, shape, dt):
            return ctx.enter_context(nc.psum_tensor(P + name, shape, dt))
        def sem(name):
            return ctx.enter_context(nc.semaphore(P + name))

        s_lw = sem("s_lw"); s_ww = sem("s_ww"); s_ld = sem("s_ld")
        s_sq = sem("s_sq"); s_stats = sem("s_stats"); s_mu = sem("s_mu")
        s_rs = sem("s_rs"); s_pr = sem("s_pr"); s_bc = sem("s_bc"); s_nm = sem("s_nm")
        s_rt = sem("s_rt"); s_gv = sem("s_gv"); s_gmm = sem("s_gmm"); s_gtf = sem("s_gtf")
        s_g9 = sem("s_g9")
        s_wst = sem("s_wst"); s_hm = sem("s_hm"); s_ge = sem("s_ge"); s_gt = sem("s_gt")
        s_gb = sem("s_gb"); s_gbe = sem("s_gbe")
        s_p2 = sem("s_p2"); s_ev2 = sem("s_ev2"); s_st = sem("s_st")
        s_vc = sem("s_vc"); s_rv1 = sem("s_rv1"); s_se = sem("s_se")
        s_gsd = sem("s_gsd"); s_g9m = sem("s_g9m")
        sv_self = sem("sv_self"); ss_self = sem("ss_self")

        X = sb("Xr", [128, NT * T], F32)
        xln = sb("xlnB", [128, NT * T], BF16)
        h_sb = sb("h_sb", [128, 32 * 1024], BF16)
        xbf = h_sb
        w1b = sb("w1b", [128, 2 * NT * 256], BF16)
        w2b = sb("w2b", [128, 2 * 32 * 128], BF16)
        lnsm = sb("lnsmB", [1, 2 * CH], F32)
        pair_bf = sb("pairB", [1, 2 * CH], BF16)
        rbuf = sb("rbufB", [128, 2 * CH], BF16)
        rwT_sb = sb("rwT_sb", [128, 64], BF16)
        rb_sb = sb("rb_sb", [1, 8], BF16)
        b1_sb = sb("b1_sb", [128, NE * 32], F32)
        b2_bf = sb("b2_bf", [9, 1024], BF16)
        gate9 = sb("gate9", [9, T], BF16)
        sel9 = sb("sel9_sb", [9, 1024], BF16)
        gstg = w1b
        gb_sb = rbuf
        lg = sb("lg", [128, 40], F32)
        gates_bf = sb("gates_bf", [128, 8], BF16)
        eps1 = sb("eps1B", [1, 1], F32)
        cb = {n: sb("cB_" + n, shp, dt) for n, shp, dt in CONST_SPEC if n in B_CONSTS}

        pH = ps("pH", [128, 2 * CH], F32)
        pO2 = ps("pO2", [128, 2 * CH], F32)
        pB = ps("pBB", [128, 2 * CH], F32)
        pS = pB

        block = ctx.enter_context(nc.Block())

        n_lw = len(B_CONSTS) + 2

        @block.sync
        def _(g):
            for name in B_CONSTS:
                g.dma_start(out=cb[name][:, :], in_=caps[name]).then_inc(s_lw, 16)
            g.dma_start(out=b1_sb.rearrange("p (e c) -> p e c", e=NE),
                        in_=b1_in.rearrange("e p c -> p e c")).then_inc(s_lw, 16)
            g.dma_start(out=sel9[:, :], in_=sel9_in).then_inc(s_lw, 16)
            for tt in range(NTT):
                g.wait_ge(s_gtf, 8 * (tt + 1))
                if tt >= 1:
                    g.wait_ge(s_gsd, 16 * tt)
                g.dma_start(out=gdram[1:9, tt * 128:(tt + 1) * 128],
                            in_=gstg[0:1, (tt % 2) * 1024:(tt % 2) * 1024 + 1024]
                            ).then_inc(s_gsd, 16)
            g.wait_ge(s_ev2, N_EV2)
            for oc in range(NCH):
                if oc >= 1:
                    g.wait_ge(s_st, 16 * oc)
                g.dma_start(out=y_out.rearrange("(j p) c -> p j c", p=128)[:, :, oc * CH:(oc + 1) * CH],
                            in_=X.rearrange("p (j c) -> p j c", j=NT)[:, :, oc * CH:(oc + 1) * CH]
                            ).then_inc(s_st, 16)
            g.wait_ge(s_st, 16 * NCH)

        @block.gpsimd
        def _(g):
            dma_wtiles(g, rwT_sb[:, :], rw_in, NT, s_ww)
            g.dma_start(out=b2_bf[:, :], in_=b2_in).then_inc(s_ww, 16)
            g.dma_start(out=rb_sb[:, :], in_=rb_in).then_inc(s_ww, 16)
            for j in range(NT):
                g.dma_start(out=tile(X, j, T),
                            in_=x_in[128 * j:128 * (j + 1), :]).then_inc(s_ld, 16)
            g.wait_ge(s_gsd, 16 * NTT)
            g.dma_start(out=gate9[1:9, :], in_=gdram[1:9, :]).then_inc(s_g9, 16)
            wi = 0; we = 0; wd = 0
            for half in range(2):
                for e in range(NE):
                    for q16 in range(16):
                        wi += 1; we += 1
                        if wi >= 2:
                            g.wait_ge(s_wst, 16 * (wi - 1))
                        if we >= 3:
                            g.wait_ge(s_hm, 4 * (we - 2))
                        g.dma_start(out=w1b.rearrange("p (s j c) -> p s j c", s=2, j=NT)
                                    [:, (we - 1) % 2, :, :],
                                    in_=w1_in[e].rearrange("(j p) c -> p j c", p=128)
                                    [:, :, q16 * 256:(q16 + 1) * 256]).then_inc(s_wst, 16)
                    for d in range(8):
                        wi += 1; wd += 1
                        g.wait_ge(s_wst, 16 * (wi - 1))
                        if wd >= 3:
                            g.wait_ge(s_p2, 2 * (wd - 2))
                        g.dma_start(out=w2b.rearrange("p (s f c) -> p s f c", s=2, f=32)
                                    [:, (wd - 1) % 2, :, :],
                                    in_=w2_in[e, d].rearrange("(f p) c -> p f c", p=128)
                                    ).then_inc(s_wst, 16)

        @block.tensor
        def _(t):
            t.wait_ge(s_lw, 16 * n_lw)
            t.wait_ge(s_ww, 48)
            t.wait_ge(s_ld, 16 * NT)
            for ci in range(NCH):
                t.wait_ge(s_sq, ci + 1)
                if ci >= 1:
                    t.wait_ge(s_mu, ci)
                for j in range(NT):
                    t.matmul(pS[0:1, 0:CH], cb["ones_col_bf"][:, :],
                             xbf[:, j * CH:(j + 1) * CH], start=(j == 0), stop=(j == NT - 1))
                for j in range(NT):
                    ins = t.matmul(pS[0:1, CH:2 * CH], cb["ones_col_bf"][:, :],
                                   xbf[:, NT * CH + j * CH: NT * CH + (j + 1) * CH],
                                   start=(j == 0), stop=(j == NT - 1))
                ins.then_inc(s_stats, 1)
                t.wait_ge(s_pr, ci + 1)
                if ci >= 1:
                    t.wait_ge(s_nm, ci)
                t.matmul(pB[:, 0:CH], cb["ones_row_bf"][0:1, :], pair_bf[0:1, 0:CH],
                         start=True, stop=True)
                t.matmul(pB[:, CH:2 * CH], cb["ones_row_bf"][0:1, :], pair_bf[0:1, CH:2 * CH],
                         start=True, stop=True).then_inc(s_bc, 1)
            t.wait_ge(s_nm, NCH)
            for tt in range(NTT):
                if tt >= 1:
                    t.wait_ge(s_gv, tt)
                for j in range(NT):
                    t.matmul(pB[:, 0:8], xln[:, j * T + tt * 128: j * T + tt * 128 + 128],
                             rwT_sb[:, j * 8:(j + 1) * 8], start=(j == 0), stop=False)
                t.matmul(pB[:, 0:8], cb["ones_row_bf"][0:1, :],
                         rb_sb[0:1, :], start=False, stop=True).then_inc(s_rt, 1)
                t.wait_ge(s_gv, tt + 1)
                t.wait_ge(s_gtf, 8 * tt)
                for e in range(8):
                    ins = t.matmul(pO2[0:1, e * 128:(e + 1) * 128],
                                   gates_bf[:, e:e + 1], cb["ident_bf"][:, :],
                                   start=True, stop=True)
                ins.then_inc(s_gmm, 1)
            t.wait_ge(s_g9, 16)
            t.wait_ge(s_g9m, 1)
            hk = 0; we = 0; wd = 0; p2g = 0; gbn = 0
            for half in range(2):
                for e in range(NE):
                    eg = half * NE + e
                    if e > 0:
                        for sub in range(2):
                            gbn += 1
                            if gbn > 2:
                                t.wait_ge(s_gbe, gbn - 2)
                            t.matmul(pB[:, sub * CH:(sub + 1) * CH],
                                     sel9[:, (e - 1) * 128:e * 128],
                                     gate9[:, half * 1024 + sub * CH:
                                           half * 1024 + (sub + 1) * CH],
                                     start=True, stop=True).then_inc(s_gb, 1)
                    for q16 in range(16):
                        we += 1
                        t.wait_ge(s_wst, 16 * (we + wd))
                        for fft in range(2):
                            for sub in range(2):
                                hk += 1
                                if hk > 2:
                                    t.wait_ge(s_ge, hk - 2)
                                slot = pH[:, ((hk - 1) % 2) * CH:((hk - 1) % 2) * CH + CH]
                                for j in range(NT):
                                    ins = t.matmul(
                                        slot,
                                        w1b[:, ((we - 1) % 2) * NT * 256 + j * 256 + fft * 128:
                                            ((we - 1) % 2) * NT * 256 + j * 256 + fft * 128 + 128],
                                        xln[:, j * T + half * 1024 + sub * CH:
                                            j * T + half * 1024 + (sub + 1) * CH],
                                        start=(j == 0), stop=(j == NT - 1))
                                ins.then_inc(s_hm, 1)
                    for d in range(8):
                        wd += 1
                        t.wait_ge(s_wst, 16 * (we + wd))
                        for sub in range(2):
                            p2g += 1
                            if p2g > 2:
                                t.wait_ge(s_ev2, p2g - 2)
                            if e > 0:
                                t.wait_ge(s_gt, 64 * (half * 8 + e - 1) + 64)
                            else:
                                t.wait_ge(s_ge, 64 * eg + 64)
                            slot = pO2[:, ((p2g - 1) % 2) * CH:((p2g - 1) % 2) * CH + CH]
                            for fft in range(32):
                                last = (fft == 31) and (e != 0)
                                ins = t.matmul(
                                    slot,
                                    w2b[:, ((wd - 1) % 2) * 32 * 128 + fft * 128:
                                        ((wd - 1) % 2) * 32 * 128 + fft * 128 + 128],
                                    h_sb[:, fft * 1024 + sub * CH: fft * 1024 + (sub + 1) * CH],
                                    start=(fft == 0), stop=last)
                            if e == 0:
                                ins = t.matmul(slot, b2_bf[:, d * 128:(d + 1) * 128],
                                               gate9[:, half * 1024 + sub * CH:
                                                     half * 1024 + (sub + 1) * CH],
                                               start=False, stop=True)
                            ins.then_inc(s_p2, 1)

        @block.scalar
        def _(s):
            ssn = [0]
            def sbump(ins):
                ins.then_inc(ss_self, 1); ssn[0] += 1
            def sfence():
                s.wait_ge(ss_self, ssn[0])
            s.wait_ge(s_lw, 16 * n_lw)
            s.wait_ge(s_ld, 16 * NT)
            for ci in range(NCH):
                s.wait_ge(s_vc, ci + 1)
                for j in range(NT):
                    ins = s.activation(xbf[:, NT * CH + j * CH: NT * CH + (j + 1) * CH],
                                       xbf[:, j * CH:(j + 1) * CH], AF.Square)
                ins.then_inc(s_sq, 1)
                s.wait_ge(s_mu, ci + 1)
                if ci >= 1:
                    s.wait_ge(s_pr, ci)
                var = lnsm[0:1, CH:2 * CH]
                rstd = pH[0:1, 0:CH]
                sbump(s.activation(rstd, var, AF.Ln, bias=eps1[0:1, 0:1]))
                sfence()
                s.activation(rstd, rstd, AF.Exp, scale=-0.5).then_inc(s_rs, 1)
            for tt in range(NTT):
                s.wait_ge(s_rv1, tt + 1)
                s.activation(lg[:, 8:16], lg[:, 0:8], AF.Exp).then_inc(s_se, 1)
            hk = 0
            for half in range(2):
                for e in range(NE):
                    eg = half * NE + e
                    if eg >= 1:
                        s.wait_ge(s_p2, 16 * eg)
                    for q16 in range(16):
                        for fft in range(2):
                            for sub in range(2):
                                hk += 1
                                s.wait_ge(s_hm, hk)
                                fg = q16 * 2 + fft
                                slot = pH[:, ((hk - 1) % 2) * CH:((hk - 1) % 2) * CH + CH]
                                s.activation(h_sb[:, fg * 1024 + sub * CH:
                                                  fg * 1024 + (sub + 1) * CH],
                                             slot, AF.Gelu_apprx_tanh,
                                             bias=b1_sb[:, e * 32 + fg: e * 32 + fg + 1]
                                             ).then_inc(s_ge, 1)

        @block.vector
        def _(v):
            svn = [0]
            def vbump(ins):
                ins.then_inc(sv_self, 1); svn[0] += 1
            def vfence():
                v.wait_ge(sv_self, svn[0])
            v.wait_ge(s_lw, 16 * n_lw)
            v.wait_ge(s_ld, 16 * NT)
            v.memset(eps1[0:1, :], LN_EPS)
            for ci in range(NCH):
                if ci >= 1:
                    v.wait_ge(s_sq, ci)
                    v.wait_ge(s_stats, ci)
                for j in range(NT):
                    ins = v.tensor_copy(xbf[:, j * CH:(j + 1) * CH],
                                        X[:, j * T + ci * CH: j * T + (ci + 1) * CH])
                ins.then_inc(s_vc, 1)
                v.wait_ge(s_stats, ci + 1)
                if ci >= 1:
                    v.wait_ge(s_rs, ci)
                mu = lnsm[0:1, 0:CH]
                ex2 = lnsm[0:1, CH:2 * CH]
                mumu = pH[0:1, 0:CH]
                vbump(v.tensor_scalar(mu, pS[0:1, 0:CH], 1.0 / D, None, ALU.mult))
                vbump(v.tensor_scalar(ex2, pS[0:1, CH:2 * CH], 1.0 / D, None, ALU.mult))
                vfence()
                vbump(v.tensor_tensor(mumu, mu, mu, ALU.mult))
                vfence()
                v.tensor_tensor(ex2, ex2, mumu, ALU.subtract).then_inc(s_mu, 1)
                v.wait_ge(s_rs, ci + 1)
                rstd = pH[0:1, 0:CH]
                nu = lnsm[0:1, CH:2 * CH]
                vbump(v.tensor_copy(pair_bf[0:1, 0:CH], rstd))
                vfence()
                vbump(v.tensor_tensor(nu, mu, rstd, ALU.mult))
                vfence()
                v.tensor_copy(pair_bf[0:1, CH:2 * CH], nu).then_inc(s_pr, 1)
                v.wait_ge(s_bc, ci + 1)
                vbump(v.tensor_copy(rbuf[:, :], pB[:, :]))
                vfence()
                for j in range(NT):
                    vbump(v.tensor_tensor(xln[:, j * T + ci * CH: j * T + (ci + 1) * CH],
                                          xbf[:, j * CH:(j + 1) * CH],
                                          rbuf[:, 0:CH], ALU.mult))
                vfence()
                for j in range(NT):
                    o = xln[:, j * T + ci * CH: j * T + (ci + 1) * CH]
                    ins = v.tensor_tensor(o, o, rbuf[:, CH:2 * CH], ALU.subtract)
                ins.then_inc(s_nm, 1)
            for tt in range(NTT):
                v.wait_ge(s_rt, tt + 1)
                vbump(v.tensor_copy(lg[:, 0:8], pB[:, 0:8]))
                vfence()
                v.tensor_copy(lg[:, 32:33], lg[:, 0:1]).then_inc(s_rv1, 1)
                v.wait_ge(s_se, tt + 1)
                pe = lg[:, 8:16]
                s1 = lg[:, 16:17]; rc1 = lg[:, 17:18]
                p = lg[:, 18:26]
                m1 = lg[:, 26:27]; cnt = lg[:, 27:28]; m2r = lg[:, 28:29]
                c2 = lg[:, 29:30]; m2 = lg[:, 30:31]; den = lg[:, 31:32]
                eq = lg[:, 32:40]
                vbump(v.reduce_sum(s1, pe, axis=AX))
                vfence()
                vbump(v.reciprocal(rc1, s1))
                vfence()
                vbump(v.tensor_scalar(p, pe, rc1, None, ALU.mult))
                vfence()
                vbump(v.reduce_max(m1, p, axis=AX))
                vfence()
                vbump(v.tensor_scalar(eq, p, m1, None, ALU.is_ge))
                vfence()
                vbump(v.reduce_sum(cnt, eq, axis=AX))
                vbump(v.tensor_scalar(eq, eq, -1e30, None, ALU.mult))
                vfence()
                vbump(v.tensor_tensor(eq, p, eq, ALU.add))
                vfence()
                vbump(v.reduce_max(m2r, eq, axis=AX))
                vfence()
                vbump(v.tensor_scalar(c2, cnt, 2.0, None, ALU.is_ge))
                vfence()
                vbump(v.tensor_tensor(m2, c2, m1, ALU.mult))
                vbump(v.tensor_scalar(c2, c2, -1.0, 1.0, ALU.mult, ALU.add))
                vfence()
                vbump(v.tensor_tensor(c2, c2, m2r, ALU.mult))
                vfence()
                vbump(v.tensor_tensor(m2, m2, c2, ALU.add))
                vfence()
                vbump(v.tensor_tensor(den, m1, m2, ALU.add))
                vbump(v.tensor_scalar(eq, p, m2, None, ALU.is_ge))
                vfence()
                vbump(v.reciprocal(den, den))
                vbump(v.tensor_tensor(eq, p, eq, ALU.mult))
                vfence()
                v.tensor_scalar(gates_bf[:, :], eq, den, None, ALU.mult).then_inc(s_gv, 1)
                v.wait_ge(s_gmm, tt + 1)
                if tt >= 2:
                    v.wait_ge(s_gsd, 16 * (tt - 1))
                for e in range(8):
                    ins = v.tensor_copy(gstg[0:1, (tt % 2) * 1024 + e * 128:
                                            (tt % 2) * 1024 + (e + 1) * 128],
                                        pO2[0:1, e * 128:(e + 1) * 128])
                    ins.then_inc(s_gtf, 1)
            v.wait_ge(s_g9, 16)
            v.memset(gate9[0:1, :], 1.0).then_inc(s_g9m, 1)
            hk = 0; p2g = 0; gbn = 0; gtn = 0
            for half in range(2):
                for e in range(NE):
                    eg = half * NE + e
                    if e > 0:
                        for sub in range(2):
                            gbn += 1
                            v.wait_ge(s_gb, gbn)
                            v.tensor_copy(gb_sb[:, sub * CH:(sub + 1) * CH],
                                          pB[:, sub * CH:(sub + 1) * CH]).then_inc(s_gbe, 1)
                        v.wait_ge(s_gbe, gbn)
                    for q16 in range(16):
                        for fft in range(2):
                            for sub in range(2):
                                hk += 1
                                if e > 0:
                                    v.wait_ge(s_ge, hk)
                                    fg = q16 * 2 + fft
                                    hsl = h_sb[:, fg * 1024 + sub * CH: fg * 1024 + (sub + 1) * CH]
                                    gtn += 1
                                    v.tensor_tensor(hsl, hsl, gb_sb[:, sub * CH:(sub + 1) * CH],
                                                    ALU.mult).then_inc(s_gt, 1)
                    for d in range(8):
                        for sub in range(2):
                            p2g += 1
                            v.wait_ge(s_p2, p2g)
                            slot = pO2[:, ((p2g - 1) % 2) * CH:((p2g - 1) % 2) * CH + CH]
                            dst = X[:, d * T + half * 1024 + sub * CH:
                                    d * T + half * 1024 + (sub + 1) * CH]
                            v.tensor_tensor(dst, dst, slot, ALU.add).then_inc(s_ev2, 1)



DBG_C = 0
C_CONSTS = ("ones_col_bf", "ones_row_bf", "ident_bf", "ident2_bf", "ones_row_f32",
            "ident_f32")

PROJ_COLS = 4384
NCVT = 18
NH = 32
NSC = 16


def build_C(n_cores=8, upto=9):
    nc = bass.Bass("TRN2", target_bir_lowering=False)
    caps = declare_constants(nc, C_CONSTS)
    x_in = nc.dram_tensor("xTc", [D, 2051], F32, kind="ExternalInput").ap()
    win_in = nc.dram_tensor("W_in_p", [D, PROJ_COLS], F32, kind="ExternalInput").ap()
    wout_in = nc.dram_tensor("W_out", [2048, D], F32, kind="ExternalInput").ap()
    convw_in = nc.dram_tensor("convw", [128, NCVT * 4], F32, kind="ExternalInput").ap()
    convb_in = nc.dram_tensor("convb", [128, NCVT], F32, kind="ExternalInput").ap()
    alog_r = nc.dram_tensor("alog_r", [1, 32], F32, kind="ExternalInput").ap()
    dtb_in = nc.dram_tensor("dtb", [32, 1], F32, kind="ExternalInput").ap()
    dn_in = nc.dram_tensor("Dn", [128, 32], F32, kind="ExternalInput").ap()
    nw_in = nc.dram_tensor("nw_row", [1, 2048], F32, kind="ExternalInput").ap()
    prev_in = nc.dram_tensor("prev_col", [64, 1], F32, kind="ExternalInput").ap()
    ut_in = nc.dram_tensor("UTones", [128, 128], F32, kind="ExternalInput").ap()
    id32_in = nc.dram_tensor("ident32", [32, 32], F32, kind="ExternalInput").ap()
    mst_in = nc.dram_tensor("maskST", [128, 128], F32, kind="ExternalInput").ap()
    sel32_in = nc.dram_tensor("sel32", [32, 32 * 128], F32, kind="ExternalInput").ap()
    y_out = nc.dram_tensor("yT", [D, T], F32, kind="ExternalOutput").ap()
    emit_C(nc, caps, x_in, win_in, wout_in, convw_in, convb_in, alog_r, dtb_in,
           dn_in, nw_in, prev_in, ut_in, id32_in, mst_in, sel32_in, y_out,
           n_cores=n_cores, upto=upto)
    return nc


def emit_C(nc, caps, x_in, win_in, wout_in, convw_in, convb_in, alog_r, dtb_in,
           dn_in, nw_in, prev_in, ut_in, id32_in, mst_in, sel32_in, y_out,
           n_cores=8, upto=9, P="C_"):
    xbc_d = nc.dram_tensor(P + "xbc_scr", [NCVT * 128, 2051], BF16).ap()
    z_d = nc.dram_tensor(P + "z_scr", [T, 2048], BF16).ap()
    yn_d = nc.dram_tensor(P + "yn_scr", [T, 2048], BF16).ap()
    stin_d = nc.dram_tensor(P + "stin_scr", [64, 2048], F32)
    stout_d = nc.dram_tensor(P + "stout_scr", [128, 2048], F32)

    LNCH = [(0, 3), (3, 512), (515, 512), (1027, 512), (1539, 512)]
    P2_BLOCKS = [(2048 + 128 * i, 128) for i in range(NCVT)] + [(4352, 32)]
    rg = [[2 * i, 2 * i + 1] for i in range(max(1, n_cores // 2))]

    with ExitStack() as ctx:
        def sb(name, shape, dt):
            return ctx.enter_context(nc.sbuf_tensor(P + name, shape, dt))
        def sem(name):
            return ctx.enter_context(nc.semaphore(P + name))

        s_T = sem("s_T"); s_V = sem("s_V"); s_S = sem("s_S")
        s_G = sem("s_G"); s_Y = sem("s_Y"); s_lw = sem("s_lw"); s_AG = sem("s_AG")
        s_hT = sem("s_hT"); s_hV = sem("s_hV"); s_hS = sem("s_hS")

        xln = sb("xlnC", [128, NT * 2051], BF16)
        dtT = sb("dtTC", [32, T], F32)
        winb = sb("winbC", [128, 2 * NT * 512], BF16)
        woutb = sb("woutbC", [128, 8 * 16 * 128], BF16)
        xbcs = sb("xbcsC", [128, 2 * 2051], BF16)
        xbcc = sb("xbccC", [128, NCVT * 512], BF16)
        xbcr = sb("xbcrC", [128, NCVT * 520], BF16)
        cvA = sb("cvAC", [128, 512], F32)
        cvB = sb("cvBC", [128, 512], F32)
        ctl = sb("ctileC", [128, 128], BF16)
        xf32 = sb("xf32C", [128, NT * CH], F32)
        Prun = sb("PrunC", [64, 2048], F32)
        Prun_bf = sb("PrunbfC", [64, 2048], BF16)

        dadj = sb("dadjC", [128, NSC * 32], F32)
        convw = sb("convwC", [128, NCVT * 4], F32)
        convb = sb("convbC", [128, NCVT], F32)
        lnsm = sb("lnsmC", [1, 2 * CH], F32)
        pair_bf = sb("pairC", [1, 2 * CH], BF16)
        rbuf = sb("rbufC", [128, 2 * CH], BF16)
        UT = sb("UTC", [128, 128], F32)
        id32 = sb("id32C", [32, 32], F32)
        mst = sb("mstC", [128, 128], F32)
        sel32 = sb("sel32C", [32, 32 * 128], F32)
        alogr = sb("alogrC", [1, 32], F32)
        dtb = sb("dtbC", [32, 1], F32)
        dn = sb("dnC", [128, 32], F32)
        nwb = sb("nwbC", [128, 2048], BF16)
        prevc = sb("prevcC", [64, 1], F32)
        Abc = sb("AbcC", [128, 32], F32)
        Arow = sb("ArowC", [1, 32], F32)
        acolb = sb("acolbC", [32, 1], F32)

        dtn = sb("dtnC", [128, 32], F32)
        ldt = sb("ldtC", [128, 32], F32)
        csb = sb("csbC", [128, 32], F32)
        acn = sb("acnC", [128, 32], F32)
        asl = sb("aslC", [128, 32], F32)
        acT = sb("acTC", [32, 128], F32)
        wdtn = sb("wdtnC", [128, 32], F32)
        wsc = sb("wscC", [128, 32], F32)
        fcols = sb("fcolsC", [64, 32], F32)
        bacr = sb("bacrC", [1, 32], F32)
        rowsc = sb("rowscC", [1, 64], F32)
        cendc = sb("cendC", [32, 1], F32)
        Gm = sb("GmC", [128, 128], BF16)
        pre2 = sb("pre2C", [128, 2 * 128], F32)
        E2 = sb("E2C", [128, 2 * 128], BF16)
        MT = sb("MTC", [128, 2 * 128], BF16)
        Xn = sb("XnC", [128, 2 * 64], BF16)
        Dx = sb("DxC", [128, 2 * 64], BF16)
        Bn = sb("BnC", [128, 64], BF16)
        Bw = sb("BwC", [128, 2 * 64], BF16)
        Pts = sb("PtsC", [64, 2 * 64], F32)
        ubuf = sb("ubufC", [128, 2 * 64], F32)
        tmpb = sb("tmpbC", [128, 2 * 64], F32)
        nasl = sb("naslC", [128, 32], F32)
        dexp = sb("dexpC", [128, 32], F32)
        rstd1 = sb("rstd1C", [128, 1], F32)
        eps1 = sb("eps1C", [1, 1], F32)
        eps128 = sb("eps128C", [128, 1], F32)
        one32 = sb("one32C", [32, 1], F32)
        cb = {n: sb("cC_" + n, shp, dt) for n, shp, dt in CONST_SPEC if n in C_CONSTS}

        nwrow = xf32[0:1, 0:2048].bitcast(F32) if False else xf32[0:1, 0:2048]
        nwrb = xbcs[0:1, 0:2048]
        dtsp = xf32[0:32, 2048:3072]
        Sin = xbcr[0:64, 2048:4096]
        xbf = xbcc[:, 0:4096]
        sqb = xbcc[:, 4096:8192]
        ynat = xbcs[:, 0:4096]
        zrl = xbcc[:, 0:4096]
        yrl = xbcc[:, 4096:8192]
        zsil = winb[:, 0:2048]
        ynf = winb[:, 2048:4096]
        ynT = xln[:, 0:8192]

        p = [ctx.enter_context(nc.psum_tensor(P + "pp%d" % i, [128, CH], F32))
             for i in range(8)]

        block = ctx.enter_context(nc.Block())

        cnt = {"T": 0, "V": 0, "S": 0, "G": 0, "Y": 0}
        SEMS = {"T": s_T, "V": s_V, "S": s_S, "G": s_G, "Y": s_Y}

        def _sec(k, body):
            if k == "T":
                @block.tensor
                def _(e): body(e)
            elif k == "V":
                @block.vector
                def _(e): body(e)
            elif k == "S":
                @block.scalar
                def _(e): body(e)
            elif k == "G":
                @block.gpsimd
                def _(e): body(e)
            else:
                @block.sync
                def _(e): body(e)

        def _step(k, f, ndma=1):
            snap = {k2: cnt[k2] for k2 in cnt}
            def body(e):
                for k2, v in snap.items():
                    if v == 0:
                        continue
                    e.wait_ge(SEMS[k2], v * (16 if k2 in ("G", "Y") else 1))
                f(e)
            _sec(k, body)
            cnt[k] += ndma if k in ("G", "Y") else 1

        def stepT(f): _step("T", lambda e: f(e).then_inc(s_T, 1))
        def stepV(f): _step("V", lambda e: f(e).then_inc(s_V, 1))
        def stepS(f): _step("S", lambda e: f(e).then_inc(s_S, 1))
        def stepG(f): _step("G", lambda e: f(e).then_inc(s_G, 16))
        def stepY(f): _step("Y", lambda e: f(e).then_inc(s_Y, 16))

        def stepG_multi(fs):
            def body(e):
                for f in fs:
                    f(e).then_inc(s_G, 16)
            _step("G", body, ndma=len(fs))

        s_vself = sem("s_vself"); s_sself = sem("s_sself")
        _mf = {"V": [0, s_vself], "S": [0, s_sself]}

        def mk_chain(k):
            st = _mf[k]
            def bump(ins):
                ins.then_inc(st[1], 1)
                st[0] += 1
            def fence(e):
                e.wait_ge(st[1], st[0])
            return bump, fence
        vbump, vfence = mk_chain("V")
        sbump, sfence = mk_chain("S")

        NLW = len(C_CONSTS) + 11

        @block.sync
        def _(g):
            def L(dst, src):
                g.dma_start(out=dst, in_=src).then_inc(s_lw, 16)
            for name in C_CONSTS:
                L(cb[name][:, :], caps[name])
            L(convw[:, :], convw_in); L(convb[:, :], convb_in)
            L(alogr[0:1, 0:32], alog_r); L(dtb[:, :], dtb_in); L(dn[:, :], dn_in)
            L(prevc[:, :], prev_in); L(UT[:, :], ut_in); L(id32[:, :], id32_in)
            L(mst[:, :], mst_in); L(sel32[:, :], sel32_in); L(nwrow[0:1, :], nw_in)

        for k in ("T", "V", "S", "G", "Y"):
            _sec(k, lambda e: e.wait_ge(s_lw, 16 * NLW))

        stepV(lambda v: v.memset(eps1[0:1, :], LN_EPS))
        stepV(lambda v: v.memset(one32[:, :], 1.0))
        stepV(lambda v: v.memset(eps128[:, :], LN_EPS))
        stepV(lambda v: v.memset(Prun[:, :], 0.0))
        stepV(lambda v: v.memset(bacr[0:1, :], 0.0))
        stepS(lambda s: s.activation(Arow[0:1, :], alogr[0:1, 0:32], AF.Exp))
        stepV(lambda v: v.tensor_scalar(Arow[0:1, :], Arow[0:1, :], -1.0, None, ALU.mult))
        stepT(lambda t: t.matmul(p[0][:, 0:32], cb["ones_row_f32"][0:1, :],
                                 Arow[0:1, :], start=True, stop=True))
        stepV(lambda v: v.tensor_copy(Abc[:, :], p[0][:, 0:32]))
        stepT(lambda t: t.matmul(p[0][0:32, 0:1], Arow[0:1, :],
                                 cb["ones_row_f32"][0:1, 0:1], start=True, stop=True))
        stepV(lambda v: v.tensor_copy(acolb[:, :], p[0][0:32, 0:1]))
        stepV(lambda v: v.tensor_copy(nwrb[0:1, :], nwrow[0:1, :]))
        for i in range(4):
            stepT(lambda t, i=i: t.matmul(p[1][:, :], cb["ones_row_bf"][0:1, :],
                                          nwrb[0:1, i * 512:(i + 1) * 512],
                                          start=True, stop=True))
            stepV(lambda v, i=i: v.tensor_copy(nwb[:, i * 512:(i + 1) * 512], p[1][:, :]))

        for ci, (t0, tw) in enumerate(LNCH):
            stepG_multi([
                (lambda g, j=j, t0=t0, tw=tw: g.dma_start(
                    out=xbf[:, j * 512:j * 512 + tw],
                    in_=x_in[128 * j:128 * (j + 1), t0:t0 + tw]))
                for j in range(NT)])
            def sqs(s, tw=tw):
                for j in range(NT):
                    ins = s.activation(sqb[:, j * 512:j * 512 + tw],
                                       xbf[:, j * 512:j * 512 + tw], AF.Square)
                return ins
            stepS(sqs)
            def stats(t, tw=tw):
                for j in range(NT):
                    t.matmul(p[2][0:1, 0:tw], cb["ones_col_bf"][:, :],
                             xbf[:, j * 512:j * 512 + tw], start=(j == 0), stop=(j == NT - 1))
                for j in range(NT):
                    ins = t.matmul(p[3][0:1, 0:tw], cb["ones_col_bf"][:, :],
                                   sqb[:, j * 512:j * 512 + tw], start=(j == 0),
                                   stop=(j == NT - 1))
                return ins
            stepT(stats)
            def lnv1(v, tw=tw):
                vbump(v.tensor_scalar(lnsm[0:1, 0:tw], p[2][0:1, 0:tw],
                                      1.0 / D, None, ALU.mult))
                vbump(v.tensor_scalar(lnsm[0:1, CH:CH + tw], p[3][0:1, 0:tw],
                                      1.0 / D, None, ALU.mult))
                vfence(v)
                vbump(v.tensor_tensor(cvA[0:1, 0:tw], lnsm[0:1, 0:tw],
                                      lnsm[0:1, 0:tw], ALU.mult))
                vfence(v)
                return v.tensor_tensor(lnsm[0:1, CH:CH + tw], lnsm[0:1, CH:CH + tw],
                                       cvA[0:1, 0:tw], ALU.subtract)
            stepV(lnv1)
            def lns1(s, tw=tw):
                sbump(s.activation(lnsm[0:1, CH:CH + tw], lnsm[0:1, CH:CH + tw],
                                   AF.Ln, bias=eps1[0:1, 0:1]))
                sfence(s)
                return s.activation(lnsm[0:1, CH:CH + tw], lnsm[0:1, CH:CH + tw],
                                    AF.Exp, scale=-0.5)
            stepS(lns1)
            def lnv2(v, tw=tw):
                vbump(v.tensor_copy(pair_bf[0:1, 0:tw], lnsm[0:1, CH:CH + tw]))
                vbump(v.tensor_tensor(lnsm[0:1, 0:tw], lnsm[0:1, 0:tw],
                                      lnsm[0:1, CH:CH + tw], ALU.mult))
                vfence(v)
                return v.tensor_copy(pair_bf[0:1, CH:CH + tw], lnsm[0:1, 0:tw])
            stepV(lnv2)
            def bcasts(t, tw=tw):
                t.matmul(p[4][:, 0:tw], cb["ones_row_bf"][0:1, :], pair_bf[0:1, 0:tw],
                         start=True, stop=True)
                return t.matmul(p[5][:, 0:tw], cb["ones_row_bf"][0:1, :],
                                pair_bf[0:1, CH:CH + tw], start=True, stop=True)
            stepT(bcasts)
            def lnv3(v, t0=t0, tw=tw):
                vbump(v.tensor_copy(rbuf[:, 0:tw], p[4][:, 0:tw]))
                vbump(v.tensor_copy(rbuf[:, CH:CH + tw], p[5][:, 0:tw]))
                vfence(v)
                for j in range(NT):
                    vbump(v.tensor_tensor(xln[:, j * 2051 + t0: j * 2051 + t0 + tw],
                                          xbf[:, j * 512:j * 512 + tw], rbuf[:, 0:tw],
                                          ALU.mult))
                vfence(v)
                for j in range(NT):
                    ins = v.tensor_tensor(xln[:, j * 2051 + t0: j * 2051 + t0 + tw],
                                          xln[:, j * 2051 + t0: j * 2051 + t0 + tw],
                                          rbuf[:, CH:CH + tw], ALU.subtract)
                    if j < NT - 1:
                        vbump(ins)
                return ins
            stepV(lnv3)

        for bi, (c0, w) in enumerate(P2_BLOCKS):
            stepG(lambda g, bi=bi, c0=c0, w=w: g.dma_start(
                out=winb.rearrange("p (s j c) -> p s j c", s=2, j=NT)[:, bi % 2, :, 0:w],
                in_=win_in.rearrange("(j p) c -> p j c", p=128)[:, :, c0:c0 + w]))
            for ci, (t0, tw) in enumerate(LNCH):
                if bi == len(P2_BLOCKS) - 1 and ci == 0:
                    continue
                def pj(t, bi=bi, w=w, t0=t0, tw=tw, ci=ci):
                    for j in range(NT):
                        ins = t.matmul(p[ci % 2][0:w, 0:tw],
                                       winb[:, (bi % 2) * NT * 512 + j * 512:
                                            (bi % 2) * NT * 512 + j * 512 + w],
                                       xln[:, j * 2051 + t0: j * 2051 + t0 + tw],
                                       start=(j == 0), stop=(j == NT - 1))
                    return ins
                stepT(pj)
                if bi < NCVT:
                    stepS(lambda s, bi=bi, t0=t0, tw=tw, ci=ci: s.activation(
                        xbcs[:, (bi % 2) * 2051 + t0:(bi % 2) * 2051 + t0 + tw],
                        p[ci % 2][:, 0:tw], AF.Copy))
                else:
                    stepS(lambda s, t0=t0, tw=tw, ci=ci: s.activation(
                        dtT[0:32, t0 - 3:t0 - 3 + tw], p[ci % 2][0:32, 0:tw],
                        AF.Identity, bias=dtb[:, 0:1]))
            if bi < NCVT:
                stepY(lambda g, bi=bi: g.dma_start(
                    out=xbc_d[bi * 128:(bi + 1) * 128, :],
                    in_=xbcs[:, (bi % 2) * 2051:(bi % 2) * 2051 + 2051]))
        for i in range(4):
            sl = dtT[0:32, i * 512:(i + 1) * 512]
            a1 = dtsp[0:32, 0:512]
            a2 = dtsp[0:32, 512:1024]
            stepS(lambda s, sl=sl, a1=a1: s.activation(a1, sl, AF.Abs))
            stepS(lambda s, a1=a1: s.activation(a1, a1, AF.Exp, scale=-1.0))
            stepS(lambda s, a1=a1: s.activation(a1, a1, AF.Ln, bias=one32[0:32, 0:1]))
            stepS(lambda s, sl=sl, a2=a2: s.activation(a2, sl, AF.Relu))
            stepV(lambda v, sl=sl, a1=a1, a2=a2: v.tensor_tensor(sl, a1, a2, ALU.add))

        for zc in range(4):
            wslot = (len(P2_BLOCKS) + zc) % 2
            stepG(lambda g, zc=zc, wslot=wslot: g.dma_start(
                out=winb.rearrange("p (s j c) -> p s j c", s=2, j=NT)[:, wslot, :, :],
                in_=win_in.rearrange("(j p) c -> p j c", p=128)[:, :, zc * 512:(zc + 1) * 512]))
            for qd in range(4):
                for k2 in range(2):
                    def zmm(t, qd=qd, k2=k2, wslot=wslot):
                        for k in (2 * k2, 2 * k2 + 1):
                            tt = 4 * qd + k
                            for j in range(NT):
                                ins = t.matmul(p[k % 2][:, :],
                                               xln[:, j * 2051 + 3 + tt * 128:
                                                   j * 2051 + 3 + tt * 128 + 128],
                                               winb[:, wslot * NT * 512 + j * 512:
                                                    wslot * NT * 512 + (j + 1) * 512],
                                               start=(j == 0), stop=(j == NT - 1))
                        return ins
                    stepT(zmm)
                    def zcp(s, k2=k2):
                        for k in (2 * k2, 2 * k2 + 1):
                            ins = s.activation(
                                xbcs[:, k * 512:(k + 1) * 512], p[k % 2][:, :], AF.Copy)
                        return ins
                    stepS(zcp)
                stepY(lambda g, zc=zc, qd=qd: g.dma_start(
                    out=z_d.rearrange("(q p) c -> p q c", p=128)
                    [:, 4 * qd:4 * qd + 4, zc * 512:(zc + 1) * 512],
                    in_=xbcs[:, 0:2048].rearrange("p (s c) -> p s c", s=4)))

        TCALL = {}
        _tc = [0]
        for c2 in range(NSC):
            Hb = c2 * NH
            seq = [("T1", Hb + 1), ("T1", Hb + 2)]
            for P2 in range(NH // 2):
                if P2 < NH // 2 - 1:
                    seq += [("T1", Hb + 2 * P2 + 3), ("T1", Hb + 2 * P2 + 4)]
                seq += [("T2", Hb + 2 * P2 + 1), ("T3", Hb + 2 * P2 + 1),
                        ("T2", Hb + 2 * P2 + 2), ("T3", Hb + 2 * P2 + 2)]
            for st2, Hx in seq:
                _tc[0] += 1
                TCALL[(st2, Hx)] = _tc[0]

        def tT1(H): return TCALL[("T1", H)]
        def tT2(H): return TCALL[("T2", H)]
        def tT3(H): return TCALL[("T3", H)]

        VPC = NH * 10 + 1

        def vccH(H, k):
            c2, h2 = divmod(H - 1, NH)
            return c2 * VPC + 1 + h2 * 10 + k

        def vpreH(H):
            c2, h2 = divmod(H - 1, NH)
            return c2 * VPC + (1 if h2 == 0 else 1 + (h2 - 1) * 10 + 2)

        for c in range(NSC if upto >= 2 else 0):
            cg, ck = c // 4, c % 4
            if ck == 0:
                stepG(lambda g, cg=cg: g.dma_start(
                    out=xbcr.rearrange("p (k c) -> p k c", c=520)[:, :, 0:515],
                    in_=xbc_d.rearrange("(k p) c -> p k c", p=128)
                    [:, :, cg * 512:cg * 512 + 515]))
                for k in range(NCVT):
                    base = k * 520
                    def convk(v, base=base, k=k):
                        vbump(v.tensor_scalar(
                            cvA[:, :], xbcr[:, base:base + 512],
                            convw[:, 4 * k:4 * k + 1], convb[:, k:k + 1],
                            ALU.mult, ALU.add))
                        for jj in range(1, 4):
                            vbump(v.tensor_scalar(
                                cvB[:, :], xbcr[:, base + jj:base + jj + 512],
                                convw[:, 4 * k + jj:4 * k + jj + 1], None, ALU.mult))
                            vfence(v)
                            ins = v.tensor_tensor(cvA[:, :], cvA[:, :], cvB[:, :],
                                                  ALU.add)
                            if jj < 3:
                                vbump(ins)
                                vfence(v)
                        return ins
                    stepV(convk)
                    stepS(lambda s, k=k: s.activation(xbcc[:, k * 512:(k + 1) * 512],
                                                      cvA[:, :], AF.Silu))
            t0c = c * 128
            stepT(lambda t, t0c=t0c: t.matmul(p[0][0:128, 0:32], dtT[:, t0c:t0c + 128],
                                              id32[:, :], start=True, stop=True))
            stepV(lambda v: v.tensor_copy(dtn[:, :], p[0][0:128, 0:32]))
            stepT(lambda t: t.matmul(p[1][0:128, 0:32], UT[:, :], dtn[:, :],
                                     start=True, stop=True))
            stepS(lambda s: s.activation(ldt[:, :], dtn[:, :], AF.Ln))
            def prepV1(v, t0c=t0c):
                vbump(v.reduce_sum(cendc[:, :], dtT[:, t0c:t0c + 128], axis=AX))
                vfence(v)
                vbump(v.tensor_tensor(cendc[:, :], cendc[:, :], acolb[:, :], ALU.mult))
                vbump(v.tensor_copy(csb[:, :], p[1][0:128, 0:32]))
                vfence(v)
                return v.tensor_tensor(acn[:, :], csb[:, :], Abc[:, :], ALU.mult)
            stepV(prepV1)
            stepT(lambda t: t.matmul(p[2][0:32, 0:128], acn[:, :],
                                     cb["ident_f32"][:, :], start=True, stop=True))
            stepT(lambda t: t.matmul(p[3][:, 0:32], cb["ones_row_f32"][0:1, :],
                                     bacr[0:1, :], start=True, stop=True))
            def prepV2(v):
                vbump(v.tensor_copy(acT[:, :], p[2][0:32, 0:128]))
                vbump(v.tensor_tensor(wsc[:, :], acn[:, :], p[3][:, 0:32], ALU.add))
                vbump(v.tensor_tensor(asl[:, :], acn[:, :], ldt[:, :], ALU.subtract))
                vfence(v)
                return v.tensor_scalar(nasl[:, :], asl[:, :], -1.0, None, ALU.mult)
            stepV(prepV2)
            stepT(lambda t: t.matmul(p[3][0:1, 0:32], cendc[:, :], id32[:, :],
                                     start=True, stop=True))
            def prepS1(s, c=c):
                sbump(s.activation(dadj[:, c * 32:(c + 1) * 32], wsc[:, :], AF.Exp))
                return s.activation(dexp[:, :], acn[:, :], AF.Exp)
            stepS(prepS1)
            def prepV3(v):
                vbump(v.tensor_copy(rowsc[0:1, 0:32], p[3][0:1, 0:32]))
                vfence(v)
                return v.tensor_tensor(bacr[0:1, :], bacr[0:1, :], rowsc[0:1, 0:32],
                                       ALU.add)
            stepV(prepV3)
            stepT(lambda t: t.matmul(p[3][:, 0:32], cb["ones_row_f32"][0:1, :],
                                     rowsc[0:1, 0:32], start=True, stop=True))
            stepV(lambda v: v.tensor_tensor(wsc[:, :], p[3][:, 0:32], asl[:, :], ALU.subtract))
            def prepS2(s):
                sbump(s.activation(wdtn[:, :], wsc[:, :], AF.Exp))
                return s.activation(rowsc[0:1, 32:64], rowsc[0:1, 0:32], AF.Exp)
            stepS(prepS2)
            bsl = xbcc[0:64, 16 * 512 + ck * 128: 16 * 512 + ck * 128 + 128]
            csl = xbcc[0:64, 17 * 512 + ck * 128: 17 * 512 + ck * 128 + 128]
            stepT(lambda t: t.matmul(p[3][0:64, 0:32], cb["ones_row_f32"][0:1, 0:64],
                                     rowsc[0:1, 32:64], start=True, stop=True))
            def prepT2(t, bsl=bsl, csl=csl):
                t.matmul(p[4][:, 0:128], bsl, csl, start=True, stop=True)
                return t.matmul(p[5][0:128, 0:64], bsl,
                                cb["ident2_bf"][0:64, :], start=True, stop=True)
            stepT(prepT2)
            def prepV4(v):
                vbump(v.tensor_copy(fcols[:, :], p[3][0:64, 0:32]))
                vbump(v.tensor_copy(Gm[:, :], p[4][:, 0:128]))
                vbump(v.tensor_copy(Bn[:, :], p[5][0:128, 0:64]))
                return v.tensor_copy(Prun_bf[:, :], Prun[:, :])
            stepV(prepV4)
            H0 = c * NH

            snapc = {k2: cnt[k2] for k2 in cnt}

            def _hw(e, snapc=snapc):
                for k2, v in snapc.items():
                    if v:
                        e.wait_ge(SEMS[k2], v * (16 if k2 in ("G", "Y") else 1))

            def prefirst(v, c=c, H0=H0):
                H1 = H0 + 1
                v.wait_ge(s_hT, tT1(H1))
                if H1 >= 3:
                    v.wait_ge(s_hS, H1 - 2)
                v.tensor_tensor(pre2[:, 0:128], p[0][:, 0:128], mst[:, :],
                                ALU.add).then_inc(s_hV, 1)
            _sec("V", prefirst)

            def heads_T(t, c=c, ck=ck, csl=csl, snapc=snapc, H0=H0):
                def t1pair(P):
                    HL = H0 + 2 * P + 1
                    HF = HL + 1
                    pb = p[2 + ((HL - 1) // 2) % 2]
                    if HL >= 4:
                        t.wait_ge(s_hV, vccH(HL - 3, 6))
                    if HL >= 3:
                        t.wait_ge(s_hV, vpreH(HL - 2))
                    t.matmul(p[0][:, 0:128], sel32[:, (2 * P) * 128:(2 * P) * 128 + 128],
                             acT[:, :], start=True, stop=True)
                    t.matmul(pb[0:128, 128:192], csl,
                             Prun_bf[:, (2 * P) * 64:(2 * P) * 64 + 64],
                             start=True, stop=True)
                    xp = xbcc[0:128, P * 512 + ck * 128: P * 512 + ck * 128 + 128]
                    t.matmul(pb[0:128, 0:128], xp, cb["ident_bf"][:, :],
                             start=True, stop=True).then_inc(s_hT, 1)
                    if HF >= 4:
                        t.wait_ge(s_hV, vccH(HF - 3, 2))
                    t.matmul(p[1][:, 0:128],
                             sel32[:, (2 * P + 1) * 128:(2 * P + 1) * 128 + 128],
                             acT[:, :], start=True, stop=True)
                    t.matmul(pb[0:128, 192:256], csl,
                             Prun_bf[:, (2 * P + 1) * 64:(2 * P + 1) * 64 + 64],
                             start=True, stop=True).then_inc(s_hT, 1)

                _hw(t, snapc)
                t1pair(0)
                for P in range(NH // 2):
                    if P < NH // 2 - 1:
                        t1pair(P + 1)
                    for H in (H0 + 2 * P + 1, H0 + 2 * P + 2):
                        par = (H - 1) % 2
                        pY = p[4 + par]; pSl = p[6 + par]
                        t.wait_ge(s_hV, vccH(H, 3))
                        t.matmul(pY[0:128, 0:64], MT[:, par * 128:par * 128 + 128],
                                 Xn[:, par * 64:par * 64 + 64], start=True,
                                 stop=True).then_inc(s_hT, 1)
                        t.wait_ge(s_hV, vccH(H, 4))
                        t.matmul(pSl[0:64, 0:64], Bw[:, par * 64:par * 64 + 64],
                                 Xn[:, par * 64:par * 64 + 64], start=True,
                                 stop=True).then_inc(s_hT, 1)

            def heads_S(s, snapc=snapc, H0=H0):
                for h in range(NH):
                    H = H0 + h + 1
                    par = (H - 1) % 2
                    if h == 0:
                        _hw(s, snapc)
                    if H >= 2:
                        s.wait_ge(s_hS, H - 1)
                    s.wait_ge(s_hV, vpreH(H))
                    s.activation(E2[:, par * 128:par * 128 + 128],
                                 pre2[:, par * 128:par * 128 + 128], AF.Exp,
                                 bias=nasl[:, h:h + 1]).then_inc(s_hS, 1)

            def heads_V(v, c=c, snapc=snapc, H0=H0):
                for h in range(NH):
                    H = H0 + h + 1
                    par = (H - 1) % 2
                    pb = p[2 + ((H - 1) // 2) % 2]
                    pSl = p[6 + par]; pY = p[4 + par]
                    if h == 0:
                        _hw(v, snapc)
                    if H >= 2:
                        v.wait_ge(s_hV, vccH(H - 1, 10))
                    v.wait_ge(s_hS, H)
                    if H >= 3:
                        v.wait_ge(s_hT, tT2(H - 2))
                    v.tensor_tensor(MT[:, par * 128:par * 128 + 128],
                                    E2[:, par * 128:par * 128 + 128], Gm[:, :],
                                    ALU.mult).then_inc(s_hV, 1)
                    if h < NH - 1:
                        par1 = H % 2
                        v.wait_ge(s_hT, tT1(H + 1))
                        if H >= 2:
                            v.wait_ge(s_hS, H - 1)
                        v.tensor_tensor(pre2[:, par1 * 128:par1 * 128 + 128],
                                        p[0 + par1][:, 0:128], mst[:, :],
                                        ALU.add).then_inc(s_hV, 1)
                    else:
                        v.memset(wsc[0:1, 0:1], 0.0).then_inc(s_hV, 1)
                    v.wait_ge(s_hT, tT1(H - par))
                    v.tensor_copy(Xn[:, par * 64:par * 64 + 64],
                                  pb[0:128, par * 64:par * 64 + 64]).then_inc(s_hV, 1)
                    v.tensor_scalar(Bw[:, par * 64:par * 64 + 64], Bn[:, :],
                                    wdtn[:, h:h + 1], None, ALU.mult).then_inc(s_hV, 1)
                    v.wait_ge(s_hV, vccH(H, 3))
                    v.tensor_scalar(Dx[:, par * 64:par * 64 + 64],
                                    Xn[:, par * 64:par * 64 + 64], dn[:, h:h + 1],
                                    None, ALU.mult).then_inc(s_hV, 1)
                    v.wait_ge(s_hT, tT1(H))
                    v.tensor_scalar(tmpb[:, par * 64:par * 64 + 64],
                                    pb[0:128, 128 + par * 64:192 + par * 64],
                                    dexp[:, h:h + 1],
                                    None, ALU.mult).then_inc(s_hV, 1)
                    v.tensor_scalar(Pts[:, par * 64:par * 64 + 64],
                                    Prun[:, h * 64:(h + 1) * 64], fcols[:, h:h + 1],
                                    None, ALU.mult).then_inc(s_hV, 1)
                    v.wait_ge(s_hT, tT2(H))
                    v.wait_ge(s_hV, vccH(H, 5))
                    v.tensor_tensor(ubuf[:, par * 64:par * 64 + 64],
                                    pY[0:128, 0:64], Dx[:, par * 64:par * 64 + 64],
                                    ALU.add).then_inc(s_hV, 1)
                    v.wait_ge(s_hV, vccH(H, 8))
                    v.tensor_tensor(
                        ynat[:, (c % 2) * 2048 + h * 64:(c % 2) * 2048 + (h + 1) * 64],
                        ubuf[:, par * 64:par * 64 + 64],
                        tmpb[:, par * 64:par * 64 + 64], ALU.add).then_inc(s_hV, 1)
                    v.wait_ge(s_hT, tT3(H))
                    v.wait_ge(s_hV, vccH(H, 7))
                    v.tensor_tensor(Prun[:, h * 64:(h + 1) * 64],
                                    Pts[:, par * 64:par * 64 + 64], pSl[0:64, 0:64],
                                    ALU.add).then_inc(s_hV, 1)

            _sec("T", heads_T)
            _sec("S", heads_S)
            _sec("V", heads_V)
            def mkT(t, c=c):
                t.wait_ge(s_hV, VPC * (c + 1))
                t.wait_ge(s_hS, NH * (c + 1))
                t.wait_ge(s_hT, 3 * NH * (c + 1))
                return t.matmul(p[0][0:1, 0:1], cb["ones_row_f32"][0:1, 0:1],
                                cb["ones_row_f32"][0:1, 0:1], start=True, stop=True)
            stepT(mkT)
            def mkV(v, c=c):
                v.wait_ge(s_hV, VPC * (c + 1))
                return v.tensor_copy(lnsm[0:1, 0:1], lnsm[0:1, 0:1])
            stepV(mkV)
            def mkS(s, c=c):
                s.wait_ge(s_hS, NH * (c + 1))
                return s.activation(lnsm[0:1, 4:5], lnsm[0:1, 4:5], AF.Copy)
            stepS(mkS)
            stepY(lambda g, c=c: g.dma_start(
                out=yn_d[c * 128:(c + 1) * 128, :],
                in_=ynat[:, (c % 2) * 2048:(c % 2) * 2048 + 2048]))

        if upto < 3:
            @block.sync
            def _(g):
                g.wait_ge(s_Y, 16 * cnt["Y"])
            return
        stepG(lambda g: g.dma_start(out=stin_d.ap(), in_=Prun[:, :]))
        def coll(g):
            return g.collective_compute(
                "AllGather", mybir.AluOpType.bypass,
                replica_groups=rg,
                ins=[stin_d.ap().opt()],
                outs=[stout_d.ap().opt()],
            )
        _step("G", lambda e: coll(e).then_inc(s_AG), ndma=0)

        @block.gpsimd
        def _(g):
            g.wait_ge(s_AG, 1)
        stepG(lambda g: g.dma_start(out=Prun[:, :], in_=stout_d.ap()[0:64, :]))
        stepV(lambda v: v.tensor_scalar(Sin[:, :], Prun[:, :], prevc[:, 0:1], None, ALU.mult))

        if upto < 4:
            @block.sync
            def _(g):
                g.wait_ge(s_Y, 16 * cnt["Y"])
            return
        stepG_multi([
            (lambda g, dm=dm: g.dma_start(
                out=woutb.rearrange("p (d2 j c) -> p d2 j c", d2=8, j=16)[:, dm, :, :],
                in_=wout_in.rearrange("(j p) c -> p j c", p=128)
                [:, :, dm * 128:(dm + 1) * 128]))
            for dm in range(8)])
        def p2loads(c2):
            return [
                (lambda g, c2=c2: g.dma_start(
                    out=yrl[:, (c2 % 2) * 2048:(c2 % 2) * 2048 + 2048],
                    in_=yn_d[c2 * 128:(c2 + 1) * 128, :])),
                (lambda g, c2=c2: g.dma_start(
                    out=zrl[:, (c2 % 2) * 2048:(c2 % 2) * 2048 + 2048],
                    in_=z_d[c2 * 128:(c2 + 1) * 128, :])),
                (lambda g, c2=c2: g.dma_start(
                    out=xbcr[:, (c2 % 2) * 131:(c2 % 2) * 131 + 131],
                    in_=xbc_d[17 * 128:18 * 128, c2 * 128:c2 * 128 + 131])),
            ]
        stepG_multi(p2loads(0))
        for c in range(NSC):
            snap_pre = {k2: cnt[k2] for k2 in cnt}
            def convC(v, c=c):
                xb0 = (c % 2) * 131
                vbump(v.tensor_scalar(cvA[:, 0:128], xbcr[:, xb0:xb0 + 128],
                                      convw[:, 4 * 17:4 * 17 + 1],
                                      convb[:, 17:18], ALU.mult, ALU.add))
                for jj in range(1, 4):
                    vbump(v.tensor_scalar(cvB[:, 0:128], xbcr[:, xb0 + jj:xb0 + jj + 128],
                                          convw[:, 4 * 17 + jj:4 * 17 + jj + 1],
                                          None, ALU.mult))
                    vfence(v)
                    ins = v.tensor_tensor(cvA[:, 0:128], cvA[:, 0:128],
                                          cvB[:, 0:128], ALU.add)
                    if jj < 3:
                        vbump(ins)
                        vfence(v)
                return ins
            stepV(convC)
            stepS(lambda s: s.activation(ctl[:, :], cvA[:, 0:128], AF.Silu))
            def umms(t):
                for i in range(4):
                    ins = t.matmul(p[i][:, :], ctl[0:64, :],
                                   Sin[:, i * 512:(i + 1) * 512],
                                   start=True, stop=True)
                return ins
            stepT(umms)
            def corr(v, c=c):
                for h in range(NH):
                    dsl = Dx[:, (h % 2) * 64:(h % 2) * 64 + 64]
                    vbump(v.tensor_scalar(
                        dsl, p[h // 8][:, (h % 8) * 64:(h % 8) * 64 + 64],
                        dadj[:, c * 32 + h: c * 32 + h + 1], None, ALU.mult))
                    vfence(v)
                    ins = v.tensor_tensor(
                        yrl[:, (c % 2) * 2048 + h * 64:(c % 2) * 2048 + (h + 1) * 64],
                        yrl[:, (c % 2) * 2048 + h * 64:(c % 2) * 2048 + (h + 1) * 64],
                        dsl, ALU.add)
                    if h < NH - 1:
                        vbump(ins)
                return ins
            stepV(corr)
            stepS(lambda s, c=c: s.activation(zsil[:, :],
                                              zrl[:, (c % 2) * 2048:(c % 2) * 2048 + 2048],
                                              AF.Silu))
            def gat1(v, c=c):
                vbump(v.tensor_tensor(ynf[:, :],
                                      yrl[:, (c % 2) * 2048:(c % 2) * 2048 + 2048],
                                      zsil[:, :], ALU.mult))
                vfence(v)
                vbump(v.tensor_tensor(zsil[:, :], ynf[:, :], ynf[:, :], ALU.mult))
                vfence(v)
                vbump(v.reduce_sum(rstd1[:, :], zsil[:, :], axis=AX))
                vfence(v)
                return v.tensor_scalar(rstd1[:, :], rstd1[:, :], 1.0 / 2048.0,
                                       None, ALU.mult)
            stepV(gat1)
            def gat2(s):
                sbump(s.activation(rstd1[:, :], rstd1[:, :], AF.Ln,
                                   bias=eps128[:, 0:1]))
                sfence(s)
                return s.activation(rstd1[:, :], rstd1[:, :], AF.Exp, scale=-0.5)
            stepS(gat2)
            def gat3(v):
                vbump(v.tensor_scalar(ynf[:, :], ynf[:, :], rstd1[:, 0:1],
                                      None, ALU.mult))
                vfence(v)
                return v.tensor_tensor(ynf[:, :], ynf[:, :], nwb[:, :], ALU.mult)
            stepV(gat3)
            for dq in range(4):
                def tmm(t, dq=dq):
                    for k in range(4):
                        di = 4 * dq + k
                        ins = t.matmul(p[k][0:128, 0:128],
                                       ynf[:, di * 128:(di + 1) * 128],
                                       cb["ident_bf"][:, :], start=True, stop=True)
                    return ins
                stepT(tmm)
                def tcp(v, dq=dq, c=c):
                    for k in range(4):
                        di = 4 * dq + k
                        ins = v.tensor_copy(
                            ynT[:, di * 512 + (c % 4) * 128:
                                di * 512 + (c % 4) * 128 + 128],
                            p[k][0:128, 0:128])
                        if k < 3:
                            vbump(ins)
                    return ins
                stepV(tcp)
            if c + 1 < NSC:
                def pf(e, c=c, snap_pre=snap_pre):
                    for k2, v2 in snap_pre.items():
                        if v2:
                            e.wait_ge(SEMS[k2], v2 * (16 if k2 in ("G", "Y") else 1))
                    for f in p2loads(c + 1):
                        f(e).then_inc(s_G, 16)
                _sec("G", pf)
                cnt["G"] += 3
            if c % 4 == 3:
                oc = c // 4
                stepG_multi([
                    (lambda g, j=j, oc=oc: g.dma_start(
                        out=tile(xf32, j, CH),
                        in_=x_in[128 * j:128 * (j + 1), 3 + oc * CH: 3 + (oc + 1) * CH]))
                    for j in range(NT)])
                for dp2 in range(4):
                    def womm(t, dp2=dp2):
                        for k in range(2):
                            dm = 2 * dp2 + k
                            for di in range(16):
                                ins = t.matmul(p[6 + k][:, :],
                                               woutb[:, dm * 16 * 128 + di * 128:
                                                     dm * 16 * 128 + (di + 1) * 128],
                                               ynT[:, di * 512:(di + 1) * 512],
                                               start=(di == 0), stop=(di == 15))
                        return ins
                    stepT(womm)
                    def wadd(v, dp2=dp2):
                        for k in range(2):
                            dm = 2 * dp2 + k
                            ins = v.tensor_tensor(
                                tile(xf32, dm, CH), tile(xf32, dm, CH), p[6 + k][:, :],
                                ALU.add)
                            if k == 0:
                                vbump(ins)
                        return ins
                    stepV(wadd)
                stepY(lambda g, oc=oc: g.dma_start(
                    out=y_out.rearrange("(j p) c -> p j c", p=128)
                    [:, :, oc * CH:(oc + 1) * CH],
                    in_=xf32.rearrange("p (j c) -> p j c", j=NT)))

        @block.sync
        def _(g):
            g.wait_ge(s_Y, 16 * cnt["Y"])



D_CONSTS = ("ones_col_bf", "ones_row_bf")


def build_D():
    nc = bass.Bass("TRN2", target_bir_lowering=False)
    caps = declare_constants(nc, D_CONSTS)
    x_in = nc.dram_tensor("xTd", [D, 2049], F32, kind="ExternalInput").ap()
    wr_in = nc.dram_tensor("WrT", [D, D], F32, kind="ExternalInput").ap()
    wk_in = nc.dram_tensor("WkT", [D, D], F32, kind="ExternalInput").ap()
    wv_in = nc.dram_tensor("WvT", [D, D], F32, kind="ExternalInput").ap()
    mur_in = nc.dram_tensor("mur", [128, 8], F32, kind="ExternalInput").ap()
    muk_in = nc.dram_tensor("muk", [128, 8], F32, kind="ExternalInput").ap()
    y_out = nc.dram_tensor("yT", [D, T], F32, kind="ExternalOutput").ap()
    emit_D(nc, caps, x_in, wr_in, wk_in, wv_in, mur_in, muk_in, y_out)
    return nc


def emit_D(nc, caps, x_in, wr_in, wk_in, wv_in, mur_in, muk_in, y_out, P="D_"):

    LNCH = [(0, 1), (1, 512), (513, 512), (1025, 512), (1537, 512)]

    with ExitStack() as ctx:
        def sb(name, shape, dt):
            return ctx.enter_context(nc.sbuf_tensor(P + name, shape, dt))
        def sem(name):
            return ctx.enter_context(nc.semaphore(P + name))

        s_T = sem("s_T"); s_V = sem("s_V"); s_S = sem("s_S")
        s_G = sem("s_G"); s_Y = sem("s_Y"); s_lw = sem("s_lw")

        xln = sb("xlnD", [128, NT * 2049], BF16)
        wrb = sb("wrbD", [128, NT * 1024], BF16)
        wkb = sb("wkbD", [128, NT * 1024], BF16)
        wvb = sb("wvbD", [128, NT * 1024], BF16)
        rT = sb("rTD", [128, NT * T], BF16)
        k2T = sb("k2TD", [128, NT * CH], BF16)
        xsh = sb("xshD", [128, NT * CH], BF16)
        xf32 = sb("xf32D", [128, NT * CH], F32)
        mur = sb("murD", [128, 8], F32)
        muk = sb("mukD", [128, 8], F32)
        omr = sb("omrD", [128, 8], F32)
        omk = sb("omkD", [128, 8], F32)
        lnsm = sb("lnsmD", [1, 2 * CH], F32)
        pair_bf = sb("pairD", [1, 2 * CH], BF16)
        rbuf = sb("rbufD", [128, 2 * CH], BF16)
        kst = sb("kstD", [128, CH], BF16)
        kst2 = sb("kst2D", [128, CH], BF16)
        eps1 = sb("eps1D", [1, 1], F32)
        cb = {n: sb("cD_" + n, shp, dt) for n, shp, dt in CONST_SPEC if n in D_CONSTS}
        xbf = rT[:, 0:4096]
        sqb = rT[:, 4096:8192]

        p = [ctx.enter_context(nc.psum_tensor(P + "pd%d" % i, [128, CH], F32))
             for i in range(6)]

        block = ctx.enter_context(nc.Block())

        cnt = {"T": 0, "V": 0, "S": 0, "G": 0, "Y": 0}
        SEMS = {"T": s_T, "V": s_V, "S": s_S, "G": s_G, "Y": s_Y}

        def _sec(k, body):
            if k == "T":
                @block.tensor
                def _(e): body(e)
            elif k == "V":
                @block.vector
                def _(e): body(e)
            elif k == "S":
                @block.scalar
                def _(e): body(e)
            elif k == "G":
                @block.gpsimd
                def _(e): body(e)
            else:
                @block.sync
                def _(e): body(e)

        def _step(k, f, ndma=1):
            snap = {k2: cnt[k2] for k2 in cnt}
            def body(e):
                for k2, v in snap.items():
                    if v == 0:
                        continue
                    e.wait_ge(SEMS[k2], v * (16 if k2 in ("G", "Y") else 1))
                f(e)
            _sec(k, body)
            cnt[k] += ndma if k in ("G", "Y") else 1

        def stepT(f): _step("T", lambda e: f(e).then_inc(s_T, 1))
        def stepV(f): _step("V", lambda e: f(e).then_inc(s_V, 1))
        def stepS(f): _step("S", lambda e: f(e).then_inc(s_S, 1))
        def stepG(f): _step("G", lambda e: f(e).then_inc(s_G, 16))
        def stepY(f): _step("Y", lambda e: f(e).then_inc(s_Y, 16))

        def stepG_multi(fs):
            def body(e):
                for f in fs:
                    f(e).then_inc(s_G, 16)
            _step("G", body, ndma=len(fs))

        s_vself = sem("s_vself"); s_sself = sem("s_sself")
        _mf = {"V": [0, s_vself], "S": [0, s_sself]}

        def mk_chain(k):
            st = _mf[k]
            def bump(ins):
                ins.then_inc(st[1], 1)
                st[0] += 1
            def fence(e):
                e.wait_ge(st[1], st[0])
            return bump, fence
        vbump, vfence = mk_chain("V")
        sbump, sfence = mk_chain("S")

        NLW = len(D_CONSTS) + 2

        @block.sync
        def _(g):
            for name in D_CONSTS:
                g.dma_start(out=cb[name][:, :], in_=caps[name]).then_inc(s_lw, 16)
            g.dma_start(out=mur[:, :], in_=mur_in).then_inc(s_lw, 16)
            g.dma_start(out=muk[:, :], in_=muk_in).then_inc(s_lw, 16)

        for k in ("T", "V", "S", "G", "Y"):
            _sec(k, lambda e: e.wait_ge(s_lw, 16 * NLW))

        stepG_multi([
            (lambda g: dma_wtiles(g, wrb[:, :], wr_in, NT)),
            (lambda g: dma_wtiles(g, wkb[:, :], wk_in, NT)),
            (lambda g: dma_wtiles(g, wvb[:, :], wv_in, NT)),
        ])
        stepV(lambda v: v.memset(eps1[0:1, :], LN_EPS))
        stepV(lambda v: v.tensor_scalar(omr[:, :], mur[:, :], -1.0, 1.0, ALU.mult, ALU.add))
        stepV(lambda v: v.tensor_scalar(omk[:, :], muk[:, :], -1.0, 1.0, ALU.mult, ALU.add))

        for ci, (t0, tw) in enumerate(LNCH):
            stepG_multi([
                (lambda g, j=j, t0=t0, tw=tw: g.dma_start(
                    out=xbf[:, j * 512:j * 512 + tw],
                    in_=x_in[128 * j:128 * (j + 1), t0:t0 + tw],
                    allow_slow_non_contiguous=(tw == 1)))
                for j in range(NT)])
            def sqs(s, tw=tw):
                for j in range(NT):
                    ins = s.activation(sqb[:, j * 512:j * 512 + tw],
                                       xbf[:, j * 512:j * 512 + tw], AF.Square)
                return ins
            stepS(sqs)
            def stats(t, tw=tw):
                for j in range(NT):
                    t.matmul(p[2][0:1, 0:tw], cb["ones_col_bf"][:, :],
                             xbf[:, j * 512:j * 512 + tw], start=(j == 0), stop=(j == NT - 1))
                for j in range(NT):
                    ins = t.matmul(p[3][0:1, 0:tw], cb["ones_col_bf"][:, :],
                                   sqb[:, j * 512:j * 512 + tw], start=(j == 0),
                                   stop=(j == NT - 1))
                return ins
            stepT(stats)
            def lnv1(v, tw=tw):
                vbump(v.tensor_scalar(lnsm[0:1, 0:tw], p[2][0:1, 0:tw],
                                      1.0 / D, None, ALU.mult))
                vbump(v.tensor_scalar(lnsm[0:1, CH:CH + tw], p[3][0:1, 0:tw],
                                      1.0 / D, None, ALU.mult))
                vfence(v)
                vbump(v.tensor_tensor(p[2][0:1, 0:tw], lnsm[0:1, 0:tw],
                                      lnsm[0:1, 0:tw], ALU.mult))
                vfence(v)
                return v.tensor_tensor(lnsm[0:1, CH:CH + tw], lnsm[0:1, CH:CH + tw],
                                       p[2][0:1, 0:tw], ALU.subtract)
            stepV(lnv1)
            def lns1(s, tw=tw):
                sbump(s.activation(lnsm[0:1, CH:CH + tw], lnsm[0:1, CH:CH + tw],
                                   AF.Ln, bias=eps1[0:1, 0:1]))
                sfence(s)
                return s.activation(lnsm[0:1, CH:CH + tw], lnsm[0:1, CH:CH + tw],
                                    AF.Exp, scale=-0.5)
            stepS(lns1)
            def lnv2(v, tw=tw):
                vbump(v.tensor_copy(pair_bf[0:1, 0:tw], lnsm[0:1, CH:CH + tw]))
                vbump(v.tensor_tensor(lnsm[0:1, 0:tw], lnsm[0:1, 0:tw],
                                      lnsm[0:1, CH:CH + tw], ALU.mult))
                vfence(v)
                return v.tensor_copy(pair_bf[0:1, CH:CH + tw], lnsm[0:1, 0:tw])
            stepV(lnv2)
            def bcasts(t, tw=tw):
                t.matmul(p[4][:, 0:tw], cb["ones_row_bf"][0:1, :], pair_bf[0:1, 0:tw],
                         start=True, stop=True)
                return t.matmul(p[5][:, 0:tw], cb["ones_row_bf"][0:1, :],
                                pair_bf[0:1, CH:CH + tw], start=True, stop=True)
            stepT(bcasts)
            def lnv3(v, t0=t0, tw=tw):
                vbump(v.tensor_copy(rbuf[:, 0:tw], p[4][:, 0:tw]))
                vbump(v.tensor_copy(rbuf[:, CH:CH + tw], p[5][:, 0:tw]))
                vfence(v)
                for j in range(NT):
                    vbump(v.tensor_tensor(xln[:, j * 2049 + t0: j * 2049 + t0 + tw],
                                          xbf[:, j * 512:j * 512 + tw], rbuf[:, 0:tw],
                                          ALU.mult))
                vfence(v)
                for j in range(NT):
                    ins = v.tensor_tensor(xln[:, j * 2049 + t0: j * 2049 + t0 + tw],
                                          xln[:, j * 2049 + t0: j * 2049 + t0 + tw],
                                          rbuf[:, CH:CH + tw], ALU.subtract)
                    if j < NT - 1:
                        vbump(ins)
                return ins
            stepV(lnv3)

        for ci in range(4):
            off = 1 + ci * 512
            def shmix(v, off=off, mu=mur, om=omr):
                for j in range(NT):
                    v.tensor_scalar(xsh[:, j * 512:(j + 1) * 512],
                                    xln[:, j * 2049 + off: j * 2049 + off + 512],
                                    mu[:, j:j + 1], None, ALU.mult)
                for j in range(NT):
                    ins = v.tensor_scalar(kst[:, 0:512] if False else
                                          xsh[:, j * 512:(j + 1) * 512],
                                          xln[:, j * 2049 + off - 1: j * 2049 + off - 1 + 512],
                                          om[:, j:j + 1], None, ALU.mult)
                return ins
            def sh1(v, off=off, mu=mur):
                for j in range(NT):
                    ins = v.tensor_scalar(xsh[:, j * 512:(j + 1) * 512],
                                          xln[:, j * 2049 + off: j * 2049 + off + 512],
                                          mu[:, j:j + 1], None, ALU.mult)
                return ins
            def sh2(v, off=off, om=omr):
                for j in range(NT):
                    ins = v.tensor_scalar(sqb[:, j * 512:(j + 1) * 512] if False else
                                          xf32.bitcast(BF16)[:, j * 512:(j + 1) * 512]
                                          if False else
                                          kst[:, 0:512],
                                          xln[:, j * 2049 + off - 1: j * 2049 + off - 1 + 512],
                                          om[:, j:j + 1], None, ALU.mult)
                return ins
            def shiftr(v, off=off):
                for j in range(NT):
                    tmp = kst[:, :] if j % 2 == 0 else kst2[:, 0:512]
                    vbump(v.tensor_scalar(
                        xsh[:, j * 512:(j + 1) * 512],
                        xln[:, j * 2049 + off: j * 2049 + off + 512],
                        mur[:, j:j + 1], None, ALU.mult))
                    vbump(v.tensor_scalar(
                        tmp, xln[:, j * 2049 + off - 1: j * 2049 + off - 1 + 512],
                        omr[:, j:j + 1], None, ALU.mult))
                    vfence(v)
                    ins = v.tensor_tensor(
                        xsh[:, j * 512:(j + 1) * 512], xsh[:, j * 512:(j + 1) * 512],
                        tmp, ALU.add)
                    if j < NT - 1:
                        vbump(ins)
                return ins
            stepV(shiftr)
            for op2 in range(NT // 2):
                def rmm(t, op2=op2):
                    for k in range(2):
                        ot = 2 * op2 + k
                        for j in range(NT):
                            ins = t.matmul(p[k][:, :],
                                           wrb[:, j * 1024 + ot * 128:
                                               j * 1024 + ot * 128 + 128],
                                           xsh[:, j * 512:(j + 1) * 512],
                                           start=(j == 0), stop=(j == NT - 1))
                    return ins
                stepT(rmm)
                def rsig(s, op2=op2, ci=ci):
                    for k in range(2):
                        ot = 2 * op2 + k
                        ins = s.activation(
                            rT[:, ot * T + ci * 512: ot * T + ci * 512 + 512],
                            p[k][:, :], AF.Sigmoid)
                    return ins
                stepS(rsig)

        for ci in range(4):
            off = 1 + ci * 512
            def shiftk(v, off=off):
                for j in range(NT):
                    tmp = kst[:, :] if j % 2 == 0 else kst2[:, 0:512]
                    vbump(v.tensor_scalar(
                        xsh[:, j * 512:(j + 1) * 512],
                        xln[:, j * 2049 + off: j * 2049 + off + 512],
                        muk[:, j:j + 1], None, ALU.mult))
                    vbump(v.tensor_scalar(
                        tmp, xln[:, j * 2049 + off - 1: j * 2049 + off - 1 + 512],
                        omk[:, j:j + 1], None, ALU.mult))
                    vfence(v)
                    ins = v.tensor_tensor(
                        xsh[:, j * 512:(j + 1) * 512], xsh[:, j * 512:(j + 1) * 512],
                        tmp, ALU.add)
                    if j < NT - 1:
                        vbump(ins)
                return ins
            stepV(shiftk)
            for op2 in range(NT // 2):
                def kmm(t, op2=op2):
                    for k in range(2):
                        ot = 2 * op2 + k
                        for j in range(NT):
                            ins = t.matmul(p[k][:, :],
                                           wkb[:, j * 1024 + ot * 128:
                                               j * 1024 + ot * 128 + 128],
                                           xsh[:, j * 512:(j + 1) * 512],
                                           start=(j == 0), stop=(j == NT - 1))
                    return ins
                stepT(kmm)
                def krel(s, op2=op2):
                    for k in range(2):
                        ot = 2 * op2 + k
                        ins = s.activation(k2T[:, ot * 512:(ot + 1) * 512],
                                           p[k][:, :], AF.Relu)
                    return ins
                stepS(krel)
                def ksq(v, op2=op2):
                    for k in range(2):
                        ot = 2 * op2 + k
                        ins = v.tensor_tensor(
                            k2T[:, ot * 512:(ot + 1) * 512],
                            k2T[:, ot * 512:(ot + 1) * 512],
                            k2T[:, ot * 512:(ot + 1) * 512], ALU.mult)
                        if k == 0:
                            vbump(ins)
                    return ins
                stepV(ksq)
            stepG_multi([
                (lambda g, j=j, ci=ci: g.dma_start(
                    out=tile(xf32, j, CH),
                    in_=x_in[128 * j:128 * (j + 1), 1 + ci * CH: 1 + (ci + 1) * CH]))
                for j in range(NT)])
            for op2 in range(NT // 2):
                def vmm(t, op2=op2):
                    for k in range(2):
                        ot = 2 * op2 + k
                        for j in range(NT):
                            ins = t.matmul(p[2 + k][:, :],
                                           wvb[:, j * 1024 + ot * 128:
                                               j * 1024 + ot * 128 + 128],
                                           k2T[:, j * 512:(j + 1) * 512],
                                           start=(j == 0), stop=(j == NT - 1))
                    return ins
                stepT(vmm)
                def vadd(v, op2=op2, ci=ci):
                    for k in range(2):
                        ot = 2 * op2 + k
                        tmp = kst[:, :] if k == 0 else kst2[:, 0:512]
                        vbump(v.tensor_tensor(
                            tmp, p[2 + k][:, :],
                            rT[:, ot * T + ci * 512: ot * T + ci * 512 + 512],
                            ALU.mult))
                        vfence(v)
                        ins = v.tensor_tensor(tile(xf32, ot, CH), tile(xf32, ot, CH),
                                              tmp, ALU.add)
                        if k == 0:
                            vbump(ins)
                    return ins
                stepV(vadd)
            stepY(lambda g, ci=ci: g.dma_start(
                out=y_out.rearrange("(j p) c -> p j c", p=128)[:, :, ci * CH:(ci + 1) * CH],
                in_=xf32.rearrange("p (j c) -> p j c", j=NT)))

        @block.sync
        def _(g):
            g.wait_ge(s_Y, 16 * cnt["Y"])



def emit_glue(nc, nb, src_ap, dst_ap, prev_ap, rg, P):
    bin_d = nc.dram_tensor(P + "bin", [128, NT * nb], F32)
    bout_d = nc.dram_tensor(P + "bout", [256, NT * nb], F32)
    with ExitStack() as ctx:
        bb = ctx.enter_context(nc.sbuf_tensor(P + "bb", [128, NT * nb], F32))
        pv = ctx.enter_context(nc.sbuf_tensor(P + "pv", [128, 1], F32))
        sg = ctx.enter_context(nc.semaphore(P + "sg"))
        sa = ctx.enter_context(nc.semaphore(P + "sa"))
        sv = ctx.enter_context(nc.semaphore(P + "sv"))
        block = ctx.enter_context(nc.Block())

        @block.gpsimd
        def _(g):
            g.dma_start(out=pv[:, :], in_=prev_ap).then_inc(sg, 16)
            g.dma_start(out=bb.rearrange("p (j c) -> p j c", j=NT),
                        in_=src_ap.rearrange("(j p) c -> p j c", p=128),
                        allow_slow_non_contiguous=(nb == 1)).then_inc(sg, 16)
            g.wait_ge(sg, 32)
            g.dma_start(out=bin_d.ap(), in_=bb[:, :]).then_inc(sg, 16)
            g.wait_ge(sg, 48)
            g.collective_compute(
                "AllGather", mybir.AluOpType.bypass,
                replica_groups=rg,
                ins=[bin_d.ap().opt()],
                outs=[bout_d.ap().opt()],
            ).then_inc(sa)
            g.wait_ge(sa, 1)
            g.dma_start(out=bb[:, :], in_=bout_d.ap()[0:128, :]).then_inc(sg, 16)
            g.wait_ge(sv, 1)
            g.dma_start(out=dst_ap.rearrange("(j p) c -> p j c", p=128),
                        in_=bb.rearrange("p (j c) -> p j c", j=NT),
                        allow_slow_non_contiguous=(nb == 1)).then_inc(sg, 16)
            g.wait_ge(sg, 80)

        @block.vector
        def _(v):
            v.wait_ge(sg, 64)
            v.tensor_scalar(bb[:, :], bb[:, :], pv[:, 0:1], None,
                            ALU.mult).then_inc(sv, 1)


def build_fused(n_cores=8):
    nc = bass.Bass("TRN2", target_bir_lowering=False)
    caps = declare_constants(nc)
    rg = [[2 * i, 2 * i + 1] for i in range(max(1, n_cores // 2))]
    x_in = nc.dram_tensor("xT_full", [D, TF], F32, kind="ExternalInput").ap()
    wq_in = nc.dram_tensor("wq", [D, D], F32, kind="ExternalInput").ap()
    wk_in = nc.dram_tensor("wk", [D, D], F32, kind="ExternalInput").ap()
    wo_in = nc.dram_tensor("wo", [D, D], F32, kind="ExternalInput").ap()
    bo_in = nc.dram_tensor("bo", [128, 8], F32, kind="ExternalInput").ap()
    w1_in = nc.dram_tensor("w1T_all", [9, D, 4096], F32, kind="ExternalInput").ap()
    w2_in = nc.dram_tensor("w2T_all", [9, 8, 4096, 128], F32, kind="ExternalInput").ap()
    b1_in = nc.dram_tensor("b1_all", [9, 128, 32], F32, kind="ExternalInput").ap()
    b2_in = nc.dram_tensor("b2_all", [9, 1024], F32, kind="ExternalInput").ap()
    rw_in = nc.dram_tensor("rwT", [D, 8], F32, kind="ExternalInput").ap()
    rb_in = nc.dram_tensor("rb", [1, 8], F32, kind="ExternalInput").ap()
    sel9_in = nc.dram_tensor("sel9", [9, 1024], BF16, kind="ExternalInput").ap()
    win_in = nc.dram_tensor("W_in_p", [D, PROJ_COLS], F32, kind="ExternalInput").ap()
    wout_in = nc.dram_tensor("W_out", [2048, D], F32, kind="ExternalInput").ap()
    convw_in = nc.dram_tensor("convw", [128, NCVT * 4], F32, kind="ExternalInput").ap()
    convb_in = nc.dram_tensor("convb", [128, NCVT], F32, kind="ExternalInput").ap()
    alog_r = nc.dram_tensor("alog_r", [1, 32], F32, kind="ExternalInput").ap()
    dtb_in = nc.dram_tensor("dtb", [32, 1], F32, kind="ExternalInput").ap()
    dn_in = nc.dram_tensor("Dn", [128, 32], F32, kind="ExternalInput").ap()
    nw_in = nc.dram_tensor("nw_row", [1, 2048], F32, kind="ExternalInput").ap()
    prev_in = nc.dram_tensor("prev_col", [64, 1], F32, kind="ExternalInput").ap()
    prev128_in = nc.dram_tensor("prev128", [128, 1], F32, kind="ExternalInput").ap()
    ut_in = nc.dram_tensor("UTones", [128, 128], F32, kind="ExternalInput").ap()
    id32_in = nc.dram_tensor("ident32", [32, 32], F32, kind="ExternalInput").ap()
    mst_in = nc.dram_tensor("maskST", [128, 128], F32, kind="ExternalInput").ap()
    sel32_in = nc.dram_tensor("sel32", [32, 32 * 128], F32, kind="ExternalInput").ap()
    wr_in = nc.dram_tensor("WrT", [D, D], F32, kind="ExternalInput").ap()
    wkd_in = nc.dram_tensor("WkT", [D, D], F32, kind="ExternalInput").ap()
    wv_in = nc.dram_tensor("WvT", [D, D], F32, kind="ExternalInput").ap()
    mur_in = nc.dram_tensor("mur", [128, 8], F32, kind="ExternalInput").ap()
    muk_in = nc.dram_tensor("muk", [128, 8], F32, kind="ExternalInput").ap()
    y_out = nc.dram_tensor("yT", [D, T], F32, kind="ExternalOutput").ap()
    xb_d = nc.dram_tensor("xb_d", [D, T], F32).ap()
    xc_d = nc.dram_tensor("xc_d", [D, 2051], F32).ap()
    xd_d = nc.dram_tensor("xd_d", [D, 2049], F32).ap()

    emit_A(nc, caps, x_in, wq_in, wk_in, wo_in, bo_in, xb_d, P="A_")
    emit_B(nc, caps, xb_d, w1_in, w2_in, b1_in, b2_in, rw_in, rb_in, sel9_in,
           xc_d[:, 3:2051], P="B_")
    emit_glue(nc, 3, xc_d[:, 2048:2051], xc_d[:, 0:3], prev128_in, rg, P="G1_")
    emit_C(nc, caps, xc_d, win_in, wout_in, convw_in, convb_in, alog_r, dtb_in,
           dn_in, nw_in, prev_in, ut_in, id32_in, mst_in, sel32_in,
           xd_d[:, 1:2049], n_cores=n_cores, P="C_")
    emit_glue(nc, 1, xd_d[:, 2048:2049], xd_d[:, 0:1], prev128_in, rg, P="G2_")
    emit_D(nc, caps, xd_d, wr_in, wkd_in, wv_in, mur_in, muk_in, y_out, P="D_")
    return nc


def _roll_xT(xb, half):
    if half == 0:
        arr = xb
    else:
        arr = np.concatenate([xb[2048:], xb[:2048]], axis=0)
    return np.ascontiguousarray(arr.T)


def _hostA(consts, ins, x, c):
    b, half = c // 2, c % 2
    m = dict(consts)
    m["xT_full"] = _roll_xT(x[b], half)
    m["wq"] = np.ascontiguousarray(ins["L0_attn_Wq"])
    m["wk"] = np.ascontiguousarray(ins["L0_attn_Wk"])
    m["wo"] = np.ascontiguousarray(ins["L0_attn_Wo"])
    m["bo"] = np.ascontiguousarray(ins["L0_attn_bo"].reshape(8, 128).T)
    return m


def _sharedB(ins):
    m = {}
    w1 = np.concatenate([ins["L0_ffn_shared_w1"][None], ins["L0_ffn_spec_w1"]], 0)
    w2 = np.concatenate([ins["L0_ffn_shared_w2"][None], ins["L0_ffn_spec_w2"]], 0)
    b1 = np.concatenate([ins["L0_ffn_shared_b1"][None], ins["L0_ffn_spec_b1"]], 0)
    b2 = np.concatenate([ins["L0_ffn_shared_b2"][None], ins["L0_ffn_spec_b2"]], 0)
    m["w1T_all"] = np.ascontiguousarray(np.transpose(w1, (0, 2, 1)))
    w2T = np.transpose(w2, (0, 2, 1))
    m["w2T_all"] = np.ascontiguousarray(w2T.reshape(9, 4096, 8, 128).transpose(0, 2, 1, 3))
    m["b1_all"] = np.ascontiguousarray(b1.reshape(9, 32, 128).transpose(0, 2, 1))
    m["b2_all"] = np.ascontiguousarray(b2)
    m["rwT"] = np.ascontiguousarray(ins["L0_ffn_router_w"].T)
    m["rb"] = np.ascontiguousarray(ins["L0_ffn_router_b"][None, :])
    return m


def _sharedC(ins):
    m = {}
    W_in = ins["L1_attn_W_in"]
    wp = np.zeros((1024, 4384), np.float32)
    wp[:, 0:2048] = W_in[:, 0:2048]
    wp[:, 2048:4096] = W_in[:, 2048:4096]
    wp[:, 4096:4160] = W_in[:, 4096:4160]
    wp[:, 4224:4288] = W_in[:, 4160:4224]
    wp[:, 4352:4384] = W_in[:, 4224:4256]
    m["W_in_p"] = wp
    m["W_out"] = np.ascontiguousarray(ins["L1_attn_W_out"])
    cw = ins["L1_attn_conv_w"][:, 0, :]
    cb_ = ins["L1_attn_conv_b"]
    cwp = np.zeros((18 * 128, 4), np.float32)
    cbp = np.zeros((18 * 128,), np.float32)
    cwp[0:2048] = cw[0:2048]; cbp[0:2048] = cb_[0:2048]
    cwp[16 * 128:16 * 128 + 64] = cw[2048:2112]
    cbp[16 * 128:16 * 128 + 64] = cb_[2048:2112]
    cwp[17 * 128:17 * 128 + 64] = cw[2112:2176]
    cbp[17 * 128:17 * 128 + 64] = cb_[2112:2176]
    m["convw"] = np.ascontiguousarray(cwp.reshape(18, 128, 4).transpose(1, 0, 2).reshape(128, 72))
    m["convb"] = np.ascontiguousarray(cbp.reshape(18, 128).T)
    m["alog_r"] = np.ascontiguousarray(ins["L1_attn_A_log"][None, :])
    m["dtb"] = np.ascontiguousarray(ins["L1_attn_dt_bias"][:, None])
    m["Dn"] = np.ascontiguousarray(np.tile(ins["L1_attn_D_param"][None, :], (128, 1)))
    m["nw_row"] = np.ascontiguousarray(ins["L1_attn_norm_w"][None, :])
    m["UTones"] = np.triu(np.ones((128, 128), np.float32))
    mstv = np.zeros((128, 128), np.float32)
    mstv[np.tril_indices(128, -1)] = -1e30
    m["maskST"] = mstv
    m["ident32"] = np.eye(32, dtype=np.float32)
    s32 = np.zeros((32, 32 * 128), np.float32)
    for h in range(32):
        s32[h, h * 128:(h + 1) * 128] = 1.0
    m["sel32"] = s32
    return m


def _sharedD(ins):
    m = {}
    m["WrT"] = np.ascontiguousarray(ins["L1_ffn_W_r"].T)
    m["WkT"] = np.ascontiguousarray(ins["L1_ffn_W_k"].T)
    m["WvT"] = np.ascontiguousarray(ins["L1_ffn_W_v"].T)
    m["mur"] = np.ascontiguousarray(ins["L1_ffn_mu_r"].reshape(8, 128).T)
    m["muk"] = np.ascontiguousarray(ins["L1_ffn_mu_k"].reshape(8, 128).T)
    return m


def _gather(res, cores):
    out = np.empty((4, 4096, 1024), dtype=np.float32)
    for i, c in enumerate(cores):
        b, half = c // 2, c % 2
        out[b, half * 2048:(half + 1) * 2048] = np.asarray(res.results[i]["yT"]).T
    return out


_NC_CACHE = {}


def _get_nc(name, builder):
    if name not in _NC_CACHE:
        _NC_CACHE[name] = builder()
    return _NC_CACHE[name]


def _fused_maps(ins, x, cores):
    consts = host_constants()
    sB = _sharedB(ins)
    sC = _sharedC(ins)
    sD = _sharedD(ins)
    maps = []
    for c in cores:
        b, half = c // 2, c % 2
        m = dict(consts)
        m.update(sB); m.update(sC); m.update(sD)
        m["xT_full"] = _roll_xT(x[b], half)
        m["wq"] = np.ascontiguousarray(ins["L0_attn_Wq"])
        m["wk"] = np.ascontiguousarray(ins["L0_attn_Wk"])
        m["wo"] = np.ascontiguousarray(ins["L0_attn_Wo"])
        m["bo"] = np.ascontiguousarray(ins["L0_attn_bo"].reshape(8, 128).T)
        m["WrT"] = sD["WrT"]
        m["prev_col"] = np.full((64, 1), float(half), np.float32)
        m["prev128"] = np.full((128, 1), float(half), np.float32)
        maps.append(m)
    return maps


def kernel(**inputs):
    ins = inputs
    cores = list(range(8))
    x = np.ascontiguousarray(ins["L0_attn_x"], dtype=np.float32)
    nc = _get_nc("F", build_fused)
    res = run_bass_kernel_spmd(nc, _fused_maps(ins, x, cores), core_ids=cores)
    return _gather(res, cores)


def kernel_multilaunch(**inputs):

    ins = inputs
    consts = host_constants()
    cores = list(range(8))
    x = np.ascontiguousarray(ins["L0_attn_x"], dtype=np.float32)

    ncA = _get_nc("A", build_A)
    res = run_bass_kernel_spmd(ncA, [_hostA(consts, ins, x, c) for c in cores],
                               core_ids=cores)
    x = _gather(res, cores)

    ncB = _get_nc("B", build_B)
    sB = _sharedB(ins)
    maps = []
    for c in cores:
        b, half = c // 2, c % 2
        m = dict(consts); m.update(sB)
        m["xT"] = np.ascontiguousarray(x[b, half * 2048:(half + 1) * 2048].T)
        maps.append(m)
    res = run_bass_kernel_spmd(ncB, maps, core_ids=cores)
    x = _gather(res, cores)

    ncC = _get_nc("C", build_C)
    sC = _sharedC(ins)
    maps = []
    for c in cores:
        b, half = c // 2, c % 2
        own = x[b, half * 2048:(half + 1) * 2048]
        if half == 0:
            full = np.concatenate([np.zeros((3, 1024), np.float32), own], 0)
        else:
            full = np.concatenate([x[b, 2048 - 3:2048], own], 0)
        m = dict(consts); m.update(sC)
        m["xTc"] = np.ascontiguousarray(full.T)
        m["prev_col"] = np.full((64, 1), float(half), np.float32)
        maps.append(m)
    res = run_bass_kernel_spmd(ncC, maps, core_ids=cores)
    x = _gather(res, cores)

    ncD = _get_nc("D", build_D)
    sD = _sharedD(ins)
    maps = []
    for c in cores:
        b, half = c // 2, c % 2
        own = x[b, half * 2048:(half + 1) * 2048]
        if half == 0:
            full = np.concatenate([np.zeros((1, 1024), np.float32), own], 0)
        else:
            full = np.concatenate([x[b, 2048 - 1:2048], own], 0)
        m = dict(consts); m.update(sD)
        m["xTd"] = np.ascontiguousarray(full.T)
        maps.append(m)
    res = run_bass_kernel_spmd(ncD, maps, core_ids=cores)
    return _gather(res, cores)


# version 13
# speedup vs baseline: 1.1842x; 1.1842x over previous
import numpy as np
import ml_dtypes
from contextlib import ExitStack

import concourse.bass as bass
import concourse.mybir as mybir
from concourse.bass_utils import run_bass_kernel_spmd

F32 = mybir.dt.float32
BF16 = mybir.dt.bfloat16
AF = mybir.ActivationFunctionType
ALU = mybir.AluOpType
AX = mybir.AxisListType.X
BT = ml_dtypes.bfloat16

D = 1024
NT = 8
T = 2048
TF = 4096
CH = 512
H = 16
LN_EPS = 1e-5

CONST_SPEC = [
    ("ident_bf", [128, 128], BF16),
    ("ident2_bf", [128, 64], BF16),
    ("ident_f32", [128, 128], F32),
    ("ones_col_bf", [128, 1], BF16),
    ("ones_row_bf", [1, 128], BF16),
    ("ones_row_f32", [1, 128], F32),
    ("sel_lo", [1, 128], BF16),
    ("sel_hi", [1, 128], BF16),
]


B_CONST_SPEC = [
    ("sel9", [9, 1024], BF16),
]


def host_constants():
    c = {}
    s9 = np.zeros((9, 1024), dtype=BT)
    for e in range(8):
        s9[e + 1, e * 128:(e + 1) * 128] = 1
    c["sel9"] = s9
    c["ident_bf"] = np.eye(128, dtype=BT)
    c["ident2_bf"] = np.tile(np.eye(64, dtype=BT), (2, 1))
    c["ident_f32"] = np.eye(128, dtype=np.float32)
    c["ones_col_bf"] = np.ones((128, 1), dtype=BT)
    c["ones_row_bf"] = np.ones((1, 128), dtype=BT)
    c["ones_row_f32"] = np.ones((1, 128), dtype=np.float32)
    c["sel_lo"] = np.concatenate([np.ones((1, 64), BT), np.zeros((1, 64), BT)], 1)
    c["sel_hi"] = np.concatenate([np.zeros((1, 64), BT), np.ones((1, 64), BT)], 1)
    return c


def declare_constants(nc, names=None):
    return {name: nc.dram_tensor(name, shape, dt, kind="ExternalInput").ap()
            for name, shape, dt in CONST_SPEC if names is None or name in names}


def tile(buf, j, width, sl=None):
    if sl is None:
        return buf[:, j * width:(j + 1) * width]
    a, b = sl
    return buf[:, j * width + a: j * width + b]



def dma_wtiles(g, dst_sb, src_dram, ntile, sem=None):
    C = src_dram.shape[1]
    ins = g.dma_start(out=dst_sb.rearrange("p (j c) -> p j c", j=ntile),
                      in_=src_dram.rearrange("(j p) c -> p j c", p=128))
    if sem is not None:
        ins.then_inc(sem, 16)
    return ins

def build_A(upto=3):
    nc = bass.Bass("TRN2", target_bir_lowering=False)
    caps = declare_constants(nc)
    x_in = nc.dram_tensor("xT_full", [D, TF], F32, kind="ExternalInput").ap()
    wq_in = nc.dram_tensor("wq", [D, D], F32, kind="ExternalInput").ap()
    wk_in = nc.dram_tensor("wk", [D, D], F32, kind="ExternalInput").ap()
    wo_in = nc.dram_tensor("wo", [D, D], F32, kind="ExternalInput").ap()
    bo_in = nc.dram_tensor("bo", [128, 8], F32, kind="ExternalInput").ap()
    y_out = nc.dram_tensor("yT", [D, T], F32, kind="ExternalOutput").ap()
    emit_A(nc, caps, x_in, wq_in, wk_in, wo_in, bo_in, y_out, upto=upto)
    return nc


A_CONSTS = ("ones_col_bf", "ones_row_bf", "sel_lo", "sel_hi")


def emit_A(nc, caps, x_in, wq_in, wk_in, wo_in, bo_in, y_out, upto=3, P=""):
    v_dram = nc.dram_tensor(P + "v_scr", [TF, D], BF16).ap()

    NCF = TF // CH
    NCO = T // CH
    NKT = TF // 128
    NHP = H // 2

    kq_groups_in_chunk = [24 if ci < NCO else 16 for ci in range(NCF)]
    kq_total = sum(kq_groups_in_chunk)

    with ExitStack() as ctx:
        def sb(name, shape, dt):
            return ctx.enter_context(nc.sbuf_tensor(P + name, shape, dt))
        def ps(name, shape, dt):
            return ctx.enter_context(nc.psum_tensor(P + name, shape, dt))
        def sem(name):
            return ctx.enter_context(nc.semaphore(P + name))

        s_lw = sem("s_lw");  s_ww = sem("s_ww");  s_ld = sem("s_ld");  s_xf = sem("s_xf")
        s_sq = sem("s_sq");  s_stats = sem("s_stats"); s_mu = sem("s_mu")
        s_rs = sem("s_rs");  s_pr = sem("s_pr");  s_bc = sem("s_bc")
        s_nm = sem("s_nm");  s_kq = sem("s_kq");  s_ev = sem("s_ev")
        s_vs = sem("s_vs");  s_vh = sem("s_vh")
        s_s = sem("s_s");    s_e = sem("s_e");    s_po = sem("s_po")
        s_rcf = sem("s_rcf"); s_rb = sem("s_rb"); s_fin = sem("s_fin")
        s_wo = sem("s_wo");  s_f3 = sem("s_f3");  s_st = sem("s_st")
        sv_self = sem("sv_self"); ss_self = sem("ss_self")

        xbf = sb("xbf", [128, NT * CH], BF16)
        xln = sb("xln", [128, NT * CH], BF16)
        sq = xln
        lnsm = sb("lnsm", [1, 4 * CH], F32)
        pair_bf = sb("pair_bf", [1, 2 * CH], BF16)
        rbuf = sb("rbuf", [128, 2 * CH], BF16)
        wqk = sb("wqk", [128, 2 * NT * 1024], BF16)
        wo_sb = sb("wo_sb", [128, NT * 1024], BF16)
        bo_sb = sb("bo_sb", [128, 8], F32)
        KT = sb("KT", [128, NT * TF], BF16)
        QT = sb("QT", [128, NT * T], BF16)
        OA = wqk
        Vc = sb("Vc", [128, 2 * 1024], BF16)
        Vhp = sb("Vhp", [128, NKT * 128], BF16)
        Pt = sb("Pt", [128, 4 * CH], BF16)
        rcbf = sb("rcbf", [1, 2 * CH], BF16)
        rcb = sb("rcb", [128, CH], BF16)
        xf32 = sb("xf32", [128, NT * CH], F32)
        eps1 = sb("eps1", [1, 1], F32)
        cb = {n: sb("c_" + n, shp, dt) for n, shp, dt in CONST_SPEC if n in A_CONSTS}

        pB = ps("pB", [128, 2 * CH], F32)
        pX = ps("pX", [128, 2 * CH], F32)
        pO = ps("pO", [128, 2 * CH], F32)
        pD2 = ps("pD2", [128, 2 * CH], F32)
        pS = pO

        block = ctx.enter_context(nc.Block())

        n_lw = len(A_CONSTS) + 1


        @block.sync
        def _(g):
            for name in A_CONSTS:
                g.dma_start(out=cb[name][:, :], in_=caps[name]).then_inc(s_lw, 16)
            g.dma_start(out=bo_sb[:, :], in_=bo_in).then_inc(s_lw, 16)
            ev_c = 0
            for ci in range(NCF):
                base_ev = ev_c
                ev_c += kq_groups_in_chunk[ci]
                for tt in range(4):
                    need = base_ev + 8 + 2 * tt + 2
                    g.wait_ge(s_ev, need)
                    vs_i = 4 * ci + tt
                    if vs_i >= 1:
                        g.wait_ge(s_vs, 16 * vs_i)
                    g.dma_start(out=v_dram[ci * CH + tt * 128: ci * CH + tt * 128 + 128, :],
                                in_=Vc[:, (tt % 2) * 1024:(tt % 2) * 1024 + 1024]
                                ).then_inc(s_vs, 16)
            if upto >= 3:
                for oc in range(NCO):
                    g.wait_ge(s_f3, NT * (oc + 1))
                    if oc >= 1:
                        g.wait_ge(s_st, 16 * oc)
                    g.dma_start(out=y_out.rearrange("(j p) c -> p j c", p=128)[:, :, oc * CH:(oc + 1) * CH],
                                in_=xf32.rearrange("p (j c) -> p j c", j=NT)
                                ).then_inc(s_st, 16)
                g.wait_ge(s_st, 16 * NCO)

        @block.gpsimd
        def _(g):
            dma_wtiles(g, wqk[:, 0:NT * 1024], wq_in, NT, s_ww)
            dma_wtiles(g, wqk[:, NT * 1024:2 * NT * 1024], wk_in, NT, s_ww)
            dma_wtiles(g, wo_sb[:, :], wo_in, NT, s_ww)
            for ci in range(NCF):
                if ci >= 1:
                    g.wait_ge(s_nm, ci)
                for j in range(NT):
                    g.dma_start(out=tile(xbf, j, CH),
                                in_=x_in[128 * j:128 * (j + 1), ci * CH:(ci + 1) * CH]
                                ).then_inc(s_ld, 16)
            if upto >= 2:
                for hp in range(NHP):
                    if hp == 0:
                        g.wait_ge(s_vs, 16 * 4 * NCF)
                    else:
                        g.wait_ge(s_po, 32 * 4 * hp)
                    g.dma_start(out=Vhp.rearrange("p (kt c) -> p kt c", c=128),
                                in_=v_dram[:, hp * 128:(hp + 1) * 128]
                                .rearrange("(kt p) c -> p kt c", p=128)
                                ).then_inc(s_vh, 16)
            if upto >= 3:
                for oc in range(NCO):
                    if oc >= 1:
                        g.wait_ge(s_st, 16 * oc)
                    for j in range(NT):
                        g.dma_start(out=tile(xf32, j, CH),
                                    in_=x_in[128 * j:128 * (j + 1), oc * CH:(oc + 1) * CH]
                                    ).then_inc(s_xf, 16)
            elif upto == 2:
                g.wait_ge(s_fin, NHP * NCO)
                g.dma_start(out=y_out.rearrange("(j p) c -> p j c", p=128),
                            in_=OA.rearrange("p (j c) -> p j c", j=NT)[:, 0:NT, 0:T]
                            ).then_inc(s_xf, 16)
                g.wait_ge(s_xf, 16)
            elif upto == 1:
                g.wait_ge(s_ev, kq_total)
                g.dma_start(out=y_out.rearrange("(j p) c -> p j c", p=128),
                            in_=QT.rearrange("p (j c) -> p j c", j=NT)).then_inc(s_xf, 16)
                g.wait_ge(s_xf, 16)

        @block.tensor
        def _(t):
            t.wait_ge(s_lw, 16 * n_lw)
            t.wait_ge(s_ww, 48)
            gq = 0
            for ci in range(NCF):
                t.wait_ge(s_ld, 16 * NT * (ci + 1))
                t.wait_ge(s_sq, ci + 1)
                if ci >= 1:
                    t.wait_ge(s_mu, ci)
                for j in range(NT):
                    t.matmul(pS[0:1, 0:CH], cb["ones_col_bf"][:, :], tile(xbf, j, CH),
                             start=(j == 0), stop=(j == NT - 1))
                for j in range(NT):
                    ins = t.matmul(pS[0:1, CH:2 * CH], cb["ones_col_bf"][:, :],
                                   tile(sq, j, CH), start=(j == 0), stop=(j == NT - 1))
                ins.then_inc(s_stats, 1)
                t.wait_ge(s_pr, ci + 1)
                if ci >= 1:
                    t.wait_ge(s_nm, ci)
                t.matmul(pB[:, 0:CH], cb["ones_row_bf"][0:1, :], pair_bf[0:1, 0:CH],
                         start=True, stop=True)
                t.matmul(pB[:, CH:2 * CH], cb["ones_row_bf"][0:1, :], pair_bf[0:1, CH:2 * CH],
                         start=True, stop=True).then_inc(s_bc, 1)
                t.wait_ge(s_nm, ci + 1)
                for gi in range(kq_groups_in_chunk[ci]):
                    gq += 1
                    if gq > 2:
                        t.wait_ge(s_ev, gq - 2)
                    slot = pX[:, ((gq - 1) % 2) * CH: ((gq - 1) % 2) * CH + CH]
                    if gi < 8:
                        ot = gi
                        for j in range(NT):
                            ins = t.matmul(slot,
                                           wqk[:, NT * 1024 + j * 1024 + ot * 128:
                                               NT * 1024 + j * 1024 + ot * 128 + 128],
                                           tile(xln, j, CH),
                                           start=(j == 0), stop=(j == NT - 1))
                    elif gi < 16:
                        tt, half = (gi - 8) // 2, (gi - 8) % 2
                        for j in range(NT):
                            ins = t.matmul(slot,
                                           xln[:, j * CH + tt * 128: j * CH + tt * 128 + 128],
                                           wqk[:, NT * 1024 + j * 1024 + half * 512:
                                               NT * 1024 + j * 1024 + half * 512 + 512],
                                           start=(j == 0), stop=(j == NT - 1))
                    else:
                        ot = gi - 16
                        for j in range(NT):
                            ins = t.matmul(slot,
                                           wqk[:, j * 1024 + ot * 128: j * 1024 + ot * 128 + 128],
                                           tile(xln, j, CH),
                                           start=(j == 0), stop=(j == NT - 1))
                    ins.then_inc(s_kq, 1)
            assert gq == kq_total

            if upto < 2:
                return
            t.wait_ge(s_ev, kq_total)
            t.wait_ge(s_mu, NCF)
            NIT = NHP * NCO * NKT

            def s_mm(it2):
                hp2, r2 = divmod(it2 - 1, NCO * NKT)
                qc2, kt2 = divmod(r2, NKT)
                if it2 > 2:
                    t.wait_ge(s_e, it2 - 2)
                slot2 = pB if (it2 % 2 == 1) else pX
                kcol = hp2 * TF + kt2 * 128
                qcol = hp2 * T + qc2 * CH
                t.matmul(slot2[:, 0:CH], KT[0:64, kcol:kcol + 128],
                         QT[0:64, qcol:qcol + CH], start=True, stop=True,
                         tile_position=(0, 0))
                t.matmul(slot2[:, CH:2 * CH], KT[64:128, kcol:kcol + 128],
                         QT[64:128, qcol:qcol + CH], start=True, stop=True,
                         tile_position=(64, 0)).then_inc(s_s, 1)

            it = 0
            s_mm(1)
            for hp in range(NHP):
                t.wait_ge(s_vh, 16 * (hp + 1))
                for qc in range(NCO):
                    grp = 4 * hp + qc
                    t.wait_ge(s_fin, grp)
                    for kt in range(NKT):
                        it += 1
                        if it < NIT:
                            s_mm(it + 1)
                        pts = Pt[:, (it % 2) * 1024:(it % 2) * 1024 + 1024]
                        t.wait_ge(s_e, it)
                        t.matmul(pD2[0:1, 0:CH], cb["ones_col_bf"][:, :], pts[:, 0:CH],
                                 start=(kt == 0), stop=(kt == NKT - 1))
                        t.matmul(pD2[0:1, CH:2 * CH], cb["ones_col_bf"][:, :],
                                 pts[:, CH:2 * CH], start=(kt == 0), stop=(kt == NKT - 1))
                        t.matmul(pO[0:64, 0:CH], Vhp[:, kt * 128:kt * 128 + 64],
                                 pts[:, 0:CH], start=(kt == 0), stop=(kt == NKT - 1),
                                 tile_position=(0, 0))
                        t.matmul(pO[64:128, CH:2 * CH], Vhp[:, kt * 128 + 64:kt * 128 + 128],
                                 pts[:, CH:2 * CH], start=(kt == 0), stop=(kt == NKT - 1),
                                 tile_position=(0, 64)).then_inc(s_po, 1)
                    t.wait_ge(s_rcf, grp + 1)
                    t.matmul(pD2[:, 0:CH], cb["sel_lo"][0:1, :], rcbf[0:1, 0:CH],
                             start=True, stop=True)
                    t.matmul(pD2[:, CH:2 * CH], cb["sel_hi"][0:1, :], rcbf[0:1, CH:2 * CH],
                             start=True, stop=True).then_inc(s_rb, 1)
            assert it == NHP * NCO * NKT

            if upto < 3:
                return
            t.wait_ge(s_fin, NHP * NCO)
            for oc in range(NCO):
                for ot in range(NT):
                    g3 = NT * oc + ot
                    if g3 >= 2:
                        t.wait_ge(s_f3, g3 - 1)
                    slot = pX[:, (g3 % 2) * CH:(g3 % 2) * CH + CH]
                    for j in range(NT):
                        ins = t.matmul(slot,
                                       wo_sb[:, j * 1024 + ot * 128: j * 1024 + ot * 128 + 128],
                                       tile(OA, j, T, (oc * CH, (oc + 1) * CH)),
                                       start=(j == 0), stop=(j == NT - 1))
                    ins.then_inc(s_wo, 1)

        @block.scalar
        def _(s):
            ssn = [0]
            def sbump(ins):
                ins.then_inc(ss_self, 1); ssn[0] += 1
            def sfence():
                s.wait_ge(ss_self, ssn[0])
            s.wait_ge(s_lw, 16 * n_lw)
            s.wait_ge(s_ww, 48)
            gq = 0
            for ci in range(NCF):
                s.wait_ge(s_ld, 16 * NT * (ci + 1))
                if ci >= 1:
                    s.wait_ge(s_kq, sum(kq_groups_in_chunk[:ci]))
                for j in range(NT):
                    ins = s.activation(tile(sq, j, CH), tile(xbf, j, CH), AF.Square)
                ins.then_inc(s_sq, 1)
                s.wait_ge(s_mu, ci + 1)
                if ci >= 1:
                    s.wait_ge(s_pr, ci)
                var = lnsm[0:1, 2 * CH:3 * CH]
                rstd = lnsm[0:1, 3 * CH:4 * CH]
                sbump(s.activation(rstd, var, AF.Ln, bias=eps1[0:1, 0:1]))
                sfence()
                s.activation(rstd, rstd, AF.Exp, scale=-0.5).then_inc(s_rs, 1)
                for gi in range(kq_groups_in_chunk[ci]):
                    gq += 1
                    s.wait_ge(s_kq, gq)
                    slot = pX[:, ((gq - 1) % 2) * CH:((gq - 1) % 2) * CH + CH]
                    if gi < 8:
                        dst = KT[:, gi * TF + ci * CH: gi * TF + ci * CH + CH]
                        ins = s.activation(dst, slot, AF.Copy)
                    elif gi < 16:
                        tt, half = (gi - 8) // 2, (gi - 8) % 2
                        vs_i = 4 * ci + tt
                        if vs_i >= 2 and half == 0:
                            s.wait_ge(s_vs, 16 * (vs_i - 1))
                        dst = Vc[:, (tt % 2) * 1024 + half * 512:
                                 (tt % 2) * 1024 + half * 512 + 512]
                        ins = s.activation(dst, slot, AF.Copy)
                    else:
                        ot = gi - 16
                        dst = QT[:, ot * T + ci * CH: ot * T + ci * CH + CH]
                        ins = s.activation(dst, slot, AF.Copy, scale=float(2.0 ** -5))
                    ins.then_inc(s_ev, 1)
            if upto < 2:
                return
            it = 0
            for hp in range(NHP):
                for qc in range(NCO):
                    for kt in range(NKT):
                        it += 1
                        s.wait_ge(s_s, it)
                        if it >= 3:
                            s.wait_ge(s_po, it - 2)
                        slot = pB if (it % 2 == 1) else pX
                        s.activation(Pt[:, (it % 2) * 1024:(it % 2) * 1024 + 1024],
                                     slot[:, :], AF.Exp).then_inc(s_e, 1)

        @block.vector
        def _(v):
            svn = [0]
            def vbump(ins):
                ins.then_inc(sv_self, 1); svn[0] += 1
            def vfence():
                v.wait_ge(sv_self, svn[0])
            v.wait_ge(s_lw, 16 * n_lw)
            v.wait_ge(s_ww, 48)
            v.memset(eps1[0:1, :], LN_EPS)
            for ci in range(NCF):
                v.wait_ge(s_stats, ci + 1)
                if ci >= 1:
                    v.wait_ge(s_rs, ci)
                mu = lnsm[0:1, 0:CH]
                ex2 = lnsm[0:1, CH:2 * CH]
                var = lnsm[0:1, 2 * CH:3 * CH]
                vbump(v.tensor_scalar(mu, pS[0:1, 0:CH], 1.0 / D, None, ALU.mult))
                vbump(v.tensor_scalar(ex2, pS[0:1, CH:2 * CH], 1.0 / D, None, ALU.mult))
                vfence()
                vbump(v.tensor_tensor(var, mu, mu, ALU.mult))
                vfence()
                v.tensor_tensor(var, ex2, var, ALU.subtract).then_inc(s_mu, 1)
                v.wait_ge(s_rs, ci + 1)
                rstd = lnsm[0:1, 3 * CH:4 * CH]
                nu = lnsm[0:1, CH:2 * CH]
                vbump(v.tensor_tensor(nu, mu, rstd, ALU.mult))
                vbump(v.tensor_copy(pair_bf[0:1, 0:CH], rstd))
                vfence()
                v.tensor_copy(pair_bf[0:1, CH:2 * CH], nu).then_inc(s_pr, 1)
                v.wait_ge(s_bc, ci + 1)
                if ci >= 1:
                    v.wait_ge(s_kq, sum(kq_groups_in_chunk[:ci]))
                vbump(v.tensor_copy(rbuf[:, :], pB[:, :]))
                vfence()
                for j in range(NT):
                    vbump(v.tensor_tensor(tile(xln, j, CH), tile(xbf, j, CH),
                                          rbuf[:, 0:CH], ALU.mult))
                vfence()
                for j in range(NT):
                    o = tile(xln, j, CH)
                    ins = v.tensor_tensor(o, o, rbuf[:, CH:2 * CH], ALU.subtract)
                ins.then_inc(s_nm, 1)
            if upto < 2:
                return
            for hp in range(NHP):
                for qc in range(NCO):
                    grp = 4 * hp + qc
                    v.wait_ge(s_po, 32 * (grp + 1))
                    vbump(v.reciprocal(lnsm[0:1, 0:CH], pD2[0:1, 0:CH]))
                    vbump(v.reciprocal(lnsm[0:1, CH:2 * CH], pD2[0:1, CH:2 * CH]))
                    vfence()
                    v.tensor_copy(rcbf[0:1, :], lnsm[0:1, 0:2 * CH]).then_inc(s_rcf, 1)
                    v.wait_ge(s_rb, grp + 1)
                    vbump(v.tensor_copy(rcb[0:64, :], pD2[0:64, 0:CH]))
                    vbump(v.tensor_copy(rcb[64:128, :], pD2[64:128, CH:2 * CH]))
                    vfence()
                    dst = tile(OA, hp, T, (qc * CH, (qc + 1) * CH))
                    v.tensor_tensor(dst[0:64, :], pO[0:64, 0:CH], rcb[0:64, :], ALU.mult)
                    v.tensor_tensor(dst[64:128, :], pO[64:128, CH:2 * CH], rcb[64:128, :],
                                    ALU.mult).then_inc(s_fin, 1)
            if upto < 3:
                return
            for oc in range(NCO):
                v.wait_ge(s_xf, 16 * NT * (oc + 1))
                for ot in range(NT):
                    g3 = NT * oc + ot
                    v.wait_ge(s_wo, g3 + 1)
                    slot = pX[:, (g3 % 2) * CH:(g3 % 2) * CH + CH]
                    dst = tile(xf32, ot, CH)
                    vbump(v.tensor_tensor(dst, dst, slot, ALU.add))
                    vfence()
                    v.tensor_scalar(dst, dst, bo_sb[:, ot:ot + 1], None,
                                    ALU.add).then_inc(s_f3, 1)



B_CONSTS = ("ones_col_bf", "ones_row_bf", "ident_bf")


def build_B():
    nc = bass.Bass("TRN2", target_bir_lowering=False)
    caps = declare_constants(nc, B_CONSTS)
    x_in = nc.dram_tensor("xT", [D, T], F32, kind="ExternalInput").ap()
    w1_in = nc.dram_tensor("w1T_all", [9, D, 4096], F32, kind="ExternalInput").ap()
    w2_in = nc.dram_tensor("w2T_all", [9, 8, 4096, 128], F32, kind="ExternalInput").ap()
    b1_in = nc.dram_tensor("b1_all", [9, 128, 32], F32, kind="ExternalInput").ap()
    b2_in = nc.dram_tensor("b2_all", [9, 1024], F32, kind="ExternalInput").ap()
    rw_in = nc.dram_tensor("rwT", [D, 8], F32, kind="ExternalInput").ap()
    rb_in = nc.dram_tensor("rb", [1, 8], F32, kind="ExternalInput").ap()
    sel9_in = nc.dram_tensor("sel9", [9, 1024], BF16, kind="ExternalInput").ap()
    y_out = nc.dram_tensor("yT", [D, T], F32, kind="ExternalOutput").ap()
    emit_B(nc, caps, x_in, w1_in, w2_in, b1_in, b2_in, rw_in, rb_in, sel9_in, y_out)
    return nc


def emit_B(nc, caps, x_in, w1_in, w2_in, b1_in, b2_in, rw_in, rb_in, sel9_in, y_out,
           P="B_"):
    gdram = nc.dram_tensor(P + "g_scr", [9, T], BF16).ap()

    NCH = T // CH
    NTT = T // 128
    NE = 9
    NQ8 = 8
    N_EV2 = 2 * NE * 8 * 2
    N_GTF = NTT * 8

    with ExitStack() as ctx:
        def sb(name, shape, dt):
            return ctx.enter_context(nc.sbuf_tensor(P + name, shape, dt))
        def ps(name, shape, dt):
            return ctx.enter_context(nc.psum_tensor(P + name, shape, dt))
        def sem(name):
            return ctx.enter_context(nc.semaphore(P + name))

        s_lw = sem("s_lw"); s_ww = sem("s_ww"); s_ld = sem("s_ld")
        s_sq = sem("s_sq"); s_stats = sem("s_stats"); s_mu = sem("s_mu")
        s_rs = sem("s_rs"); s_pr = sem("s_pr"); s_bc = sem("s_bc"); s_nm = sem("s_nm")
        s_rt = sem("s_rt"); s_gv = sem("s_gv"); s_gmm = sem("s_gmm"); s_gtf = sem("s_gtf")
        s_g9 = sem("s_g9")
        s_wst = sem("s_wst"); s_hm = sem("s_hm"); s_ge = sem("s_ge"); s_gt = sem("s_gt")
        s_gb = sem("s_gb"); s_gbe = sem("s_gbe")
        s_p2 = sem("s_p2"); s_ev2 = sem("s_ev2"); s_st = sem("s_st")
        s_vc = sem("s_vc"); s_rv1 = sem("s_rv1"); s_se = sem("s_se")
        s_gsd = sem("s_gsd"); s_g9m = sem("s_g9m")
        sv_self = sem("sv_self"); ss_self = sem("ss_self")

        X = sb("Xr", [128, NT * T], F32)
        xln = sb("xlnB", [128, NT * T], BF16)
        h_sb = sb("h_sb", [128, 32 * 1024], BF16)
        xbf = h_sb
        w1b = sb("w1b", [128, 2 * NT * 256], BF16)
        w2b = sb("w2b", [128, 2 * 32 * 128], BF16)
        lnsm = sb("lnsmB", [1, 2 * CH], F32)
        pair_bf = sb("pairB", [1, 2 * CH], BF16)
        rbuf = sb("rbufB", [128, 2 * CH], BF16)
        rwT_sb = sb("rwT_sb", [128, 64], BF16)
        rb_sb = sb("rb_sb", [1, 8], BF16)
        b1_sb = sb("b1_sb", [128, NE * 32], F32)
        b2_bf = sb("b2_bf", [9, 1024], BF16)
        gate9 = sb("gate9", [9, T], BF16)
        sel9 = sb("sel9_sb", [9, 1024], BF16)
        gstg = w1b
        gb_sb = rbuf
        lg = sb("lg", [128, 40], F32)
        gates_bf = sb("gates_bf", [128, 8], BF16)
        eps1 = sb("eps1B", [1, 1], F32)
        cb = {n: sb("cB_" + n, shp, dt) for n, shp, dt in CONST_SPEC if n in B_CONSTS}

        pH = ps("pH", [128, 2 * CH], F32)
        pO2 = ps("pO2", [128, 2 * CH], F32)
        pB = ps("pBB", [128, 2 * CH], F32)
        pS = pB

        block = ctx.enter_context(nc.Block())

        n_lw = len(B_CONSTS) + 2

        @block.sync
        def _(g):
            for name in B_CONSTS:
                g.dma_start(out=cb[name][:, :], in_=caps[name]).then_inc(s_lw, 16)
            g.dma_start(out=b1_sb.rearrange("p (e c) -> p e c", e=NE),
                        in_=b1_in.rearrange("e p c -> p e c")).then_inc(s_lw, 16)
            g.dma_start(out=sel9[:, :], in_=sel9_in).then_inc(s_lw, 16)
            for tt in range(NTT):
                g.wait_ge(s_gtf, 8 * (tt + 1))
                if tt >= 1:
                    g.wait_ge(s_gsd, 16 * tt)
                g.dma_start(out=gdram[1:9, tt * 128:(tt + 1) * 128],
                            in_=gstg[0:1, (tt % 2) * 1024:(tt % 2) * 1024 + 1024]
                            ).then_inc(s_gsd, 16)
            g.wait_ge(s_ev2, N_EV2)
            for oc in range(NCH):
                if oc >= 1:
                    g.wait_ge(s_st, 16 * oc)
                g.dma_start(out=y_out.rearrange("(j p) c -> p j c", p=128)[:, :, oc * CH:(oc + 1) * CH],
                            in_=X.rearrange("p (j c) -> p j c", j=NT)[:, :, oc * CH:(oc + 1) * CH]
                            ).then_inc(s_st, 16)
            g.wait_ge(s_st, 16 * NCH)

        @block.gpsimd
        def _(g):
            dma_wtiles(g, rwT_sb[:, :], rw_in, NT, s_ww)
            g.dma_start(out=b2_bf[:, :], in_=b2_in).then_inc(s_ww, 16)
            g.dma_start(out=rb_sb[:, :], in_=rb_in).then_inc(s_ww, 16)
            for j in range(NT):
                g.dma_start(out=tile(X, j, T),
                            in_=x_in[128 * j:128 * (j + 1), :]).then_inc(s_ld, 16)
            g.wait_ge(s_gsd, 16 * NTT)
            g.dma_start(out=gate9[1:9, :], in_=gdram[1:9, :]).then_inc(s_g9, 16)
            wi = 0; we = 0; wd = 0
            for half in range(2):
                for e in range(NE):
                    for q16 in range(16):
                        wi += 1; we += 1
                        if wi >= 2:
                            g.wait_ge(s_wst, 16 * (wi - 1))
                        if we >= 3:
                            g.wait_ge(s_hm, 4 * (we - 2))
                        g.dma_start(out=w1b.rearrange("p (s j c) -> p s j c", s=2, j=NT)
                                    [:, (we - 1) % 2, :, :],
                                    in_=w1_in[e].rearrange("(j p) c -> p j c", p=128)
                                    [:, :, q16 * 256:(q16 + 1) * 256]).then_inc(s_wst, 16)
                    for d in range(8):
                        wi += 1; wd += 1
                        g.wait_ge(s_wst, 16 * (wi - 1))
                        if wd >= 3:
                            g.wait_ge(s_p2, 2 * (wd - 2))
                        g.dma_start(out=w2b.rearrange("p (s f c) -> p s f c", s=2, f=32)
                                    [:, (wd - 1) % 2, :, :],
                                    in_=w2_in[e, d].rearrange("(f p) c -> p f c", p=128)
                                    ).then_inc(s_wst, 16)

        @block.tensor
        def _(t):
            t.wait_ge(s_lw, 16 * n_lw)
            t.wait_ge(s_ww, 48)
            t.wait_ge(s_ld, 16 * NT)
            for ci in range(NCH):
                t.wait_ge(s_sq, ci + 1)
                if ci >= 1:
                    t.wait_ge(s_mu, ci)
                for j in range(NT):
                    t.matmul(pS[0:1, 0:CH], cb["ones_col_bf"][:, :],
                             xbf[:, j * CH:(j + 1) * CH], start=(j == 0), stop=(j == NT - 1))
                for j in range(NT):
                    ins = t.matmul(pS[0:1, CH:2 * CH], cb["ones_col_bf"][:, :],
                                   xbf[:, NT * CH + j * CH: NT * CH + (j + 1) * CH],
                                   start=(j == 0), stop=(j == NT - 1))
                ins.then_inc(s_stats, 1)
                t.wait_ge(s_pr, ci + 1)
                if ci >= 1:
                    t.wait_ge(s_nm, ci)
                t.matmul(pB[:, 0:CH], cb["ones_row_bf"][0:1, :], pair_bf[0:1, 0:CH],
                         start=True, stop=True)
                t.matmul(pB[:, CH:2 * CH], cb["ones_row_bf"][0:1, :], pair_bf[0:1, CH:2 * CH],
                         start=True, stop=True).then_inc(s_bc, 1)
            t.wait_ge(s_nm, NCH)
            for tt in range(NTT):
                if tt >= 1:
                    t.wait_ge(s_gv, tt)
                for j in range(NT):
                    t.matmul(pB[:, 0:8], xln[:, j * T + tt * 128: j * T + tt * 128 + 128],
                             rwT_sb[:, j * 8:(j + 1) * 8], start=(j == 0), stop=False)
                t.matmul(pB[:, 0:8], cb["ones_row_bf"][0:1, :],
                         rb_sb[0:1, :], start=False, stop=True).then_inc(s_rt, 1)
                t.wait_ge(s_gv, tt + 1)
                t.wait_ge(s_gtf, 8 * tt)
                for e in range(8):
                    ins = t.matmul(pO2[0:1, e * 128:(e + 1) * 128],
                                   gates_bf[:, e:e + 1], cb["ident_bf"][:, :],
                                   start=True, stop=True)
                ins.then_inc(s_gmm, 1)
            t.wait_ge(s_g9, 16)
            t.wait_ge(s_g9m, 1)
            hk = 0; we = 0; wd = 0; p2g = 0; gbn = 0
            for half in range(2):
                for e in range(NE):
                    eg = half * NE + e
                    if e > 0:
                        for sub in range(2):
                            gbn += 1
                            if gbn > 2:
                                t.wait_ge(s_gbe, gbn - 2)
                            t.matmul(pB[:, sub * CH:(sub + 1) * CH],
                                     sel9[:, (e - 1) * 128:e * 128],
                                     gate9[:, half * 1024 + sub * CH:
                                           half * 1024 + (sub + 1) * CH],
                                     start=True, stop=True).then_inc(s_gb, 1)
                    for q16 in range(16):
                        we += 1
                        t.wait_ge(s_wst, 16 * (we + wd))
                        for fft in range(2):
                            for sub in range(2):
                                hk += 1
                                if hk > 2:
                                    t.wait_ge(s_ge, hk - 2)
                                slot = pH[:, ((hk - 1) % 2) * CH:((hk - 1) % 2) * CH + CH]
                                for j in range(NT):
                                    ins = t.matmul(
                                        slot,
                                        w1b[:, ((we - 1) % 2) * NT * 256 + j * 256 + fft * 128:
                                            ((we - 1) % 2) * NT * 256 + j * 256 + fft * 128 + 128],
                                        xln[:, j * T + half * 1024 + sub * CH:
                                            j * T + half * 1024 + (sub + 1) * CH],
                                        start=(j == 0), stop=(j == NT - 1))
                                ins.then_inc(s_hm, 1)
                    for d in range(8):
                        wd += 1
                        t.wait_ge(s_wst, 16 * (we + wd))
                        for sub in range(2):
                            p2g += 1
                            if p2g > 2:
                                t.wait_ge(s_ev2, p2g - 2)
                            if e > 0:
                                t.wait_ge(s_gt, 64 * (half * 8 + e - 1) + 64)
                            else:
                                t.wait_ge(s_ge, 64 * eg + 64)
                            slot = pO2[:, ((p2g - 1) % 2) * CH:((p2g - 1) % 2) * CH + CH]
                            for fft in range(32):
                                last = (fft == 31) and (e != 0)
                                ins = t.matmul(
                                    slot,
                                    w2b[:, ((wd - 1) % 2) * 32 * 128 + fft * 128:
                                        ((wd - 1) % 2) * 32 * 128 + fft * 128 + 128],
                                    h_sb[:, fft * 1024 + sub * CH: fft * 1024 + (sub + 1) * CH],
                                    start=(fft == 0), stop=last)
                            if e == 0:
                                ins = t.matmul(slot, b2_bf[:, d * 128:(d + 1) * 128],
                                               gate9[:, half * 1024 + sub * CH:
                                                     half * 1024 + (sub + 1) * CH],
                                               start=False, stop=True)
                            ins.then_inc(s_p2, 1)

        @block.scalar
        def _(s):
            ssn = [0]
            def sbump(ins):
                ins.then_inc(ss_self, 1); ssn[0] += 1
            def sfence():
                s.wait_ge(ss_self, ssn[0])
            s.wait_ge(s_lw, 16 * n_lw)
            s.wait_ge(s_ld, 16 * NT)
            for ci in range(NCH):
                s.wait_ge(s_vc, ci + 1)
                for j in range(NT):
                    ins = s.activation(xbf[:, NT * CH + j * CH: NT * CH + (j + 1) * CH],
                                       xbf[:, j * CH:(j + 1) * CH], AF.Square)
                ins.then_inc(s_sq, 1)
                s.wait_ge(s_mu, ci + 1)
                if ci >= 1:
                    s.wait_ge(s_pr, ci)
                var = lnsm[0:1, CH:2 * CH]
                rstd = pH[0:1, 0:CH]
                sbump(s.activation(rstd, var, AF.Ln, bias=eps1[0:1, 0:1]))
                sfence()
                s.activation(rstd, rstd, AF.Exp, scale=-0.5).then_inc(s_rs, 1)
            for tt in range(NTT):
                s.wait_ge(s_rv1, tt + 1)
                s.activation(lg[:, 8:16], lg[:, 0:8], AF.Exp).then_inc(s_se, 1)
            hk = 0
            for half in range(2):
                for e in range(NE):
                    eg = half * NE + e
                    if eg >= 1:
                        s.wait_ge(s_p2, 16 * eg)
                    for q16 in range(16):
                        for fft in range(2):
                            for sub in range(2):
                                hk += 1
                                s.wait_ge(s_hm, hk)
                                fg = q16 * 2 + fft
                                slot = pH[:, ((hk - 1) % 2) * CH:((hk - 1) % 2) * CH + CH]
                                s.activation(h_sb[:, fg * 1024 + sub * CH:
                                                  fg * 1024 + (sub + 1) * CH],
                                             slot, AF.Gelu_apprx_tanh,
                                             bias=b1_sb[:, e * 32 + fg: e * 32 + fg + 1]
                                             ).then_inc(s_ge, 1)

        @block.vector
        def _(v):
            svn = [0]
            def vbump(ins):
                ins.then_inc(sv_self, 1); svn[0] += 1
            def vfence():
                v.wait_ge(sv_self, svn[0])
            v.wait_ge(s_lw, 16 * n_lw)
            v.wait_ge(s_ld, 16 * NT)
            v.memset(eps1[0:1, :], LN_EPS)
            for ci in range(NCH):
                if ci >= 1:
                    v.wait_ge(s_sq, ci)
                    v.wait_ge(s_stats, ci)
                for j in range(NT):
                    ins = v.tensor_copy(xbf[:, j * CH:(j + 1) * CH],
                                        X[:, j * T + ci * CH: j * T + (ci + 1) * CH])
                ins.then_inc(s_vc, 1)
                v.wait_ge(s_stats, ci + 1)
                if ci >= 1:
                    v.wait_ge(s_rs, ci)
                mu = lnsm[0:1, 0:CH]
                ex2 = lnsm[0:1, CH:2 * CH]
                mumu = pH[0:1, 0:CH]
                vbump(v.tensor_scalar(mu, pS[0:1, 0:CH], 1.0 / D, None, ALU.mult))
                vbump(v.tensor_scalar(ex2, pS[0:1, CH:2 * CH], 1.0 / D, None, ALU.mult))
                vfence()
                vbump(v.tensor_tensor(mumu, mu, mu, ALU.mult))
                vfence()
                v.tensor_tensor(ex2, ex2, mumu, ALU.subtract).then_inc(s_mu, 1)
                v.wait_ge(s_rs, ci + 1)
                rstd = pH[0:1, 0:CH]
                nu = lnsm[0:1, CH:2 * CH]
                vbump(v.tensor_copy(pair_bf[0:1, 0:CH], rstd))
                vfence()
                vbump(v.tensor_tensor(nu, mu, rstd, ALU.mult))
                vfence()
                v.tensor_copy(pair_bf[0:1, CH:2 * CH], nu).then_inc(s_pr, 1)
                v.wait_ge(s_bc, ci + 1)
                vbump(v.tensor_copy(rbuf[:, :], pB[:, :]))
                vfence()
                for j in range(NT):
                    vbump(v.tensor_tensor(xln[:, j * T + ci * CH: j * T + (ci + 1) * CH],
                                          xbf[:, j * CH:(j + 1) * CH],
                                          rbuf[:, 0:CH], ALU.mult))
                vfence()
                for j in range(NT):
                    o = xln[:, j * T + ci * CH: j * T + (ci + 1) * CH]
                    ins = v.tensor_tensor(o, o, rbuf[:, CH:2 * CH], ALU.subtract)
                ins.then_inc(s_nm, 1)
            for tt in range(NTT):
                v.wait_ge(s_rt, tt + 1)
                vbump(v.tensor_copy(lg[:, 0:8], pB[:, 0:8]))
                vfence()
                v.tensor_copy(lg[:, 32:33], lg[:, 0:1]).then_inc(s_rv1, 1)
                v.wait_ge(s_se, tt + 1)
                pe = lg[:, 8:16]
                s1 = lg[:, 16:17]; rc1 = lg[:, 17:18]
                p = lg[:, 18:26]
                m1 = lg[:, 26:27]; cnt = lg[:, 27:28]; m2r = lg[:, 28:29]
                c2 = lg[:, 29:30]; m2 = lg[:, 30:31]; den = lg[:, 31:32]
                eq = lg[:, 32:40]
                vbump(v.reduce_sum(s1, pe, axis=AX))
                vfence()
                vbump(v.reciprocal(rc1, s1))
                vfence()
                vbump(v.tensor_scalar(p, pe, rc1, None, ALU.mult))
                vfence()
                vbump(v.reduce_max(m1, p, axis=AX))
                vfence()
                vbump(v.tensor_scalar(eq, p, m1, None, ALU.is_ge))
                vfence()
                vbump(v.reduce_sum(cnt, eq, axis=AX))
                vbump(v.tensor_scalar(eq, eq, -1e30, None, ALU.mult))
                vfence()
                vbump(v.tensor_tensor(eq, p, eq, ALU.add))
                vfence()
                vbump(v.reduce_max(m2r, eq, axis=AX))
                vfence()
                vbump(v.tensor_scalar(c2, cnt, 2.0, None, ALU.is_ge))
                vfence()
                vbump(v.tensor_tensor(m2, c2, m1, ALU.mult))
                vbump(v.tensor_scalar(c2, c2, -1.0, 1.0, ALU.mult, ALU.add))
                vfence()
                vbump(v.tensor_tensor(c2, c2, m2r, ALU.mult))
                vfence()
                vbump(v.tensor_tensor(m2, m2, c2, ALU.add))
                vfence()
                vbump(v.tensor_tensor(den, m1, m2, ALU.add))
                vbump(v.tensor_scalar(eq, p, m2, None, ALU.is_ge))
                vfence()
                vbump(v.reciprocal(den, den))
                vbump(v.tensor_tensor(eq, p, eq, ALU.mult))
                vfence()
                v.tensor_scalar(gates_bf[:, :], eq, den, None, ALU.mult).then_inc(s_gv, 1)
                v.wait_ge(s_gmm, tt + 1)
                if tt >= 2:
                    v.wait_ge(s_gsd, 16 * (tt - 1))
                for e in range(8):
                    ins = v.tensor_copy(gstg[0:1, (tt % 2) * 1024 + e * 128:
                                            (tt % 2) * 1024 + (e + 1) * 128],
                                        pO2[0:1, e * 128:(e + 1) * 128])
                    ins.then_inc(s_gtf, 1)
            v.wait_ge(s_g9, 16)
            v.memset(gate9[0:1, :], 1.0).then_inc(s_g9m, 1)
            hk = 0; p2g = 0; gbn = 0; gtn = 0
            for half in range(2):
                for e in range(NE):
                    eg = half * NE + e
                    if e > 0:
                        for sub in range(2):
                            gbn += 1
                            v.wait_ge(s_gb, gbn)
                            v.tensor_copy(gb_sb[:, sub * CH:(sub + 1) * CH],
                                          pB[:, sub * CH:(sub + 1) * CH]).then_inc(s_gbe, 1)
                        v.wait_ge(s_gbe, gbn)
                    for q16 in range(16):
                        for fft in range(2):
                            for sub in range(2):
                                hk += 1
                                if e > 0:
                                    v.wait_ge(s_ge, hk)
                                    fg = q16 * 2 + fft
                                    hsl = h_sb[:, fg * 1024 + sub * CH: fg * 1024 + (sub + 1) * CH]
                                    gtn += 1
                                    v.tensor_tensor(hsl, hsl, gb_sb[:, sub * CH:(sub + 1) * CH],
                                                    ALU.mult).then_inc(s_gt, 1)
                    for d in range(8):
                        for sub in range(2):
                            p2g += 1
                            v.wait_ge(s_p2, p2g)
                            slot = pO2[:, ((p2g - 1) % 2) * CH:((p2g - 1) % 2) * CH + CH]
                            dst = X[:, d * T + half * 1024 + sub * CH:
                                    d * T + half * 1024 + (sub + 1) * CH]
                            v.tensor_tensor(dst, dst, slot, ALU.add).then_inc(s_ev2, 1)



DBG_C = 0
C_CONSTS = ("ones_col_bf", "ones_row_bf", "ident_bf", "ident2_bf", "ones_row_f32",
            "ident_f32")

PROJ_COLS = 4384
NCVT = 18
NH = 32
NSC = 16


def build_C(n_cores=8, upto=9):
    nc = bass.Bass("TRN2", target_bir_lowering=False)
    caps = declare_constants(nc, C_CONSTS)
    x_in = nc.dram_tensor("xTc", [D, 2051], F32, kind="ExternalInput").ap()
    win_in = nc.dram_tensor("W_in_p", [D, PROJ_COLS], F32, kind="ExternalInput").ap()
    wout_in = nc.dram_tensor("W_out", [2048, D], F32, kind="ExternalInput").ap()
    convw_in = nc.dram_tensor("convw", [128, NCVT * 4], F32, kind="ExternalInput").ap()
    convb_in = nc.dram_tensor("convb", [128, NCVT], F32, kind="ExternalInput").ap()
    alog_r = nc.dram_tensor("alog_r", [1, 32], F32, kind="ExternalInput").ap()
    dtb_in = nc.dram_tensor("dtb", [32, 1], F32, kind="ExternalInput").ap()
    dn_in = nc.dram_tensor("Dn", [128, 32], F32, kind="ExternalInput").ap()
    nw_in = nc.dram_tensor("nw_row", [1, 2048], F32, kind="ExternalInput").ap()
    prev_in = nc.dram_tensor("prev_col", [64, 1], F32, kind="ExternalInput").ap()
    ut_in = nc.dram_tensor("UTones", [128, 128], F32, kind="ExternalInput").ap()
    id32_in = nc.dram_tensor("ident32", [32, 32], F32, kind="ExternalInput").ap()
    mst_in = nc.dram_tensor("maskST", [128, 128], F32, kind="ExternalInput").ap()
    sel32_in = nc.dram_tensor("sel32", [32, 32 * 128], F32, kind="ExternalInput").ap()
    y_out = nc.dram_tensor("yT", [D, T], F32, kind="ExternalOutput").ap()
    emit_C(nc, caps, x_in, win_in, wout_in, convw_in, convb_in, alog_r, dtb_in,
           dn_in, nw_in, prev_in, ut_in, id32_in, mst_in, sel32_in, y_out,
           n_cores=n_cores, upto=upto)
    return nc


def emit_C(nc, caps, x_in, win_in, wout_in, convw_in, convb_in, alog_r, dtb_in,
           dn_in, nw_in, prev_in, ut_in, id32_in, mst_in, sel32_in, y_out,
           n_cores=8, upto=9, P="C_"):
    xbc_d = nc.dram_tensor(P + "xbc_scr", [NCVT * 128, 2051], BF16).ap()
    z_d = nc.dram_tensor(P + "z_scr", [T, 2048], BF16).ap()
    yn_d = nc.dram_tensor(P + "yn_scr", [T, 2048], BF16).ap()
    stin_d = nc.dram_tensor(P + "stin_scr", [64, 2048], F32)
    stout_d = nc.dram_tensor(P + "stout_scr", [128, 2048], F32)

    LNCH = [(0, 3), (3, 512), (515, 512), (1027, 512), (1539, 512)]
    P2_BLOCKS = [(2048 + 128 * i, 128) for i in range(NCVT)] + [(4352, 32)]
    rg = [[2 * i, 2 * i + 1] for i in range(max(1, n_cores // 2))]

    with ExitStack() as ctx:
        def sb(name, shape, dt):
            return ctx.enter_context(nc.sbuf_tensor(P + name, shape, dt))
        def sem(name):
            return ctx.enter_context(nc.semaphore(P + name))

        s_T = sem("s_T"); s_V = sem("s_V"); s_S = sem("s_S")
        s_G = sem("s_G"); s_Y = sem("s_Y"); s_lw = sem("s_lw"); s_AG = sem("s_AG")
        s_hT = sem("s_hT"); s_hV = sem("s_hV"); s_hS = sem("s_hS")

        xln = sb("xlnC", [128, NT * 2051], BF16)
        dtT = sb("dtTC", [32, T], F32)
        winb = sb("winbC", [128, 2 * NT * 512], BF16)
        woutb = sb("woutbC", [128, 8 * 16 * 128], BF16)
        xbcs = sb("xbcsC", [128, 2 * 2051], BF16)
        xbcc = sb("xbccC", [128, NCVT * 512], BF16)
        xbcr = sb("xbcrC", [128, NCVT * 520], BF16)
        cvA = sb("cvAC", [128, 512], F32)
        cvB = sb("cvBC", [128, 512], F32)
        ctl = sb("ctileC", [128, 128], BF16)
        xf32 = sb("xf32C", [128, NT * CH], F32)
        Prun = sb("PrunC", [64, 2048], F32)
        Prun_bf = sb("PrunbfC", [64, 2048], BF16)

        dadj = sb("dadjC", [128, NSC * 32], F32)
        convw = sb("convwC", [128, NCVT * 4], F32)
        convb = sb("convbC", [128, NCVT], F32)
        lnsm = sb("lnsmC", [1, 2 * CH], F32)
        pair_bf = sb("pairC", [1, 2 * CH], BF16)
        rbuf = sb("rbufC", [128, 2 * CH], BF16)
        UT = sb("UTC", [128, 128], F32)
        id32 = sb("id32C", [32, 32], F32)
        mst = sb("mstC", [128, 128], F32)
        sel32 = sb("sel32C", [32, 32 * 128], F32)
        alogr = sb("alogrC", [1, 32], F32)
        dtb = sb("dtbC", [32, 1], F32)
        dn = sb("dnC", [128, 32], F32)
        nwb = sb("nwbC", [128, 2048], BF16)
        prevc = sb("prevcC", [64, 1], F32)
        Abc = sb("AbcC", [128, 32], F32)
        Arow = sb("ArowC", [1, 32], F32)
        acolb = sb("acolbC", [32, 1], F32)

        dtn = sb("dtnC", [128, 32], F32)
        ldt = sb("ldtC", [128, 32], F32)
        csb = sb("csbC", [128, 32], F32)
        acn = sb("acnC", [128, 32], F32)
        asl = sb("aslC", [128, 32], F32)
        acT = sb("acTC", [32, 128], F32)
        wdtn = sb("wdtnC", [128, 32], F32)
        wsc = sb("wscC", [128, 32], F32)
        fcols = sb("fcolsC", [64, 32], F32)
        bacr = sb("bacrC", [1, 32], F32)
        rowsc = sb("rowscC", [1, 64], F32)
        cendc = sb("cendC", [32, 1], F32)
        Gm = sb("GmC", [128, 128], BF16)
        pre2 = sb("pre2C", [128, 2 * 128], F32)
        E2 = sb("E2C", [128, 2 * 128], BF16)
        MT = sb("MTC", [128, 2 * 128], BF16)
        Xn = sb("XnC", [128, 2 * 64], BF16)
        Dx = sb("DxC", [128, 2 * 64], BF16)
        Bn = sb("BnC", [128, 64], BF16)
        Bw = sb("BwC", [128, 2 * 64], BF16)
        Pts = sb("PtsC", [64, 2 * 64], F32)
        ubuf = sb("ubufC", [128, 2 * 64], F32)
        tmpb = sb("tmpbC", [128, 2 * 64], F32)
        nasl = sb("naslC", [128, 32], F32)
        dexp = sb("dexpC", [128, 32], F32)
        rstd1 = sb("rstd1C", [128, 1], F32)
        eps1 = sb("eps1C", [1, 1], F32)
        eps128 = sb("eps128C", [128, 1], F32)
        one32 = sb("one32C", [32, 1], F32)
        cb = {n: sb("cC_" + n, shp, dt) for n, shp, dt in CONST_SPEC if n in C_CONSTS}

        nwrow = xf32[0:1, 0:2048].bitcast(F32) if False else xf32[0:1, 0:2048]
        nwrb = xbcs[0:1, 0:2048]
        dtsp = xf32[0:32, 2048:3072]
        Sin = xbcr[0:64, 2048:4096]
        xbf = xbcc[:, 0:4096]
        sqb = xbcc[:, 4096:8192]
        ynat = xbcs[:, 0:4096]
        zrl = xbcc[:, 0:4096]
        yrl = xbcc[:, 4096:8192]
        zsil = winb[:, 0:2048]
        ynf = winb[:, 2048:4096]
        ynT = xln[:, 0:8192]

        p = [ctx.enter_context(nc.psum_tensor(P + "pp%d" % i, [128, CH], F32))
             for i in range(8)]

        block = ctx.enter_context(nc.Block())

        cnt = {"T": 0, "V": 0, "S": 0, "G": 0, "Y": 0}
        SEMS = {"T": s_T, "V": s_V, "S": s_S, "G": s_G, "Y": s_Y}

        def _sec(k, body):
            if k == "T":
                @block.tensor
                def _(e): body(e)
            elif k == "V":
                @block.vector
                def _(e): body(e)
            elif k == "S":
                @block.scalar
                def _(e): body(e)
            elif k == "G":
                @block.gpsimd
                def _(e): body(e)
            else:
                @block.sync
                def _(e): body(e)

        def _step(k, f, ndma=1):
            snap = {k2: cnt[k2] for k2 in cnt}
            def body(e):
                for k2, v in snap.items():
                    if v == 0:
                        continue
                    e.wait_ge(SEMS[k2], v * (16 if k2 in ("G", "Y") else 1))
                f(e)
            _sec(k, body)
            cnt[k] += ndma if k in ("G", "Y") else 1

        def stepT(f): _step("T", lambda e: f(e).then_inc(s_T, 1))
        def stepV(f): _step("V", lambda e: f(e).then_inc(s_V, 1))
        def stepS(f): _step("S", lambda e: f(e).then_inc(s_S, 1))
        def stepG(f): _step("G", lambda e: f(e).then_inc(s_G, 16))
        def stepY(f): _step("Y", lambda e: f(e).then_inc(s_Y, 16))

        def stepG_multi(fs):
            def body(e):
                for f in fs:
                    f(e).then_inc(s_G, 16)
            _step("G", body, ndma=len(fs))

        s_vself = sem("s_vself"); s_sself = sem("s_sself")
        _mf = {"V": [0, s_vself], "S": [0, s_sself]}

        def mk_chain(k):
            st = _mf[k]
            def bump(ins):
                ins.then_inc(st[1], 1)
                st[0] += 1
            def fence(e):
                e.wait_ge(st[1], st[0])
            return bump, fence
        vbump, vfence = mk_chain("V")
        sbump, sfence = mk_chain("S")

        NLW = len(C_CONSTS) + 11

        @block.sync
        def _(g):
            def L(dst, src):
                g.dma_start(out=dst, in_=src).then_inc(s_lw, 16)
            for name in C_CONSTS:
                L(cb[name][:, :], caps[name])
            L(convw[:, :], convw_in); L(convb[:, :], convb_in)
            L(alogr[0:1, 0:32], alog_r); L(dtb[:, :], dtb_in); L(dn[:, :], dn_in)
            L(prevc[:, :], prev_in); L(UT[:, :], ut_in); L(id32[:, :], id32_in)
            L(mst[:, :], mst_in); L(sel32[:, :], sel32_in); L(nwrow[0:1, :], nw_in)

        for k in ("T", "V", "S", "G", "Y"):
            _sec(k, lambda e: e.wait_ge(s_lw, 16 * NLW))

        stepV(lambda v: v.memset(eps1[0:1, :], LN_EPS))
        stepV(lambda v: v.memset(one32[:, :], 1.0))
        stepV(lambda v: v.memset(eps128[:, :], LN_EPS))
        stepV(lambda v: v.memset(Prun[:, :], 0.0))
        stepV(lambda v: v.memset(bacr[0:1, :], 0.0))
        stepS(lambda s: s.activation(Arow[0:1, :], alogr[0:1, 0:32], AF.Exp))
        stepV(lambda v: v.tensor_scalar(Arow[0:1, :], Arow[0:1, :], -1.0, None, ALU.mult))
        stepT(lambda t: t.matmul(p[0][:, 0:32], cb["ones_row_f32"][0:1, :],
                                 Arow[0:1, :], start=True, stop=True))
        stepV(lambda v: v.tensor_copy(Abc[:, :], p[0][:, 0:32]))
        stepT(lambda t: t.matmul(p[0][0:32, 0:1], Arow[0:1, :],
                                 cb["ones_row_f32"][0:1, 0:1], start=True, stop=True))
        stepV(lambda v: v.tensor_copy(acolb[:, :], p[0][0:32, 0:1]))
        stepV(lambda v: v.tensor_copy(nwrb[0:1, :], nwrow[0:1, :]))
        for i in range(4):
            stepT(lambda t, i=i: t.matmul(p[1][:, :], cb["ones_row_bf"][0:1, :],
                                          nwrb[0:1, i * 512:(i + 1) * 512],
                                          start=True, stop=True))
            stepV(lambda v, i=i: v.tensor_copy(nwb[:, i * 512:(i + 1) * 512], p[1][:, :]))

        for ci, (t0, tw) in enumerate(LNCH):
            stepG_multi([
                (lambda g, j=j, t0=t0, tw=tw: g.dma_start(
                    out=xbf[:, j * 512:j * 512 + tw],
                    in_=x_in[128 * j:128 * (j + 1), t0:t0 + tw]))
                for j in range(NT)])
            def sqs(s, tw=tw):
                for j in range(NT):
                    ins = s.activation(sqb[:, j * 512:j * 512 + tw],
                                       xbf[:, j * 512:j * 512 + tw], AF.Square)
                return ins
            stepS(sqs)
            def stats(t, tw=tw):
                for j in range(NT):
                    t.matmul(p[2][0:1, 0:tw], cb["ones_col_bf"][:, :],
                             xbf[:, j * 512:j * 512 + tw], start=(j == 0), stop=(j == NT - 1))
                for j in range(NT):
                    ins = t.matmul(p[3][0:1, 0:tw], cb["ones_col_bf"][:, :],
                                   sqb[:, j * 512:j * 512 + tw], start=(j == 0),
                                   stop=(j == NT - 1))
                return ins
            stepT(stats)
            def lnv1(v, tw=tw):
                vbump(v.tensor_scalar(lnsm[0:1, 0:tw], p[2][0:1, 0:tw],
                                      1.0 / D, None, ALU.mult))
                vbump(v.tensor_scalar(lnsm[0:1, CH:CH + tw], p[3][0:1, 0:tw],
                                      1.0 / D, None, ALU.mult))
                vfence(v)
                vbump(v.tensor_tensor(cvA[0:1, 0:tw], lnsm[0:1, 0:tw],
                                      lnsm[0:1, 0:tw], ALU.mult))
                vfence(v)
                return v.tensor_tensor(lnsm[0:1, CH:CH + tw], lnsm[0:1, CH:CH + tw],
                                       cvA[0:1, 0:tw], ALU.subtract)
            stepV(lnv1)
            def lns1(s, tw=tw):
                sbump(s.activation(lnsm[0:1, CH:CH + tw], lnsm[0:1, CH:CH + tw],
                                   AF.Ln, bias=eps1[0:1, 0:1]))
                sfence(s)
                return s.activation(lnsm[0:1, CH:CH + tw], lnsm[0:1, CH:CH + tw],
                                    AF.Exp, scale=-0.5)
            stepS(lns1)
            def lnv2(v, tw=tw):
                vbump(v.tensor_copy(pair_bf[0:1, 0:tw], lnsm[0:1, CH:CH + tw]))
                vbump(v.tensor_tensor(lnsm[0:1, 0:tw], lnsm[0:1, 0:tw],
                                      lnsm[0:1, CH:CH + tw], ALU.mult))
                vfence(v)
                return v.tensor_copy(pair_bf[0:1, CH:CH + tw], lnsm[0:1, 0:tw])
            stepV(lnv2)
            def bcasts(t, tw=tw):
                t.matmul(p[4][:, 0:tw], cb["ones_row_bf"][0:1, :], pair_bf[0:1, 0:tw],
                         start=True, stop=True)
                return t.matmul(p[5][:, 0:tw], cb["ones_row_bf"][0:1, :],
                                pair_bf[0:1, CH:CH + tw], start=True, stop=True)
            stepT(bcasts)
            def lnv3(v, t0=t0, tw=tw):
                vbump(v.tensor_copy(rbuf[:, 0:tw], p[4][:, 0:tw]))
                vbump(v.tensor_copy(rbuf[:, CH:CH + tw], p[5][:, 0:tw]))
                vfence(v)
                for j in range(NT):
                    vbump(v.tensor_tensor(xln[:, j * 2051 + t0: j * 2051 + t0 + tw],
                                          xbf[:, j * 512:j * 512 + tw], rbuf[:, 0:tw],
                                          ALU.mult))
                vfence(v)
                for j in range(NT):
                    ins = v.tensor_tensor(xln[:, j * 2051 + t0: j * 2051 + t0 + tw],
                                          xln[:, j * 2051 + t0: j * 2051 + t0 + tw],
                                          rbuf[:, CH:CH + tw], ALU.subtract)
                    if j < NT - 1:
                        vbump(ins)
                return ins
            stepV(lnv3)

        for bi, (c0, w) in enumerate(P2_BLOCKS):
            stepG(lambda g, bi=bi, c0=c0, w=w: g.dma_start(
                out=winb.rearrange("p (s j c) -> p s j c", s=2, j=NT)[:, bi % 2, :, 0:w],
                in_=win_in.rearrange("(j p) c -> p j c", p=128)[:, :, c0:c0 + w]))
            cis = [(ci, t0, tw) for ci, (t0, tw) in enumerate(LNCH)
                   if not (bi == len(P2_BLOCKS) - 1 and ci == 0)]
            for g0 in range(0, len(cis), 2):
                grp = cis[g0:g0 + 2]
                def pjp(t, bi=bi, w=w, grp=grp):
                    for ci, t0, tw in grp:
                        for j in range(NT):
                            ins = t.matmul(p[ci % 2][0:w, 0:tw],
                                           winb[:, (bi % 2) * NT * 512 + j * 512:
                                                (bi % 2) * NT * 512 + j * 512 + w],
                                           xln[:, j * 2051 + t0: j * 2051 + t0 + tw],
                                           start=(j == 0), stop=(j == NT - 1))
                    return ins
                stepT(pjp)
                def pcp(s, bi=bi, grp=grp):
                    for ci, t0, tw in grp:
                        if bi < NCVT:
                            ins = s.activation(
                                xbcs[:, (bi % 2) * 2051 + t0:(bi % 2) * 2051 + t0 + tw],
                                p[ci % 2][:, 0:tw], AF.Copy)
                        else:
                            ins = s.activation(
                                dtT[0:32, t0 - 3:t0 - 3 + tw], p[ci % 2][0:32, 0:tw],
                                AF.Identity, bias=dtb[:, 0:1])
                    return ins
                stepS(pcp)
            if bi < NCVT:
                stepY(lambda g, bi=bi: g.dma_start(
                    out=xbc_d[bi * 128:(bi + 1) * 128, :],
                    in_=xbcs[:, (bi % 2) * 2051:(bi % 2) * 2051 + 2051]))
        for i in range(4):
            sl = dtT[0:32, i * 512:(i + 1) * 512]
            a1 = dtsp[0:32, 0:512]
            a2 = dtsp[0:32, 512:1024]
            def spchain(s, sl=sl, a1=a1, a2=a2):
                sbump(s.activation(a1, sl, AF.Abs))
                sfence(s)
                sbump(s.activation(a1, a1, AF.Exp, scale=-1.0))
                sfence(s)
                sbump(s.activation(a1, a1, AF.Ln, bias=one32[0:32, 0:1]))
                return s.activation(a2, sl, AF.Relu)
            stepS(spchain)
            stepV(lambda v, sl=sl, a1=a1, a2=a2: v.tensor_tensor(sl, a1, a2, ALU.add))

        for zc in range(4):
            wslot = (len(P2_BLOCKS) + zc) % 2
            stepG(lambda g, zc=zc, wslot=wslot: g.dma_start(
                out=winb.rearrange("p (s j c) -> p s j c", s=2, j=NT)[:, wslot, :, :],
                in_=win_in.rearrange("(j p) c -> p j c", p=128)[:, :, zc * 512:(zc + 1) * 512]))
            for qd in range(4):
                for k2 in range(2):
                    def zmm(t, qd=qd, k2=k2, wslot=wslot):
                        for k in (2 * k2, 2 * k2 + 1):
                            tt = 4 * qd + k
                            for j in range(NT):
                                ins = t.matmul(p[k % 2][:, :],
                                               xln[:, j * 2051 + 3 + tt * 128:
                                                   j * 2051 + 3 + tt * 128 + 128],
                                               winb[:, wslot * NT * 512 + j * 512:
                                                    wslot * NT * 512 + (j + 1) * 512],
                                               start=(j == 0), stop=(j == NT - 1))
                        return ins
                    stepT(zmm)
                    def zcp(s, k2=k2):
                        for k in (2 * k2, 2 * k2 + 1):
                            ins = s.activation(
                                xbcs[:, k * 512:(k + 1) * 512], p[k % 2][:, :], AF.Copy)
                        return ins
                    stepS(zcp)
                stepY(lambda g, zc=zc, qd=qd: g.dma_start(
                    out=z_d.rearrange("(q p) c -> p q c", p=128)
                    [:, 4 * qd:4 * qd + 4, zc * 512:(zc + 1) * 512],
                    in_=xbcs[:, 0:2048].rearrange("p (s c) -> p s c", s=4)))

        TCALL = {}
        _tc = [0]
        for c2 in range(NSC):
            Hb = c2 * NH
            seq = [("T1", Hb + 1), ("T1", Hb + 2)]
            for P2 in range(NH // 2):
                if P2 < NH // 2 - 1:
                    seq += [("T1", Hb + 2 * P2 + 3), ("T1", Hb + 2 * P2 + 4)]
                seq += [("T2", Hb + 2 * P2 + 1), ("T3", Hb + 2 * P2 + 1),
                        ("T2", Hb + 2 * P2 + 2), ("T3", Hb + 2 * P2 + 2)]
            for st2, Hx in seq:
                _tc[0] += 1
                TCALL[(st2, Hx)] = _tc[0]

        def tT1(H): return TCALL[("T1", H)]
        def tT2(H): return TCALL[("T2", H)]
        def tT3(H): return TCALL[("T3", H)]

        VPC = NH * 10 + 1

        def vccH(H, k):
            c2, h2 = divmod(H - 1, NH)
            return c2 * VPC + 1 + h2 * 10 + k

        def vpreH(H):
            c2, h2 = divmod(H - 1, NH)
            return c2 * VPC + (1 if h2 == 0 else 1 + (h2 - 1) * 10 + 2)

        for c in range(NSC if upto >= 2 else 0):
            cg, ck = c // 4, c % 4
            if ck == 0:
                stepG(lambda g, cg=cg: g.dma_start(
                    out=xbcr.rearrange("p (k c) -> p k c", c=520)[:, :, 0:515],
                    in_=xbc_d.rearrange("(k p) c -> p k c", p=128)
                    [:, :, cg * 512:cg * 512 + 515]))
                for k in range(NCVT):
                    base = k * 520
                    def convk(v, base=base, k=k):
                        vbump(v.tensor_scalar(
                            cvA[:, :], xbcr[:, base:base + 512],
                            convw[:, 4 * k:4 * k + 1], convb[:, k:k + 1],
                            ALU.mult, ALU.add))
                        for jj in range(1, 4):
                            vbump(v.tensor_scalar(
                                cvB[:, :], xbcr[:, base + jj:base + jj + 512],
                                convw[:, 4 * k + jj:4 * k + jj + 1], None, ALU.mult))
                            vfence(v)
                            ins = v.tensor_tensor(cvA[:, :], cvA[:, :], cvB[:, :],
                                                  ALU.add)
                            if jj < 3:
                                vbump(ins)
                                vfence(v)
                        return ins
                    stepV(convk)
                    stepS(lambda s, k=k: s.activation(xbcc[:, k * 512:(k + 1) * 512],
                                                      cvA[:, :], AF.Silu))
            t0c = c * 128
            stepT(lambda t, t0c=t0c: t.matmul(p[0][0:128, 0:32], dtT[:, t0c:t0c + 128],
                                              id32[:, :], start=True, stop=True))
            stepV(lambda v: v.tensor_copy(dtn[:, :], p[0][0:128, 0:32]))
            stepT(lambda t: t.matmul(p[1][0:128, 0:32], UT[:, :], dtn[:, :],
                                     start=True, stop=True))
            stepS(lambda s: s.activation(ldt[:, :], dtn[:, :], AF.Ln))
            def prepV1(v, t0c=t0c):
                vbump(v.reduce_sum(cendc[:, :], dtT[:, t0c:t0c + 128], axis=AX))
                vfence(v)
                vbump(v.tensor_tensor(cendc[:, :], cendc[:, :], acolb[:, :], ALU.mult))
                vbump(v.tensor_copy(csb[:, :], p[1][0:128, 0:32]))
                vfence(v)
                return v.tensor_tensor(acn[:, :], csb[:, :], Abc[:, :], ALU.mult)
            stepV(prepV1)
            stepT(lambda t: t.matmul(p[2][0:32, 0:128], acn[:, :],
                                     cb["ident_f32"][:, :], start=True, stop=True))
            stepT(lambda t: t.matmul(p[3][:, 0:32], cb["ones_row_f32"][0:1, :],
                                     bacr[0:1, :], start=True, stop=True))
            def prepV2(v):
                vbump(v.tensor_copy(acT[:, :], p[2][0:32, 0:128]))
                vbump(v.tensor_tensor(wsc[:, :], acn[:, :], p[3][:, 0:32], ALU.add))
                vbump(v.tensor_tensor(asl[:, :], acn[:, :], ldt[:, :], ALU.subtract))
                vfence(v)
                return v.tensor_scalar(nasl[:, :], asl[:, :], -1.0, None, ALU.mult)
            stepV(prepV2)
            stepT(lambda t: t.matmul(p[3][0:1, 0:32], cendc[:, :], id32[:, :],
                                     start=True, stop=True))
            def prepS1(s, c=c):
                sbump(s.activation(dadj[:, c * 32:(c + 1) * 32], wsc[:, :], AF.Exp))
                return s.activation(dexp[:, :], acn[:, :], AF.Exp)
            stepS(prepS1)
            def prepV3(v):
                vbump(v.tensor_copy(rowsc[0:1, 0:32], p[3][0:1, 0:32]))
                vfence(v)
                return v.tensor_tensor(bacr[0:1, :], bacr[0:1, :], rowsc[0:1, 0:32],
                                       ALU.add)
            stepV(prepV3)
            stepT(lambda t: t.matmul(p[3][:, 0:32], cb["ones_row_f32"][0:1, :],
                                     rowsc[0:1, 0:32], start=True, stop=True))
            stepV(lambda v: v.tensor_tensor(wsc[:, :], p[3][:, 0:32], asl[:, :], ALU.subtract))
            def prepS2(s):
                sbump(s.activation(wdtn[:, :], wsc[:, :], AF.Exp))
                return s.activation(rowsc[0:1, 32:64], rowsc[0:1, 0:32], AF.Exp)
            stepS(prepS2)
            bsl = xbcc[0:64, 16 * 512 + ck * 128: 16 * 512 + ck * 128 + 128]
            csl = xbcc[0:64, 17 * 512 + ck * 128: 17 * 512 + ck * 128 + 128]
            stepT(lambda t: t.matmul(p[3][0:64, 0:32], cb["ones_row_f32"][0:1, 0:64],
                                     rowsc[0:1, 32:64], start=True, stop=True))
            def prepT2(t, bsl=bsl, csl=csl):
                t.matmul(p[4][:, 0:128], bsl, csl, start=True, stop=True)
                return t.matmul(p[5][0:128, 0:64], bsl,
                                cb["ident2_bf"][0:64, :], start=True, stop=True)
            stepT(prepT2)
            def prepV4(v):
                vbump(v.tensor_copy(fcols[:, :], p[3][0:64, 0:32]))
                vbump(v.tensor_copy(Gm[:, :], p[4][:, 0:128]))
                vbump(v.tensor_copy(Bn[:, :], p[5][0:128, 0:64]))
                return v.tensor_copy(Prun_bf[:, :], Prun[:, :])
            stepV(prepV4)
            H0 = c * NH

            snapc = {k2: cnt[k2] for k2 in cnt}

            def _hw(e, snapc=snapc):
                for k2, v in snapc.items():
                    if v:
                        e.wait_ge(SEMS[k2], v * (16 if k2 in ("G", "Y") else 1))

            def prefirst(v, c=c, H0=H0):
                H1 = H0 + 1
                v.wait_ge(s_hT, tT1(H1))
                if H1 >= 3:
                    v.wait_ge(s_hS, H1 - 2)
                v.tensor_tensor(pre2[:, 0:128], p[0][:, 0:128], mst[:, :],
                                ALU.add).then_inc(s_hV, 1)
            _sec("V", prefirst)

            def heads_T(t, c=c, ck=ck, csl=csl, snapc=snapc, H0=H0):
                def t1pair(P):
                    HL = H0 + 2 * P + 1
                    HF = HL + 1
                    pb = p[2 + ((HL - 1) // 2) % 2]
                    if HL >= 4:
                        t.wait_ge(s_hV, vccH(HL - 3, 6))
                    if HL >= 3:
                        t.wait_ge(s_hV, vpreH(HL - 2))
                    t.matmul(p[0][:, 0:128], sel32[:, (2 * P) * 128:(2 * P) * 128 + 128],
                             acT[:, :], start=True, stop=True)
                    t.matmul(pb[0:128, 128:192], csl,
                             Prun_bf[:, (2 * P) * 64:(2 * P) * 64 + 64],
                             start=True, stop=True)
                    xp = xbcc[0:128, P * 512 + ck * 128: P * 512 + ck * 128 + 128]
                    t.matmul(pb[0:128, 0:128], xp, cb["ident_bf"][:, :],
                             start=True, stop=True).then_inc(s_hT, 1)
                    if HF >= 4:
                        t.wait_ge(s_hV, vccH(HF - 3, 2))
                    t.matmul(p[1][:, 0:128],
                             sel32[:, (2 * P + 1) * 128:(2 * P + 1) * 128 + 128],
                             acT[:, :], start=True, stop=True)
                    t.matmul(pb[0:128, 192:256], csl,
                             Prun_bf[:, (2 * P + 1) * 64:(2 * P + 1) * 64 + 64],
                             start=True, stop=True).then_inc(s_hT, 1)

                _hw(t, snapc)
                t1pair(0)
                for P in range(NH // 2):
                    if P < NH // 2 - 1:
                        t1pair(P + 1)
                    for H in (H0 + 2 * P + 1, H0 + 2 * P + 2):
                        par = (H - 1) % 2
                        pY = p[4 + par]; pSl = p[6 + par]
                        t.wait_ge(s_hV, vccH(H, 3))
                        t.matmul(pY[0:128, 0:64], MT[:, par * 128:par * 128 + 128],
                                 Xn[:, par * 64:par * 64 + 64], start=True,
                                 stop=True).then_inc(s_hT, 1)
                        t.wait_ge(s_hV, vccH(H, 4))
                        t.matmul(pSl[0:64, 0:64], Bw[:, par * 64:par * 64 + 64],
                                 Xn[:, par * 64:par * 64 + 64], start=True,
                                 stop=True).then_inc(s_hT, 1)

            def heads_S(s, snapc=snapc, H0=H0):
                for h in range(NH):
                    H = H0 + h + 1
                    par = (H - 1) % 2
                    if h == 0:
                        _hw(s, snapc)
                    if H >= 2:
                        s.wait_ge(s_hS, H - 1)
                    s.wait_ge(s_hV, vpreH(H))
                    s.activation(E2[:, par * 128:par * 128 + 128],
                                 pre2[:, par * 128:par * 128 + 128], AF.Exp,
                                 bias=nasl[:, h:h + 1]).then_inc(s_hS, 1)

            def heads_V(v, c=c, snapc=snapc, H0=H0):
                for h in range(NH):
                    H = H0 + h + 1
                    par = (H - 1) % 2
                    pb = p[2 + ((H - 1) // 2) % 2]
                    pSl = p[6 + par]; pY = p[4 + par]
                    if h == 0:
                        _hw(v, snapc)
                    if H >= 2:
                        v.wait_ge(s_hV, vccH(H - 1, 10))
                    v.wait_ge(s_hS, H)
                    if H >= 3:
                        v.wait_ge(s_hT, tT2(H - 2))
                    v.tensor_tensor(MT[:, par * 128:par * 128 + 128],
                                    E2[:, par * 128:par * 128 + 128], Gm[:, :],
                                    ALU.mult).then_inc(s_hV, 1)
                    if h < NH - 1:
                        par1 = H % 2
                        v.wait_ge(s_hT, tT1(H + 1))
                        if H >= 2:
                            v.wait_ge(s_hS, H - 1)
                        v.tensor_tensor(pre2[:, par1 * 128:par1 * 128 + 128],
                                        p[0 + par1][:, 0:128], mst[:, :],
                                        ALU.add).then_inc(s_hV, 1)
                    else:
                        v.memset(wsc[0:1, 0:1], 0.0).then_inc(s_hV, 1)
                    v.wait_ge(s_hT, tT1(H - par))
                    v.tensor_copy(Xn[:, par * 64:par * 64 + 64],
                                  pb[0:128, par * 64:par * 64 + 64]).then_inc(s_hV, 1)
                    v.tensor_scalar(Bw[:, par * 64:par * 64 + 64], Bn[:, :],
                                    wdtn[:, h:h + 1], None, ALU.mult).then_inc(s_hV, 1)
                    v.wait_ge(s_hV, vccH(H, 3))
                    v.tensor_scalar(Dx[:, par * 64:par * 64 + 64],
                                    Xn[:, par * 64:par * 64 + 64], dn[:, h:h + 1],
                                    None, ALU.mult).then_inc(s_hV, 1)
                    v.wait_ge(s_hT, tT1(H))
                    v.tensor_scalar(tmpb[:, par * 64:par * 64 + 64],
                                    pb[0:128, 128 + par * 64:192 + par * 64],
                                    dexp[:, h:h + 1],
                                    None, ALU.mult).then_inc(s_hV, 1)
                    v.tensor_scalar(Pts[:, par * 64:par * 64 + 64],
                                    Prun[:, h * 64:(h + 1) * 64], fcols[:, h:h + 1],
                                    None, ALU.mult).then_inc(s_hV, 1)
                    v.wait_ge(s_hT, tT2(H))
                    v.wait_ge(s_hV, vccH(H, 5))
                    v.tensor_tensor(ubuf[:, par * 64:par * 64 + 64],
                                    pY[0:128, 0:64], Dx[:, par * 64:par * 64 + 64],
                                    ALU.add).then_inc(s_hV, 1)
                    v.wait_ge(s_hV, vccH(H, 8))
                    v.tensor_tensor(
                        ynat[:, (c % 2) * 2048 + h * 64:(c % 2) * 2048 + (h + 1) * 64],
                        ubuf[:, par * 64:par * 64 + 64],
                        tmpb[:, par * 64:par * 64 + 64], ALU.add).then_inc(s_hV, 1)
                    v.wait_ge(s_hT, tT3(H))
                    v.wait_ge(s_hV, vccH(H, 7))
                    v.tensor_tensor(Prun[:, h * 64:(h + 1) * 64],
                                    Pts[:, par * 64:par * 64 + 64], pSl[0:64, 0:64],
                                    ALU.add).then_inc(s_hV, 1)

            _sec("T", heads_T)
            _sec("S", heads_S)
            _sec("V", heads_V)
            def mkT(t, c=c):
                t.wait_ge(s_hV, VPC * (c + 1))
                t.wait_ge(s_hS, NH * (c + 1))
                t.wait_ge(s_hT, 3 * NH * (c + 1))
                return t.matmul(p[0][0:1, 0:1], cb["ones_row_f32"][0:1, 0:1],
                                cb["ones_row_f32"][0:1, 0:1], start=True, stop=True)
            stepT(mkT)
            def mkV(v, c=c):
                v.wait_ge(s_hV, VPC * (c + 1))
                return v.tensor_copy(lnsm[0:1, 0:1], lnsm[0:1, 0:1])
            stepV(mkV)
            def mkS(s, c=c):
                s.wait_ge(s_hS, NH * (c + 1))
                return s.activation(lnsm[0:1, 4:5], lnsm[0:1, 4:5], AF.Copy)
            stepS(mkS)
            stepY(lambda g, c=c: g.dma_start(
                out=yn_d[c * 128:(c + 1) * 128, :],
                in_=ynat[:, (c % 2) * 2048:(c % 2) * 2048 + 2048]))

        if upto < 3:
            @block.sync
            def _(g):
                g.wait_ge(s_Y, 16 * cnt["Y"])
            return
        stepG(lambda g: g.dma_start(out=stin_d.ap(), in_=Prun[:, :]))
        def coll(g):
            return g.collective_compute(
                "AllGather", mybir.AluOpType.bypass,
                replica_groups=rg,
                ins=[stin_d.ap().opt()],
                outs=[stout_d.ap().opt()],
            )
        _step("G", lambda e: coll(e).then_inc(s_AG), ndma=0)

        @block.gpsimd
        def _(g):
            g.wait_ge(s_AG, 1)
        stepG(lambda g: g.dma_start(out=Prun[:, :], in_=stout_d.ap()[0:64, :]))
        stepV(lambda v: v.tensor_scalar(Sin[:, :], Prun[:, :], prevc[:, 0:1], None, ALU.mult))

        if upto < 4:
            @block.sync
            def _(g):
                g.wait_ge(s_Y, 16 * cnt["Y"])
            return
        stepG_multi([
            (lambda g, dm=dm: g.dma_start(
                out=woutb.rearrange("p (d2 j c) -> p d2 j c", d2=8, j=16)[:, dm, :, :],
                in_=wout_in.rearrange("(j p) c -> p j c", p=128)
                [:, :, dm * 128:(dm + 1) * 128]))
            for dm in range(8)])
        def p2loads(c2):
            return [
                (lambda g, c2=c2: g.dma_start(
                    out=yrl[:, (c2 % 2) * 2048:(c2 % 2) * 2048 + 2048],
                    in_=yn_d[c2 * 128:(c2 + 1) * 128, :])),
                (lambda g, c2=c2: g.dma_start(
                    out=zrl[:, (c2 % 2) * 2048:(c2 % 2) * 2048 + 2048],
                    in_=z_d[c2 * 128:(c2 + 1) * 128, :])),
                (lambda g, c2=c2: g.dma_start(
                    out=xbcr[:, (c2 % 2) * 131:(c2 % 2) * 131 + 131],
                    in_=xbc_d[17 * 128:18 * 128, c2 * 128:c2 * 128 + 131])),
            ]
        stepG_multi(p2loads(0))
        for c in range(NSC):
            snap_pre = {k2: cnt[k2] for k2 in cnt}
            def convC(v, c=c):
                xb0 = (c % 2) * 131
                vbump(v.tensor_scalar(cvA[:, 0:128], xbcr[:, xb0:xb0 + 128],
                                      convw[:, 4 * 17:4 * 17 + 1],
                                      convb[:, 17:18], ALU.mult, ALU.add))
                for jj in range(1, 4):
                    vbump(v.tensor_scalar(cvB[:, 0:128], xbcr[:, xb0 + jj:xb0 + jj + 128],
                                          convw[:, 4 * 17 + jj:4 * 17 + jj + 1],
                                          None, ALU.mult))
                    vfence(v)
                    ins = v.tensor_tensor(cvA[:, 0:128], cvA[:, 0:128],
                                          cvB[:, 0:128], ALU.add)
                    if jj < 3:
                        vbump(ins)
                        vfence(v)
                return ins
            stepV(convC)
            stepS(lambda s: s.activation(ctl[:, :], cvA[:, 0:128], AF.Silu))
            def umms(t):
                for i in range(4):
                    ins = t.matmul(p[i][:, :], ctl[0:64, :],
                                   Sin[:, i * 512:(i + 1) * 512],
                                   start=True, stop=True)
                return ins
            stepT(umms)
            def corr(v, c=c):
                for h in range(NH):
                    dsl = Dx[:, (h % 2) * 64:(h % 2) * 64 + 64]
                    vbump(v.tensor_scalar(
                        dsl, p[h // 8][:, (h % 8) * 64:(h % 8) * 64 + 64],
                        dadj[:, c * 32 + h: c * 32 + h + 1], None, ALU.mult))
                    vfence(v)
                    ins = v.tensor_tensor(
                        yrl[:, (c % 2) * 2048 + h * 64:(c % 2) * 2048 + (h + 1) * 64],
                        yrl[:, (c % 2) * 2048 + h * 64:(c % 2) * 2048 + (h + 1) * 64],
                        dsl, ALU.add)
                    if h < NH - 1:
                        vbump(ins)
                return ins
            stepV(corr)
            stepS(lambda s, c=c: s.activation(zsil[:, :],
                                              zrl[:, (c % 2) * 2048:(c % 2) * 2048 + 2048],
                                              AF.Silu))
            def gat1(v, c=c):
                vbump(v.tensor_tensor(ynf[:, :],
                                      yrl[:, (c % 2) * 2048:(c % 2) * 2048 + 2048],
                                      zsil[:, :], ALU.mult))
                vfence(v)
                vbump(v.tensor_tensor(zsil[:, :], ynf[:, :], ynf[:, :], ALU.mult))
                vfence(v)
                vbump(v.reduce_sum(rstd1[:, :], zsil[:, :], axis=AX))
                vfence(v)
                return v.tensor_scalar(rstd1[:, :], rstd1[:, :], 1.0 / 2048.0,
                                       None, ALU.mult)
            stepV(gat1)
            def gat2(s):
                sbump(s.activation(rstd1[:, :], rstd1[:, :], AF.Ln,
                                   bias=eps128[:, 0:1]))
                sfence(s)
                return s.activation(rstd1[:, :], rstd1[:, :], AF.Exp, scale=-0.5)
            stepS(gat2)
            def gat3(v):
                vbump(v.tensor_scalar(ynf[:, :], ynf[:, :], rstd1[:, 0:1],
                                      None, ALU.mult))
                vfence(v)
                return v.tensor_tensor(ynf[:, :], ynf[:, :], nwb[:, :], ALU.mult)
            stepV(gat3)
            for dq in range(4):
                def tmm(t, dq=dq):
                    for k in range(4):
                        di = 4 * dq + k
                        ins = t.matmul(p[k][0:128, 0:128],
                                       ynf[:, di * 128:(di + 1) * 128],
                                       cb["ident_bf"][:, :], start=True, stop=True)
                    return ins
                stepT(tmm)
                def tcp(v, dq=dq, c=c):
                    for k in range(4):
                        di = 4 * dq + k
                        ins = v.tensor_copy(
                            ynT[:, di * 512 + (c % 4) * 128:
                                di * 512 + (c % 4) * 128 + 128],
                            p[k][0:128, 0:128])
                        if k < 3:
                            vbump(ins)
                    return ins
                stepV(tcp)
            if c + 1 < NSC:
                def pf(e, c=c, snap_pre=snap_pre):
                    for k2, v2 in snap_pre.items():
                        if v2:
                            e.wait_ge(SEMS[k2], v2 * (16 if k2 in ("G", "Y") else 1))
                    for f in p2loads(c + 1):
                        f(e).then_inc(s_G, 16)
                _sec("G", pf)
                cnt["G"] += 3
            if c % 4 == 3:
                oc = c // 4
                stepG_multi([
                    (lambda g, j=j, oc=oc: g.dma_start(
                        out=tile(xf32, j, CH),
                        in_=x_in[128 * j:128 * (j + 1), 3 + oc * CH: 3 + (oc + 1) * CH]))
                    for j in range(NT)])
                for dp2 in range(4):
                    def womm(t, dp2=dp2):
                        for k in range(2):
                            dm = 2 * dp2 + k
                            for di in range(16):
                                ins = t.matmul(p[6 + k][:, :],
                                               woutb[:, dm * 16 * 128 + di * 128:
                                                     dm * 16 * 128 + (di + 1) * 128],
                                               ynT[:, di * 512:(di + 1) * 512],
                                               start=(di == 0), stop=(di == 15))
                        return ins
                    stepT(womm)
                    def wadd(v, dp2=dp2):
                        for k in range(2):
                            dm = 2 * dp2 + k
                            ins = v.tensor_tensor(
                                tile(xf32, dm, CH), tile(xf32, dm, CH), p[6 + k][:, :],
                                ALU.add)
                            if k == 0:
                                vbump(ins)
                        return ins
                    stepV(wadd)
                stepY(lambda g, oc=oc: g.dma_start(
                    out=y_out.rearrange("(j p) c -> p j c", p=128)
                    [:, :, oc * CH:(oc + 1) * CH],
                    in_=xf32.rearrange("p (j c) -> p j c", j=NT)))

        @block.sync
        def _(g):
            g.wait_ge(s_Y, 16 * cnt["Y"])



D_CONSTS = ("ones_col_bf", "ones_row_bf")


def build_D():
    nc = bass.Bass("TRN2", target_bir_lowering=False)
    caps = declare_constants(nc, D_CONSTS)
    x_in = nc.dram_tensor("xTd", [D, 2049], F32, kind="ExternalInput").ap()
    wr_in = nc.dram_tensor("WrT", [D, D], F32, kind="ExternalInput").ap()
    wk_in = nc.dram_tensor("WkT", [D, D], F32, kind="ExternalInput").ap()
    wv_in = nc.dram_tensor("WvT", [D, D], F32, kind="ExternalInput").ap()
    mur_in = nc.dram_tensor("mur", [128, 8], F32, kind="ExternalInput").ap()
    muk_in = nc.dram_tensor("muk", [128, 8], F32, kind="ExternalInput").ap()
    y_out = nc.dram_tensor("yT", [D, T], F32, kind="ExternalOutput").ap()
    emit_D(nc, caps, x_in, wr_in, wk_in, wv_in, mur_in, muk_in, y_out)
    return nc


def emit_D(nc, caps, x_in, wr_in, wk_in, wv_in, mur_in, muk_in, y_out, P="D_"):

    LNCH = [(0, 1), (1, 512), (513, 512), (1025, 512), (1537, 512)]

    with ExitStack() as ctx:
        def sb(name, shape, dt):
            return ctx.enter_context(nc.sbuf_tensor(P + name, shape, dt))
        def sem(name):
            return ctx.enter_context(nc.semaphore(P + name))

        s_T = sem("s_T"); s_V = sem("s_V"); s_S = sem("s_S")
        s_G = sem("s_G"); s_Y = sem("s_Y"); s_lw = sem("s_lw")

        xln = sb("xlnD", [128, NT * 2049], BF16)
        wrb = sb("wrbD", [128, NT * 1024], BF16)
        wkb = sb("wkbD", [128, NT * 1024], BF16)
        wvb = sb("wvbD", [128, NT * 1024], BF16)
        rT = sb("rTD", [128, NT * T], BF16)
        k2T = sb("k2TD", [128, NT * CH], BF16)
        xsh = sb("xshD", [128, NT * CH], BF16)
        xf32 = sb("xf32D", [128, NT * CH], F32)
        mur = sb("murD", [128, 8], F32)
        muk = sb("mukD", [128, 8], F32)
        omr = sb("omrD", [128, 8], F32)
        omk = sb("omkD", [128, 8], F32)
        lnsm = sb("lnsmD", [1, 2 * CH], F32)
        pair_bf = sb("pairD", [1, 2 * CH], BF16)
        rbuf = sb("rbufD", [128, 2 * CH], BF16)
        kst = sb("kstD", [128, CH], BF16)
        kst2 = sb("kst2D", [128, CH], BF16)
        eps1 = sb("eps1D", [1, 1], F32)
        cb = {n: sb("cD_" + n, shp, dt) for n, shp, dt in CONST_SPEC if n in D_CONSTS}
        xbf = rT[:, 0:4096]
        sqb = rT[:, 4096:8192]

        p = [ctx.enter_context(nc.psum_tensor(P + "pd%d" % i, [128, CH], F32))
             for i in range(6)]

        block = ctx.enter_context(nc.Block())

        cnt = {"T": 0, "V": 0, "S": 0, "G": 0, "Y": 0}
        SEMS = {"T": s_T, "V": s_V, "S": s_S, "G": s_G, "Y": s_Y}

        def _sec(k, body):
            if k == "T":
                @block.tensor
                def _(e): body(e)
            elif k == "V":
                @block.vector
                def _(e): body(e)
            elif k == "S":
                @block.scalar
                def _(e): body(e)
            elif k == "G":
                @block.gpsimd
                def _(e): body(e)
            else:
                @block.sync
                def _(e): body(e)

        def _step(k, f, ndma=1):
            snap = {k2: cnt[k2] for k2 in cnt}
            def body(e):
                for k2, v in snap.items():
                    if v == 0:
                        continue
                    e.wait_ge(SEMS[k2], v * (16 if k2 in ("G", "Y") else 1))
                f(e)
            _sec(k, body)
            cnt[k] += ndma if k in ("G", "Y") else 1

        def stepT(f): _step("T", lambda e: f(e).then_inc(s_T, 1))
        def stepV(f): _step("V", lambda e: f(e).then_inc(s_V, 1))
        def stepS(f): _step("S", lambda e: f(e).then_inc(s_S, 1))
        def stepG(f): _step("G", lambda e: f(e).then_inc(s_G, 16))
        def stepY(f): _step("Y", lambda e: f(e).then_inc(s_Y, 16))

        def stepG_multi(fs):
            def body(e):
                for f in fs:
                    f(e).then_inc(s_G, 16)
            _step("G", body, ndma=len(fs))

        s_vself = sem("s_vself"); s_sself = sem("s_sself")
        _mf = {"V": [0, s_vself], "S": [0, s_sself]}

        def mk_chain(k):
            st = _mf[k]
            def bump(ins):
                ins.then_inc(st[1], 1)
                st[0] += 1
            def fence(e):
                e.wait_ge(st[1], st[0])
            return bump, fence
        vbump, vfence = mk_chain("V")
        sbump, sfence = mk_chain("S")

        NLW = len(D_CONSTS) + 2

        @block.sync
        def _(g):
            for name in D_CONSTS:
                g.dma_start(out=cb[name][:, :], in_=caps[name]).then_inc(s_lw, 16)
            g.dma_start(out=mur[:, :], in_=mur_in).then_inc(s_lw, 16)
            g.dma_start(out=muk[:, :], in_=muk_in).then_inc(s_lw, 16)

        for k in ("T", "V", "S", "G", "Y"):
            _sec(k, lambda e: e.wait_ge(s_lw, 16 * NLW))

        stepG_multi([
            (lambda g: dma_wtiles(g, wrb[:, :], wr_in, NT)),
            (lambda g: dma_wtiles(g, wkb[:, :], wk_in, NT)),
            (lambda g: dma_wtiles(g, wvb[:, :], wv_in, NT)),
        ])
        stepV(lambda v: v.memset(eps1[0:1, :], LN_EPS))
        stepV(lambda v: v.tensor_scalar(omr[:, :], mur[:, :], -1.0, 1.0, ALU.mult, ALU.add))
        stepV(lambda v: v.tensor_scalar(omk[:, :], muk[:, :], -1.0, 1.0, ALU.mult, ALU.add))

        for ci, (t0, tw) in enumerate(LNCH):
            stepG_multi([
                (lambda g, j=j, t0=t0, tw=tw: g.dma_start(
                    out=xbf[:, j * 512:j * 512 + tw],
                    in_=x_in[128 * j:128 * (j + 1), t0:t0 + tw],
                    allow_slow_non_contiguous=(tw == 1)))
                for j in range(NT)])
            def sqs(s, tw=tw):
                for j in range(NT):
                    ins = s.activation(sqb[:, j * 512:j * 512 + tw],
                                       xbf[:, j * 512:j * 512 + tw], AF.Square)
                return ins
            stepS(sqs)
            def stats(t, tw=tw):
                for j in range(NT):
                    t.matmul(p[2][0:1, 0:tw], cb["ones_col_bf"][:, :],
                             xbf[:, j * 512:j * 512 + tw], start=(j == 0), stop=(j == NT - 1))
                for j in range(NT):
                    ins = t.matmul(p[3][0:1, 0:tw], cb["ones_col_bf"][:, :],
                                   sqb[:, j * 512:j * 512 + tw], start=(j == 0),
                                   stop=(j == NT - 1))
                return ins
            stepT(stats)
            def lnv1(v, tw=tw):
                vbump(v.tensor_scalar(lnsm[0:1, 0:tw], p[2][0:1, 0:tw],
                                      1.0 / D, None, ALU.mult))
                vbump(v.tensor_scalar(lnsm[0:1, CH:CH + tw], p[3][0:1, 0:tw],
                                      1.0 / D, None, ALU.mult))
                vfence(v)
                vbump(v.tensor_tensor(p[2][0:1, 0:tw], lnsm[0:1, 0:tw],
                                      lnsm[0:1, 0:tw], ALU.mult))
                vfence(v)
                return v.tensor_tensor(lnsm[0:1, CH:CH + tw], lnsm[0:1, CH:CH + tw],
                                       p[2][0:1, 0:tw], ALU.subtract)
            stepV(lnv1)
            def lns1(s, tw=tw):
                sbump(s.activation(lnsm[0:1, CH:CH + tw], lnsm[0:1, CH:CH + tw],
                                   AF.Ln, bias=eps1[0:1, 0:1]))
                sfence(s)
                return s.activation(lnsm[0:1, CH:CH + tw], lnsm[0:1, CH:CH + tw],
                                    AF.Exp, scale=-0.5)
            stepS(lns1)
            def lnv2(v, tw=tw):
                vbump(v.tensor_copy(pair_bf[0:1, 0:tw], lnsm[0:1, CH:CH + tw]))
                vbump(v.tensor_tensor(lnsm[0:1, 0:tw], lnsm[0:1, 0:tw],
                                      lnsm[0:1, CH:CH + tw], ALU.mult))
                vfence(v)
                return v.tensor_copy(pair_bf[0:1, CH:CH + tw], lnsm[0:1, 0:tw])
            stepV(lnv2)
            def bcasts(t, tw=tw):
                t.matmul(p[4][:, 0:tw], cb["ones_row_bf"][0:1, :], pair_bf[0:1, 0:tw],
                         start=True, stop=True)
                return t.matmul(p[5][:, 0:tw], cb["ones_row_bf"][0:1, :],
                                pair_bf[0:1, CH:CH + tw], start=True, stop=True)
            stepT(bcasts)
            def lnv3(v, t0=t0, tw=tw):
                vbump(v.tensor_copy(rbuf[:, 0:tw], p[4][:, 0:tw]))
                vbump(v.tensor_copy(rbuf[:, CH:CH + tw], p[5][:, 0:tw]))
                vfence(v)
                for j in range(NT):
                    vbump(v.tensor_tensor(xln[:, j * 2049 + t0: j * 2049 + t0 + tw],
                                          xbf[:, j * 512:j * 512 + tw], rbuf[:, 0:tw],
                                          ALU.mult))
                vfence(v)
                for j in range(NT):
                    ins = v.tensor_tensor(xln[:, j * 2049 + t0: j * 2049 + t0 + tw],
                                          xln[:, j * 2049 + t0: j * 2049 + t0 + tw],
                                          rbuf[:, CH:CH + tw], ALU.subtract)
                    if j < NT - 1:
                        vbump(ins)
                return ins
            stepV(lnv3)

        for ci in range(4):
            off = 1 + ci * 512
            def shmix(v, off=off, mu=mur, om=omr):
                for j in range(NT):
                    v.tensor_scalar(xsh[:, j * 512:(j + 1) * 512],
                                    xln[:, j * 2049 + off: j * 2049 + off + 512],
                                    mu[:, j:j + 1], None, ALU.mult)
                for j in range(NT):
                    ins = v.tensor_scalar(kst[:, 0:512] if False else
                                          xsh[:, j * 512:(j + 1) * 512],
                                          xln[:, j * 2049 + off - 1: j * 2049 + off - 1 + 512],
                                          om[:, j:j + 1], None, ALU.mult)
                return ins
            def sh1(v, off=off, mu=mur):
                for j in range(NT):
                    ins = v.tensor_scalar(xsh[:, j * 512:(j + 1) * 512],
                                          xln[:, j * 2049 + off: j * 2049 + off + 512],
                                          mu[:, j:j + 1], None, ALU.mult)
                return ins
            def sh2(v, off=off, om=omr):
                for j in range(NT):
                    ins = v.tensor_scalar(sqb[:, j * 512:(j + 1) * 512] if False else
                                          xf32.bitcast(BF16)[:, j * 512:(j + 1) * 512]
                                          if False else
                                          kst[:, 0:512],
                                          xln[:, j * 2049 + off - 1: j * 2049 + off - 1 + 512],
                                          om[:, j:j + 1], None, ALU.mult)
                return ins
            def shiftr(v, off=off):
                for j in range(NT):
                    tmp = kst[:, :] if j % 2 == 0 else kst2[:, 0:512]
                    vbump(v.tensor_scalar(
                        xsh[:, j * 512:(j + 1) * 512],
                        xln[:, j * 2049 + off: j * 2049 + off + 512],
                        mur[:, j:j + 1], None, ALU.mult))
                    vbump(v.tensor_scalar(
                        tmp, xln[:, j * 2049 + off - 1: j * 2049 + off - 1 + 512],
                        omr[:, j:j + 1], None, ALU.mult))
                    vfence(v)
                    ins = v.tensor_tensor(
                        xsh[:, j * 512:(j + 1) * 512], xsh[:, j * 512:(j + 1) * 512],
                        tmp, ALU.add)
                    if j < NT - 1:
                        vbump(ins)
                return ins
            stepV(shiftr)
            for op2 in range(NT // 2):
                def rmm(t, op2=op2):
                    for k in range(2):
                        ot = 2 * op2 + k
                        for j in range(NT):
                            ins = t.matmul(p[k][:, :],
                                           wrb[:, j * 1024 + ot * 128:
                                               j * 1024 + ot * 128 + 128],
                                           xsh[:, j * 512:(j + 1) * 512],
                                           start=(j == 0), stop=(j == NT - 1))
                    return ins
                stepT(rmm)
                def rsig(s, op2=op2, ci=ci):
                    for k in range(2):
                        ot = 2 * op2 + k
                        ins = s.activation(
                            rT[:, ot * T + ci * 512: ot * T + ci * 512 + 512],
                            p[k][:, :], AF.Sigmoid)
                    return ins
                stepS(rsig)

        for ci in range(4):
            off = 1 + ci * 512
            def shiftk(v, off=off):
                for j in range(NT):
                    tmp = kst[:, :] if j % 2 == 0 else kst2[:, 0:512]
                    vbump(v.tensor_scalar(
                        xsh[:, j * 512:(j + 1) * 512],
                        xln[:, j * 2049 + off: j * 2049 + off + 512],
                        muk[:, j:j + 1], None, ALU.mult))
                    vbump(v.tensor_scalar(
                        tmp, xln[:, j * 2049 + off - 1: j * 2049 + off - 1 + 512],
                        omk[:, j:j + 1], None, ALU.mult))
                    vfence(v)
                    ins = v.tensor_tensor(
                        xsh[:, j * 512:(j + 1) * 512], xsh[:, j * 512:(j + 1) * 512],
                        tmp, ALU.add)
                    if j < NT - 1:
                        vbump(ins)
                return ins
            stepV(shiftk)
            for op2 in range(NT // 2):
                def kmm(t, op2=op2):
                    for k in range(2):
                        ot = 2 * op2 + k
                        for j in range(NT):
                            ins = t.matmul(p[k][:, :],
                                           wkb[:, j * 1024 + ot * 128:
                                               j * 1024 + ot * 128 + 128],
                                           xsh[:, j * 512:(j + 1) * 512],
                                           start=(j == 0), stop=(j == NT - 1))
                    return ins
                stepT(kmm)
                def krel(s, op2=op2):
                    for k in range(2):
                        ot = 2 * op2 + k
                        ins = s.activation(k2T[:, ot * 512:(ot + 1) * 512],
                                           p[k][:, :], AF.Relu)
                    return ins
                stepS(krel)
                def ksq(v, op2=op2):
                    for k in range(2):
                        ot = 2 * op2 + k
                        ins = v.tensor_tensor(
                            k2T[:, ot * 512:(ot + 1) * 512],
                            k2T[:, ot * 512:(ot + 1) * 512],
                            k2T[:, ot * 512:(ot + 1) * 512], ALU.mult)
                        if k == 0:
                            vbump(ins)
                    return ins
                stepV(ksq)
            stepG_multi([
                (lambda g, j=j, ci=ci: g.dma_start(
                    out=tile(xf32, j, CH),
                    in_=x_in[128 * j:128 * (j + 1), 1 + ci * CH: 1 + (ci + 1) * CH]))
                for j in range(NT)])
            for op2 in range(NT // 2):
                def vmm(t, op2=op2):
                    for k in range(2):
                        ot = 2 * op2 + k
                        for j in range(NT):
                            ins = t.matmul(p[2 + k][:, :],
                                           wvb[:, j * 1024 + ot * 128:
                                               j * 1024 + ot * 128 + 128],
                                           k2T[:, j * 512:(j + 1) * 512],
                                           start=(j == 0), stop=(j == NT - 1))
                    return ins
                stepT(vmm)
                def vadd(v, op2=op2, ci=ci):
                    for k in range(2):
                        ot = 2 * op2 + k
                        tmp = kst[:, :] if k == 0 else kst2[:, 0:512]
                        vbump(v.tensor_tensor(
                            tmp, p[2 + k][:, :],
                            rT[:, ot * T + ci * 512: ot * T + ci * 512 + 512],
                            ALU.mult))
                        vfence(v)
                        ins = v.tensor_tensor(tile(xf32, ot, CH), tile(xf32, ot, CH),
                                              tmp, ALU.add)
                        if k == 0:
                            vbump(ins)
                    return ins
                stepV(vadd)
            stepY(lambda g, ci=ci: g.dma_start(
                out=y_out.rearrange("(j p) c -> p j c", p=128)[:, :, ci * CH:(ci + 1) * CH],
                in_=xf32.rearrange("p (j c) -> p j c", j=NT)))

        @block.sync
        def _(g):
            g.wait_ge(s_Y, 16 * cnt["Y"])



def emit_glue(nc, nb, src_ap, dst_ap, prev_ap, rg, P):
    bin_d = nc.dram_tensor(P + "bin", [128, NT * nb], F32)
    bout_d = nc.dram_tensor(P + "bout", [256, NT * nb], F32)
    with ExitStack() as ctx:
        bb = ctx.enter_context(nc.sbuf_tensor(P + "bb", [128, NT * nb], F32))
        pv = ctx.enter_context(nc.sbuf_tensor(P + "pv", [128, 1], F32))
        sg = ctx.enter_context(nc.semaphore(P + "sg"))
        sa = ctx.enter_context(nc.semaphore(P + "sa"))
        sv = ctx.enter_context(nc.semaphore(P + "sv"))
        block = ctx.enter_context(nc.Block())

        @block.gpsimd
        def _(g):
            g.dma_start(out=pv[:, :], in_=prev_ap).then_inc(sg, 16)
            g.dma_start(out=bb.rearrange("p (j c) -> p j c", j=NT),
                        in_=src_ap.rearrange("(j p) c -> p j c", p=128),
                        allow_slow_non_contiguous=(nb == 1)).then_inc(sg, 16)
            g.wait_ge(sg, 32)
            g.dma_start(out=bin_d.ap(), in_=bb[:, :]).then_inc(sg, 16)
            g.wait_ge(sg, 48)
            g.collective_compute(
                "AllGather", mybir.AluOpType.bypass,
                replica_groups=rg,
                ins=[bin_d.ap().opt()],
                outs=[bout_d.ap().opt()],
            ).then_inc(sa)
            g.wait_ge(sa, 1)
            g.dma_start(out=bb[:, :], in_=bout_d.ap()[0:128, :]).then_inc(sg, 16)
            g.wait_ge(sv, 1)
            g.dma_start(out=dst_ap.rearrange("(j p) c -> p j c", p=128),
                        in_=bb.rearrange("p (j c) -> p j c", j=NT),
                        allow_slow_non_contiguous=(nb == 1)).then_inc(sg, 16)
            g.wait_ge(sg, 80)

        @block.vector
        def _(v):
            v.wait_ge(sg, 64)
            v.tensor_scalar(bb[:, :], bb[:, :], pv[:, 0:1], None,
                            ALU.mult).then_inc(sv, 1)


def build_fused(n_cores=8):
    nc = bass.Bass("TRN2", target_bir_lowering=False)
    caps = declare_constants(nc)
    rg = [[2 * i, 2 * i + 1] for i in range(max(1, n_cores // 2))]
    x_in = nc.dram_tensor("xT_full", [D, TF], F32, kind="ExternalInput").ap()
    wq_in = nc.dram_tensor("wq", [D, D], F32, kind="ExternalInput").ap()
    wk_in = nc.dram_tensor("wk", [D, D], F32, kind="ExternalInput").ap()
    wo_in = nc.dram_tensor("wo", [D, D], F32, kind="ExternalInput").ap()
    bo_in = nc.dram_tensor("bo", [128, 8], F32, kind="ExternalInput").ap()
    w1_in = nc.dram_tensor("w1T_all", [9, D, 4096], F32, kind="ExternalInput").ap()
    w2_in = nc.dram_tensor("w2T_all", [9, 8, 4096, 128], F32, kind="ExternalInput").ap()
    b1_in = nc.dram_tensor("b1_all", [9, 128, 32], F32, kind="ExternalInput").ap()
    b2_in = nc.dram_tensor("b2_all", [9, 1024], F32, kind="ExternalInput").ap()
    rw_in = nc.dram_tensor("rwT", [D, 8], F32, kind="ExternalInput").ap()
    rb_in = nc.dram_tensor("rb", [1, 8], F32, kind="ExternalInput").ap()
    sel9_in = nc.dram_tensor("sel9", [9, 1024], BF16, kind="ExternalInput").ap()
    win_in = nc.dram_tensor("W_in_p", [D, PROJ_COLS], F32, kind="ExternalInput").ap()
    wout_in = nc.dram_tensor("W_out", [2048, D], F32, kind="ExternalInput").ap()
    convw_in = nc.dram_tensor("convw", [128, NCVT * 4], F32, kind="ExternalInput").ap()
    convb_in = nc.dram_tensor("convb", [128, NCVT], F32, kind="ExternalInput").ap()
    alog_r = nc.dram_tensor("alog_r", [1, 32], F32, kind="ExternalInput").ap()
    dtb_in = nc.dram_tensor("dtb", [32, 1], F32, kind="ExternalInput").ap()
    dn_in = nc.dram_tensor("Dn", [128, 32], F32, kind="ExternalInput").ap()
    nw_in = nc.dram_tensor("nw_row", [1, 2048], F32, kind="ExternalInput").ap()
    prev_in = nc.dram_tensor("prev_col", [64, 1], F32, kind="ExternalInput").ap()
    prev128_in = nc.dram_tensor("prev128", [128, 1], F32, kind="ExternalInput").ap()
    ut_in = nc.dram_tensor("UTones", [128, 128], F32, kind="ExternalInput").ap()
    id32_in = nc.dram_tensor("ident32", [32, 32], F32, kind="ExternalInput").ap()
    mst_in = nc.dram_tensor("maskST", [128, 128], F32, kind="ExternalInput").ap()
    sel32_in = nc.dram_tensor("sel32", [32, 32 * 128], F32, kind="ExternalInput").ap()
    wr_in = nc.dram_tensor("WrT", [D, D], F32, kind="ExternalInput").ap()
    wkd_in = nc.dram_tensor("WkT", [D, D], F32, kind="ExternalInput").ap()
    wv_in = nc.dram_tensor("WvT", [D, D], F32, kind="ExternalInput").ap()
    mur_in = nc.dram_tensor("mur", [128, 8], F32, kind="ExternalInput").ap()
    muk_in = nc.dram_tensor("muk", [128, 8], F32, kind="ExternalInput").ap()
    y_out = nc.dram_tensor("yT", [D, T], F32, kind="ExternalOutput").ap()
    xb_d = nc.dram_tensor("xb_d", [D, T], F32).ap()
    xc_d = nc.dram_tensor("xc_d", [D, 2051], F32).ap()
    xd_d = nc.dram_tensor("xd_d", [D, 2049], F32).ap()

    emit_A(nc, caps, x_in, wq_in, wk_in, wo_in, bo_in, xb_d, P="A_")
    emit_B(nc, caps, xb_d, w1_in, w2_in, b1_in, b2_in, rw_in, rb_in, sel9_in,
           xc_d[:, 3:2051], P="B_")
    emit_glue(nc, 3, xc_d[:, 2048:2051], xc_d[:, 0:3], prev128_in, rg, P="G1_")
    emit_C(nc, caps, xc_d, win_in, wout_in, convw_in, convb_in, alog_r, dtb_in,
           dn_in, nw_in, prev_in, ut_in, id32_in, mst_in, sel32_in,
           xd_d[:, 1:2049], n_cores=n_cores, P="C_")
    emit_glue(nc, 1, xd_d[:, 2048:2049], xd_d[:, 0:1], prev128_in, rg, P="G2_")
    emit_D(nc, caps, xd_d, wr_in, wkd_in, wv_in, mur_in, muk_in, y_out, P="D_")
    return nc


def _roll_xT(xb, half):
    if half == 0:
        arr = xb
    else:
        arr = np.concatenate([xb[2048:], xb[:2048]], axis=0)
    return np.ascontiguousarray(arr.T)


def _hostA(consts, ins, x, c):
    b, half = c // 2, c % 2
    m = dict(consts)
    m["xT_full"] = _roll_xT(x[b], half)
    m["wq"] = np.ascontiguousarray(ins["L0_attn_Wq"])
    m["wk"] = np.ascontiguousarray(ins["L0_attn_Wk"])
    m["wo"] = np.ascontiguousarray(ins["L0_attn_Wo"])
    m["bo"] = np.ascontiguousarray(ins["L0_attn_bo"].reshape(8, 128).T)
    return m


def _sharedB(ins):
    m = {}
    w1 = np.concatenate([ins["L0_ffn_shared_w1"][None], ins["L0_ffn_spec_w1"]], 0)
    w2 = np.concatenate([ins["L0_ffn_shared_w2"][None], ins["L0_ffn_spec_w2"]], 0)
    b1 = np.concatenate([ins["L0_ffn_shared_b1"][None], ins["L0_ffn_spec_b1"]], 0)
    b2 = np.concatenate([ins["L0_ffn_shared_b2"][None], ins["L0_ffn_spec_b2"]], 0)
    m["w1T_all"] = np.ascontiguousarray(np.transpose(w1, (0, 2, 1)))
    w2T = np.transpose(w2, (0, 2, 1))
    m["w2T_all"] = np.ascontiguousarray(w2T.reshape(9, 4096, 8, 128).transpose(0, 2, 1, 3))
    m["b1_all"] = np.ascontiguousarray(b1.reshape(9, 32, 128).transpose(0, 2, 1))
    m["b2_all"] = np.ascontiguousarray(b2)
    m["rwT"] = np.ascontiguousarray(ins["L0_ffn_router_w"].T)
    m["rb"] = np.ascontiguousarray(ins["L0_ffn_router_b"][None, :])
    return m


def _sharedC(ins):
    m = {}
    W_in = ins["L1_attn_W_in"]
    wp = np.zeros((1024, 4384), np.float32)
    wp[:, 0:2048] = W_in[:, 0:2048]
    wp[:, 2048:4096] = W_in[:, 2048:4096]
    wp[:, 4096:4160] = W_in[:, 4096:4160]
    wp[:, 4224:4288] = W_in[:, 4160:4224]
    wp[:, 4352:4384] = W_in[:, 4224:4256]
    m["W_in_p"] = wp
    m["W_out"] = np.ascontiguousarray(ins["L1_attn_W_out"])
    cw = ins["L1_attn_conv_w"][:, 0, :]
    cb_ = ins["L1_attn_conv_b"]
    cwp = np.zeros((18 * 128, 4), np.float32)
    cbp = np.zeros((18 * 128,), np.float32)
    cwp[0:2048] = cw[0:2048]; cbp[0:2048] = cb_[0:2048]
    cwp[16 * 128:16 * 128 + 64] = cw[2048:2112]
    cbp[16 * 128:16 * 128 + 64] = cb_[2048:2112]
    cwp[17 * 128:17 * 128 + 64] = cw[2112:2176]
    cbp[17 * 128:17 * 128 + 64] = cb_[2112:2176]
    m["convw"] = np.ascontiguousarray(cwp.reshape(18, 128, 4).transpose(1, 0, 2).reshape(128, 72))
    m["convb"] = np.ascontiguousarray(cbp.reshape(18, 128).T)
    m["alog_r"] = np.ascontiguousarray(ins["L1_attn_A_log"][None, :])
    m["dtb"] = np.ascontiguousarray(ins["L1_attn_dt_bias"][:, None])
    m["Dn"] = np.ascontiguousarray(np.tile(ins["L1_attn_D_param"][None, :], (128, 1)))
    m["nw_row"] = np.ascontiguousarray(ins["L1_attn_norm_w"][None, :])
    m["UTones"] = np.triu(np.ones((128, 128), np.float32))
    mstv = np.zeros((128, 128), np.float32)
    mstv[np.tril_indices(128, -1)] = -1e30
    m["maskST"] = mstv
    m["ident32"] = np.eye(32, dtype=np.float32)
    s32 = np.zeros((32, 32 * 128), np.float32)
    for h in range(32):
        s32[h, h * 128:(h + 1) * 128] = 1.0
    m["sel32"] = s32
    return m


def _sharedD(ins):
    m = {}
    m["WrT"] = np.ascontiguousarray(ins["L1_ffn_W_r"].T)
    m["WkT"] = np.ascontiguousarray(ins["L1_ffn_W_k"].T)
    m["WvT"] = np.ascontiguousarray(ins["L1_ffn_W_v"].T)
    m["mur"] = np.ascontiguousarray(ins["L1_ffn_mu_r"].reshape(8, 128).T)
    m["muk"] = np.ascontiguousarray(ins["L1_ffn_mu_k"].reshape(8, 128).T)
    return m


def _gather(res, cores):
    out = np.empty((4, 4096, 1024), dtype=np.float32)
    for i, c in enumerate(cores):
        b, half = c // 2, c % 2
        out[b, half * 2048:(half + 1) * 2048] = np.asarray(res.results[i]["yT"]).T
    return out


_NC_CACHE = {}


def _get_nc(name, builder):
    if name not in _NC_CACHE:
        _NC_CACHE[name] = builder()
    return _NC_CACHE[name]


def _fused_maps(ins, x, cores):
    consts = host_constants()
    sB = _sharedB(ins)
    sC = _sharedC(ins)
    sD = _sharedD(ins)
    maps = []
    for c in cores:
        b, half = c // 2, c % 2
        m = dict(consts)
        m.update(sB); m.update(sC); m.update(sD)
        m["xT_full"] = _roll_xT(x[b], half)
        m["wq"] = np.ascontiguousarray(ins["L0_attn_Wq"])
        m["wk"] = np.ascontiguousarray(ins["L0_attn_Wk"])
        m["wo"] = np.ascontiguousarray(ins["L0_attn_Wo"])
        m["bo"] = np.ascontiguousarray(ins["L0_attn_bo"].reshape(8, 128).T)
        m["WrT"] = sD["WrT"]
        m["prev_col"] = np.full((64, 1), float(half), np.float32)
        m["prev128"] = np.full((128, 1), float(half), np.float32)
        maps.append(m)
    return maps


def kernel(**inputs):
    ins = inputs
    cores = list(range(8))
    x = np.ascontiguousarray(ins["L0_attn_x"], dtype=np.float32)
    nc = _get_nc("F", build_fused)
    res = run_bass_kernel_spmd(nc, _fused_maps(ins, x, cores), core_ids=cores)
    return _gather(res, cores)


def kernel_multilaunch(**inputs):

    ins = inputs
    consts = host_constants()
    cores = list(range(8))
    x = np.ascontiguousarray(ins["L0_attn_x"], dtype=np.float32)

    ncA = _get_nc("A", build_A)
    res = run_bass_kernel_spmd(ncA, [_hostA(consts, ins, x, c) for c in cores],
                               core_ids=cores)
    x = _gather(res, cores)

    ncB = _get_nc("B", build_B)
    sB = _sharedB(ins)
    maps = []
    for c in cores:
        b, half = c // 2, c % 2
        m = dict(consts); m.update(sB)
        m["xT"] = np.ascontiguousarray(x[b, half * 2048:(half + 1) * 2048].T)
        maps.append(m)
    res = run_bass_kernel_spmd(ncB, maps, core_ids=cores)
    x = _gather(res, cores)

    ncC = _get_nc("C", build_C)
    sC = _sharedC(ins)
    maps = []
    for c in cores:
        b, half = c // 2, c % 2
        own = x[b, half * 2048:(half + 1) * 2048]
        if half == 0:
            full = np.concatenate([np.zeros((3, 1024), np.float32), own], 0)
        else:
            full = np.concatenate([x[b, 2048 - 3:2048], own], 0)
        m = dict(consts); m.update(sC)
        m["xTc"] = np.ascontiguousarray(full.T)
        m["prev_col"] = np.full((64, 1), float(half), np.float32)
        maps.append(m)
    res = run_bass_kernel_spmd(ncC, maps, core_ids=cores)
    x = _gather(res, cores)

    ncD = _get_nc("D", build_D)
    sD = _sharedD(ins)
    maps = []
    for c in cores:
        b, half = c // 2, c % 2
        own = x[b, half * 2048:(half + 1) * 2048]
        if half == 0:
            full = np.concatenate([np.zeros((1, 1024), np.float32), own], 0)
        else:
            full = np.concatenate([x[b, 2048 - 1:2048], own], 0)
        m = dict(consts); m.update(sD)
        m["xTd"] = np.ascontiguousarray(full.T)
        maps.append(m)
    res = run_bass_kernel_spmd(ncD, maps, core_ids=cores)
    return _gather(res, cores)


# version 14 (final)
# speedup vs baseline: 1.2016x; 1.0147x over previous
import numpy as np
import ml_dtypes
from contextlib import ExitStack

import concourse.bass as bass
import concourse.mybir as mybir
from concourse.bass_utils import run_bass_kernel_spmd

F32 = mybir.dt.float32
BF16 = mybir.dt.bfloat16
AF = mybir.ActivationFunctionType
ALU = mybir.AluOpType
AX = mybir.AxisListType.X
BT = ml_dtypes.bfloat16

D = 1024
NT = 8
T = 2048
TF = 4096
CH = 512
H = 16
LN_EPS = 1e-5

CONST_SPEC = [
    ("ident_bf", [128, 128], BF16),
    ("ident2_bf", [128, 64], BF16),
    ("ident_f32", [128, 128], F32),
    ("ones_col_bf", [128, 1], BF16),
    ("ones_row_bf", [1, 128], BF16),
    ("ones_row_f32", [1, 128], F32),
    ("sel_lo", [1, 128], BF16),
    ("sel_hi", [1, 128], BF16),
]


B_CONST_SPEC = [
    ("sel9", [9, 1024], BF16),
]


def host_constants():
    c = {}
    s9 = np.zeros((9, 1024), dtype=BT)
    for e in range(8):
        s9[e + 1, e * 128:(e + 1) * 128] = 1
    c["sel9"] = s9
    c["ident_bf"] = np.eye(128, dtype=BT)
    c["ident2_bf"] = np.tile(np.eye(64, dtype=BT), (2, 1))
    c["ident_f32"] = np.eye(128, dtype=np.float32)
    c["ones_col_bf"] = np.ones((128, 1), dtype=BT)
    c["ones_row_bf"] = np.ones((1, 128), dtype=BT)
    c["ones_row_f32"] = np.ones((1, 128), dtype=np.float32)
    c["sel_lo"] = np.concatenate([np.ones((1, 64), BT), np.zeros((1, 64), BT)], 1)
    c["sel_hi"] = np.concatenate([np.zeros((1, 64), BT), np.ones((1, 64), BT)], 1)
    return c


def declare_constants(nc, names=None):
    return {name: nc.dram_tensor(name, shape, dt, kind="ExternalInput").ap()
            for name, shape, dt in CONST_SPEC if names is None or name in names}


def tile(buf, j, width, sl=None):
    if sl is None:
        return buf[:, j * width:(j + 1) * width]
    a, b = sl
    return buf[:, j * width + a: j * width + b]



def dma_wtiles(g, dst_sb, src_dram, ntile, sem=None):
    C = src_dram.shape[1]
    ins = g.dma_start(out=dst_sb.rearrange("p (j c) -> p j c", j=ntile),
                      in_=src_dram.rearrange("(j p) c -> p j c", p=128))
    if sem is not None:
        ins.then_inc(sem, 16)
    return ins

def build_A(upto=3):
    nc = bass.Bass("TRN2", target_bir_lowering=False)
    caps = declare_constants(nc)
    x_in = nc.dram_tensor("xT_full", [D, TF], F32, kind="ExternalInput").ap()
    wq_in = nc.dram_tensor("wq", [D, D], F32, kind="ExternalInput").ap()
    wk_in = nc.dram_tensor("wk", [D, D], F32, kind="ExternalInput").ap()
    wo_in = nc.dram_tensor("wo", [D, D], F32, kind="ExternalInput").ap()
    bo_in = nc.dram_tensor("bo", [128, 8], F32, kind="ExternalInput").ap()
    y_out = nc.dram_tensor("yT", [D, T], F32, kind="ExternalOutput").ap()
    emit_A(nc, caps, x_in, wq_in, wk_in, wo_in, bo_in, y_out, upto=upto)
    return nc


A_CONSTS = ("ones_col_bf", "ones_row_bf", "sel_lo", "sel_hi")


def emit_A(nc, caps, x_in, wq_in, wk_in, wo_in, bo_in, y_out, upto=3, P=""):
    v_dram = nc.dram_tensor(P + "v_scr", [TF, D], BF16).ap()

    NCF = TF // CH
    NCO = T // CH
    NKT = TF // 128
    NHP = H // 2

    kq_groups_in_chunk = [24 if ci < NCO else 16 for ci in range(NCF)]
    kq_total = sum(kq_groups_in_chunk)

    with ExitStack() as ctx:
        def sb(name, shape, dt):
            return ctx.enter_context(nc.sbuf_tensor(P + name, shape, dt))
        def ps(name, shape, dt):
            return ctx.enter_context(nc.psum_tensor(P + name, shape, dt))
        def sem(name):
            return ctx.enter_context(nc.semaphore(P + name))

        s_lw = sem("s_lw");  s_ww = sem("s_ww");  s_ld = sem("s_ld");  s_xf = sem("s_xf")
        s_sq = sem("s_sq");  s_stats = sem("s_stats"); s_mu = sem("s_mu")
        s_rs = sem("s_rs");  s_pr = sem("s_pr");  s_bc = sem("s_bc")
        s_nm = sem("s_nm");  s_kq = sem("s_kq");  s_ev = sem("s_ev")
        s_vs = sem("s_vs");  s_vh = sem("s_vh")
        s_s = sem("s_s");    s_e = sem("s_e");    s_po = sem("s_po")
        s_rcf = sem("s_rcf"); s_rb = sem("s_rb"); s_fin = sem("s_fin")
        s_wo = sem("s_wo");  s_f3 = sem("s_f3");  s_st = sem("s_st")
        sv_self = sem("sv_self"); ss_self = sem("ss_self")

        xbf = sb("xbf", [128, NT * CH], BF16)
        xln = sb("xln", [128, NT * CH], BF16)
        sq = xln
        lnsm = sb("lnsm", [1, 4 * CH], F32)
        pair_bf = sb("pair_bf", [1, 2 * CH], BF16)
        rbuf = sb("rbuf", [128, 2 * CH], BF16)
        wqk = sb("wqk", [128, 2 * NT * 1024], BF16)
        wo_sb = sb("wo_sb", [128, NT * 1024], BF16)
        bo_sb = sb("bo_sb", [128, 8], F32)
        KT = sb("KT", [128, NT * TF], BF16)
        QT = sb("QT", [128, NT * T], BF16)
        OA = wqk
        Vc = sb("Vc", [128, 2 * 1024], BF16)
        Vhp = sb("Vhp", [128, NKT * 128], BF16)
        Pt = sb("Pt", [128, 4 * CH], BF16)
        rcbf = sb("rcbf", [1, 2 * CH], BF16)
        rcb = sb("rcb", [128, CH], BF16)
        xf32 = sb("xf32", [128, NT * CH], F32)
        eps1 = sb("eps1", [1, 1], F32)
        cb = {n: sb("c_" + n, shp, dt) for n, shp, dt in CONST_SPEC if n in A_CONSTS}

        pB = ps("pB", [128, 2 * CH], F32)
        pX = ps("pX", [128, 2 * CH], F32)
        pO = ps("pO", [128, 2 * CH], F32)
        pD2 = ps("pD2", [128, 2 * CH], F32)
        pS = pO

        block = ctx.enter_context(nc.Block())

        n_lw = len(A_CONSTS) + 1


        @block.sync
        def _(g):
            for name in A_CONSTS:
                g.dma_start(out=cb[name][:, :], in_=caps[name]).then_inc(s_lw, 16)
            g.dma_start(out=bo_sb[:, :], in_=bo_in).then_inc(s_lw, 16)
            ev_c = 0
            for ci in range(NCF):
                base_ev = ev_c
                ev_c += kq_groups_in_chunk[ci]
                for tt in range(4):
                    need = base_ev + 8 + 2 * tt + 2
                    g.wait_ge(s_ev, need)
                    vs_i = 4 * ci + tt
                    if vs_i >= 1:
                        g.wait_ge(s_vs, 16 * vs_i)
                    g.dma_start(out=v_dram[ci * CH + tt * 128: ci * CH + tt * 128 + 128, :],
                                in_=Vc[:, (tt % 2) * 1024:(tt % 2) * 1024 + 1024]
                                ).then_inc(s_vs, 16)
            if upto >= 3:
                for oc in range(NCO):
                    g.wait_ge(s_f3, NT * (oc + 1))
                    if oc >= 1:
                        g.wait_ge(s_st, 16 * oc)
                    g.dma_start(out=y_out.rearrange("(j p) c -> p j c", p=128)[:, :, oc * CH:(oc + 1) * CH],
                                in_=xf32.rearrange("p (j c) -> p j c", j=NT)
                                ).then_inc(s_st, 16)
                g.wait_ge(s_st, 16 * NCO)

        @block.gpsimd
        def _(g):
            dma_wtiles(g, wqk[:, 0:NT * 1024], wq_in, NT, s_ww)
            dma_wtiles(g, wqk[:, NT * 1024:2 * NT * 1024], wk_in, NT, s_ww)
            dma_wtiles(g, wo_sb[:, :], wo_in, NT, s_ww)
            for ci in range(NCF):
                if ci >= 1:
                    g.wait_ge(s_nm, ci)
                for j in range(NT):
                    g.dma_start(out=tile(xbf, j, CH),
                                in_=x_in[128 * j:128 * (j + 1), ci * CH:(ci + 1) * CH]
                                ).then_inc(s_ld, 16)
            if upto >= 2:
                for hp in range(NHP):
                    if hp == 0:
                        g.wait_ge(s_vs, 16 * 4 * NCF)
                    else:
                        g.wait_ge(s_po, 32 * 4 * hp)
                    g.dma_start(out=Vhp.rearrange("p (kt c) -> p kt c", c=128),
                                in_=v_dram[:, hp * 128:(hp + 1) * 128]
                                .rearrange("(kt p) c -> p kt c", p=128)
                                ).then_inc(s_vh, 16)
            if upto >= 3:
                for oc in range(NCO):
                    if oc >= 1:
                        g.wait_ge(s_st, 16 * oc)
                    for j in range(NT):
                        g.dma_start(out=tile(xf32, j, CH),
                                    in_=x_in[128 * j:128 * (j + 1), oc * CH:(oc + 1) * CH]
                                    ).then_inc(s_xf, 16)
            elif upto == 2:
                g.wait_ge(s_fin, NHP * NCO)
                g.dma_start(out=y_out.rearrange("(j p) c -> p j c", p=128),
                            in_=OA.rearrange("p (j c) -> p j c", j=NT)[:, 0:NT, 0:T]
                            ).then_inc(s_xf, 16)
                g.wait_ge(s_xf, 16)
            elif upto == 1:
                g.wait_ge(s_ev, kq_total)
                g.dma_start(out=y_out.rearrange("(j p) c -> p j c", p=128),
                            in_=QT.rearrange("p (j c) -> p j c", j=NT)).then_inc(s_xf, 16)
                g.wait_ge(s_xf, 16)

        @block.tensor
        def _(t):
            t.wait_ge(s_lw, 16 * n_lw)
            t.wait_ge(s_ww, 48)
            gq = 0
            for ci in range(NCF):
                t.wait_ge(s_ld, 16 * NT * (ci + 1))
                t.wait_ge(s_sq, ci + 1)
                if ci >= 1:
                    t.wait_ge(s_mu, ci)
                for j in range(NT):
                    t.matmul(pS[0:1, 0:CH], cb["ones_col_bf"][:, :], tile(xbf, j, CH),
                             start=(j == 0), stop=(j == NT - 1))
                for j in range(NT):
                    ins = t.matmul(pS[0:1, CH:2 * CH], cb["ones_col_bf"][:, :],
                                   tile(sq, j, CH), start=(j == 0), stop=(j == NT - 1))
                ins.then_inc(s_stats, 1)
                t.wait_ge(s_pr, ci + 1)
                if ci >= 1:
                    t.wait_ge(s_nm, ci)
                t.matmul(pB[:, 0:CH], cb["ones_row_bf"][0:1, :], pair_bf[0:1, 0:CH],
                         start=True, stop=True)
                t.matmul(pB[:, CH:2 * CH], cb["ones_row_bf"][0:1, :], pair_bf[0:1, CH:2 * CH],
                         start=True, stop=True).then_inc(s_bc, 1)
                t.wait_ge(s_nm, ci + 1)
                for gi in range(kq_groups_in_chunk[ci]):
                    gq += 1
                    if gq > 2:
                        t.wait_ge(s_ev, gq - 2)
                    slot = pX[:, ((gq - 1) % 2) * CH: ((gq - 1) % 2) * CH + CH]
                    if gi < 8:
                        ot = gi
                        for j in range(NT):
                            ins = t.matmul(slot,
                                           wqk[:, NT * 1024 + j * 1024 + ot * 128:
                                               NT * 1024 + j * 1024 + ot * 128 + 128],
                                           tile(xln, j, CH),
                                           start=(j == 0), stop=(j == NT - 1))
                    elif gi < 16:
                        tt, half = (gi - 8) // 2, (gi - 8) % 2
                        for j in range(NT):
                            ins = t.matmul(slot,
                                           xln[:, j * CH + tt * 128: j * CH + tt * 128 + 128],
                                           wqk[:, NT * 1024 + j * 1024 + half * 512:
                                               NT * 1024 + j * 1024 + half * 512 + 512],
                                           start=(j == 0), stop=(j == NT - 1))
                    else:
                        ot = gi - 16
                        for j in range(NT):
                            ins = t.matmul(slot,
                                           wqk[:, j * 1024 + ot * 128: j * 1024 + ot * 128 + 128],
                                           tile(xln, j, CH),
                                           start=(j == 0), stop=(j == NT - 1))
                    ins.then_inc(s_kq, 1)
            assert gq == kq_total

            if upto < 2:
                return
            t.wait_ge(s_ev, kq_total)
            t.wait_ge(s_mu, NCF)
            NIT = NHP * NCO * NKT

            def s_mm(it2):
                hp2, r2 = divmod(it2 - 1, NCO * NKT)
                qc2, kt2 = divmod(r2, NKT)
                if it2 > 2:
                    t.wait_ge(s_e, it2 - 2)
                slot2 = pB if (it2 % 2 == 1) else pX
                kcol = hp2 * TF + kt2 * 128
                qcol = hp2 * T + qc2 * CH
                t.matmul(slot2[:, 0:CH], KT[0:64, kcol:kcol + 128],
                         QT[0:64, qcol:qcol + CH], start=True, stop=True,
                         tile_position=(0, 0))
                t.matmul(slot2[:, CH:2 * CH], KT[64:128, kcol:kcol + 128],
                         QT[64:128, qcol:qcol + CH], start=True, stop=True,
                         tile_position=(64, 0)).then_inc(s_s, 1)

            it = 0
            s_mm(1)
            for hp in range(NHP):
                t.wait_ge(s_vh, 16 * (hp + 1))
                for qc in range(NCO):
                    grp = 4 * hp + qc
                    t.wait_ge(s_fin, grp)
                    for kt in range(NKT):
                        it += 1
                        if it < NIT:
                            s_mm(it + 1)
                        pts = Pt[:, (it % 2) * 1024:(it % 2) * 1024 + 1024]
                        t.wait_ge(s_e, it)
                        t.matmul(pD2[0:1, 0:CH], cb["ones_col_bf"][:, :], pts[:, 0:CH],
                                 start=(kt == 0), stop=(kt == NKT - 1))
                        t.matmul(pD2[0:1, CH:2 * CH], cb["ones_col_bf"][:, :],
                                 pts[:, CH:2 * CH], start=(kt == 0), stop=(kt == NKT - 1))
                        t.matmul(pO[0:64, 0:CH], Vhp[:, kt * 128:kt * 128 + 64],
                                 pts[:, 0:CH], start=(kt == 0), stop=(kt == NKT - 1),
                                 tile_position=(0, 0))
                        t.matmul(pO[64:128, CH:2 * CH], Vhp[:, kt * 128 + 64:kt * 128 + 128],
                                 pts[:, CH:2 * CH], start=(kt == 0), stop=(kt == NKT - 1),
                                 tile_position=(0, 64)).then_inc(s_po, 1)
                    t.wait_ge(s_rcf, grp + 1)
                    t.matmul(pD2[:, 0:CH], cb["sel_lo"][0:1, :], rcbf[0:1, 0:CH],
                             start=True, stop=True)
                    t.matmul(pD2[:, CH:2 * CH], cb["sel_hi"][0:1, :], rcbf[0:1, CH:2 * CH],
                             start=True, stop=True).then_inc(s_rb, 1)
            assert it == NHP * NCO * NKT

            if upto < 3:
                return
            t.wait_ge(s_fin, NHP * NCO)
            for oc in range(NCO):
                for ot in range(NT):
                    g3 = NT * oc + ot
                    if g3 >= 2:
                        t.wait_ge(s_f3, g3 - 1)
                    slot = pX[:, (g3 % 2) * CH:(g3 % 2) * CH + CH]
                    for j in range(NT):
                        ins = t.matmul(slot,
                                       wo_sb[:, j * 1024 + ot * 128: j * 1024 + ot * 128 + 128],
                                       tile(OA, j, T, (oc * CH, (oc + 1) * CH)),
                                       start=(j == 0), stop=(j == NT - 1))
                    ins.then_inc(s_wo, 1)

        @block.scalar
        def _(s):
            ssn = [0]
            def sbump(ins):
                ins.then_inc(ss_self, 1); ssn[0] += 1
            def sfence():
                s.wait_ge(ss_self, ssn[0])
            s.wait_ge(s_lw, 16 * n_lw)
            s.wait_ge(s_ww, 48)
            gq = 0
            for ci in range(NCF):
                s.wait_ge(s_ld, 16 * NT * (ci + 1))
                if ci >= 1:
                    s.wait_ge(s_kq, sum(kq_groups_in_chunk[:ci]))
                for j in range(NT):
                    ins = s.activation(tile(sq, j, CH), tile(xbf, j, CH), AF.Square)
                ins.then_inc(s_sq, 1)
                s.wait_ge(s_mu, ci + 1)
                if ci >= 1:
                    s.wait_ge(s_pr, ci)
                var = lnsm[0:1, 2 * CH:3 * CH]
                rstd = lnsm[0:1, 3 * CH:4 * CH]
                sbump(s.activation(rstd, var, AF.Ln, bias=eps1[0:1, 0:1]))
                sfence()
                s.activation(rstd, rstd, AF.Exp, scale=-0.5).then_inc(s_rs, 1)
                for gi in range(kq_groups_in_chunk[ci]):
                    gq += 1
                    s.wait_ge(s_kq, gq)
                    slot = pX[:, ((gq - 1) % 2) * CH:((gq - 1) % 2) * CH + CH]
                    if gi < 8:
                        dst = KT[:, gi * TF + ci * CH: gi * TF + ci * CH + CH]
                        ins = s.activation(dst, slot, AF.Copy)
                    elif gi < 16:
                        tt, half = (gi - 8) // 2, (gi - 8) % 2
                        vs_i = 4 * ci + tt
                        if vs_i >= 2 and half == 0:
                            s.wait_ge(s_vs, 16 * (vs_i - 1))
                        dst = Vc[:, (tt % 2) * 1024 + half * 512:
                                 (tt % 2) * 1024 + half * 512 + 512]
                        ins = s.activation(dst, slot, AF.Copy)
                    else:
                        ot = gi - 16
                        dst = QT[:, ot * T + ci * CH: ot * T + ci * CH + CH]
                        ins = s.activation(dst, slot, AF.Copy, scale=float(2.0 ** -5))
                    ins.then_inc(s_ev, 1)
            if upto < 2:
                return
            it = 0
            for hp in range(NHP):
                for qc in range(NCO):
                    for kt in range(NKT):
                        it += 1
                        s.wait_ge(s_s, it)
                        if it >= 3:
                            s.wait_ge(s_po, it - 2)
                        slot = pB if (it % 2 == 1) else pX
                        s.activation(Pt[:, (it % 2) * 1024:(it % 2) * 1024 + 1024],
                                     slot[:, :], AF.Exp).then_inc(s_e, 1)

        @block.vector
        def _(v):
            svn = [0]
            def vbump(ins):
                ins.then_inc(sv_self, 1); svn[0] += 1
            def vfence():
                v.wait_ge(sv_self, svn[0])
            v.wait_ge(s_lw, 16 * n_lw)
            v.wait_ge(s_ww, 48)
            v.memset(eps1[0:1, :], LN_EPS)
            for ci in range(NCF):
                v.wait_ge(s_stats, ci + 1)
                if ci >= 1:
                    v.wait_ge(s_rs, ci)
                mu = lnsm[0:1, 0:CH]
                ex2 = lnsm[0:1, CH:2 * CH]
                var = lnsm[0:1, 2 * CH:3 * CH]
                vbump(v.tensor_scalar(mu, pS[0:1, 0:CH], 1.0 / D, None, ALU.mult))
                vbump(v.tensor_scalar(ex2, pS[0:1, CH:2 * CH], 1.0 / D, None, ALU.mult))
                vfence()
                vbump(v.tensor_tensor(var, mu, mu, ALU.mult))
                vfence()
                v.tensor_tensor(var, ex2, var, ALU.subtract).then_inc(s_mu, 1)
                v.wait_ge(s_rs, ci + 1)
                rstd = lnsm[0:1, 3 * CH:4 * CH]
                nu = lnsm[0:1, CH:2 * CH]
                vbump(v.tensor_tensor(nu, mu, rstd, ALU.mult))
                vbump(v.tensor_copy(pair_bf[0:1, 0:CH], rstd))
                vfence()
                v.tensor_copy(pair_bf[0:1, CH:2 * CH], nu).then_inc(s_pr, 1)
                v.wait_ge(s_bc, ci + 1)
                if ci >= 1:
                    v.wait_ge(s_kq, sum(kq_groups_in_chunk[:ci]))
                vbump(v.tensor_copy(rbuf[:, :], pB[:, :]))
                vfence()
                for j in range(NT):
                    vbump(v.tensor_tensor(tile(xln, j, CH), tile(xbf, j, CH),
                                          rbuf[:, 0:CH], ALU.mult))
                vfence()
                for j in range(NT):
                    o = tile(xln, j, CH)
                    ins = v.tensor_tensor(o, o, rbuf[:, CH:2 * CH], ALU.subtract)
                ins.then_inc(s_nm, 1)
            if upto < 2:
                return
            for hp in range(NHP):
                for qc in range(NCO):
                    grp = 4 * hp + qc
                    v.wait_ge(s_po, 32 * (grp + 1))
                    vbump(v.reciprocal(lnsm[0:1, 0:CH], pD2[0:1, 0:CH]))
                    vbump(v.reciprocal(lnsm[0:1, CH:2 * CH], pD2[0:1, CH:2 * CH]))
                    vfence()
                    v.tensor_copy(rcbf[0:1, :], lnsm[0:1, 0:2 * CH]).then_inc(s_rcf, 1)
                    v.wait_ge(s_rb, grp + 1)
                    vbump(v.tensor_copy(rcb[0:64, :], pD2[0:64, 0:CH]))
                    vbump(v.tensor_copy(rcb[64:128, :], pD2[64:128, CH:2 * CH]))
                    vfence()
                    dst = tile(OA, hp, T, (qc * CH, (qc + 1) * CH))
                    v.tensor_tensor(dst[0:64, :], pO[0:64, 0:CH], rcb[0:64, :], ALU.mult)
                    v.tensor_tensor(dst[64:128, :], pO[64:128, CH:2 * CH], rcb[64:128, :],
                                    ALU.mult).then_inc(s_fin, 1)
            if upto < 3:
                return
            for oc in range(NCO):
                v.wait_ge(s_xf, 16 * NT * (oc + 1))
                for ot in range(NT):
                    g3 = NT * oc + ot
                    v.wait_ge(s_wo, g3 + 1)
                    slot = pX[:, (g3 % 2) * CH:(g3 % 2) * CH + CH]
                    dst = tile(xf32, ot, CH)
                    vbump(v.tensor_tensor(dst, dst, slot, ALU.add))
                    vfence()
                    v.tensor_scalar(dst, dst, bo_sb[:, ot:ot + 1], None,
                                    ALU.add).then_inc(s_f3, 1)



B_CONSTS = ("ones_col_bf", "ones_row_bf", "ident_bf")


def build_B():
    nc = bass.Bass("TRN2", target_bir_lowering=False)
    caps = declare_constants(nc, B_CONSTS)
    x_in = nc.dram_tensor("xT", [D, T], F32, kind="ExternalInput").ap()
    w1_in = nc.dram_tensor("w1T_all", [9, D, 4096], F32, kind="ExternalInput").ap()
    w2_in = nc.dram_tensor("w2T_all", [9, 8, 4096, 128], F32, kind="ExternalInput").ap()
    b1_in = nc.dram_tensor("b1_all", [9, 128, 32], F32, kind="ExternalInput").ap()
    b2_in = nc.dram_tensor("b2_all", [9, 1024], F32, kind="ExternalInput").ap()
    rw_in = nc.dram_tensor("rwT", [D, 8], F32, kind="ExternalInput").ap()
    rb_in = nc.dram_tensor("rb", [1, 8], F32, kind="ExternalInput").ap()
    sel9_in = nc.dram_tensor("sel9", [9, 1024], BF16, kind="ExternalInput").ap()
    y_out = nc.dram_tensor("yT", [D, T], F32, kind="ExternalOutput").ap()
    emit_B(nc, caps, x_in, w1_in, w2_in, b1_in, b2_in, rw_in, rb_in, sel9_in, y_out)
    return nc


def emit_B(nc, caps, x_in, w1_in, w2_in, b1_in, b2_in, rw_in, rb_in, sel9_in, y_out,
           P="B_"):
    gdram = nc.dram_tensor(P + "g_scr", [9, T], BF16).ap()

    NCH = T // CH
    NTT = T // 128
    NE = 9
    NQ8 = 8
    N_EV2 = 2 * NE * 8 * 2
    N_GTF = NTT * 8

    with ExitStack() as ctx:
        def sb(name, shape, dt):
            return ctx.enter_context(nc.sbuf_tensor(P + name, shape, dt))
        def ps(name, shape, dt):
            return ctx.enter_context(nc.psum_tensor(P + name, shape, dt))
        def sem(name):
            return ctx.enter_context(nc.semaphore(P + name))

        s_lw = sem("s_lw"); s_ww = sem("s_ww"); s_ld = sem("s_ld")
        s_sq = sem("s_sq"); s_stats = sem("s_stats"); s_mu = sem("s_mu")
        s_rs = sem("s_rs"); s_pr = sem("s_pr"); s_bc = sem("s_bc"); s_nm = sem("s_nm")
        s_rt = sem("s_rt"); s_gv = sem("s_gv"); s_gmm = sem("s_gmm"); s_gtf = sem("s_gtf")
        s_g9 = sem("s_g9")
        s_wst = sem("s_wst"); s_hm = sem("s_hm"); s_ge = sem("s_ge"); s_gt = sem("s_gt")
        s_gb = sem("s_gb"); s_gbe = sem("s_gbe")
        s_p2 = sem("s_p2"); s_ev2 = sem("s_ev2"); s_st = sem("s_st")
        s_vc = sem("s_vc"); s_rv1 = sem("s_rv1"); s_se = sem("s_se")
        s_gsd = sem("s_gsd"); s_g9m = sem("s_g9m")
        sv_self = sem("sv_self"); ss_self = sem("ss_self")

        X = sb("Xr", [128, NT * T], F32)
        xln = sb("xlnB", [128, NT * T], BF16)
        h_sb = sb("h_sb", [128, 32 * 1024], BF16)
        xbf = h_sb
        w1b = sb("w1b", [128, 2 * NT * 256], BF16)
        w2b = sb("w2b", [128, 2 * 32 * 128], BF16)
        lnsm = sb("lnsmB", [1, 2 * CH], F32)
        pair_bf = sb("pairB", [1, 2 * CH], BF16)
        rbuf = sb("rbufB", [128, 2 * CH], BF16)
        rwT_sb = sb("rwT_sb", [128, 64], BF16)
        rb_sb = sb("rb_sb", [1, 8], BF16)
        b1_sb = sb("b1_sb", [128, NE * 32], F32)
        b2_bf = sb("b2_bf", [9, 1024], BF16)
        gate9 = sb("gate9", [9, T], BF16)
        sel9 = sb("sel9_sb", [9, 1024], BF16)
        gstg = w1b
        gb_sb = rbuf
        lg = sb("lg", [128, 40], F32)
        gates_bf = sb("gates_bf", [128, 8], BF16)
        eps1 = sb("eps1B", [1, 1], F32)
        cb = {n: sb("cB_" + n, shp, dt) for n, shp, dt in CONST_SPEC if n in B_CONSTS}

        pH = ps("pH", [128, 2 * CH], F32)
        pO2 = ps("pO2", [128, 2 * CH], F32)
        pB = ps("pBB", [128, 2 * CH], F32)
        pS = pB

        block = ctx.enter_context(nc.Block())

        n_lw = len(B_CONSTS) + 2

        @block.sync
        def _(g):
            for name in B_CONSTS:
                g.dma_start(out=cb[name][:, :], in_=caps[name]).then_inc(s_lw, 16)
            g.dma_start(out=b1_sb.rearrange("p (e c) -> p e c", e=NE),
                        in_=b1_in.rearrange("e p c -> p e c")).then_inc(s_lw, 16)
            g.dma_start(out=sel9[:, :], in_=sel9_in).then_inc(s_lw, 16)
            for tt in range(NTT):
                g.wait_ge(s_gtf, 8 * (tt + 1))
                if tt >= 1:
                    g.wait_ge(s_gsd, 16 * tt)
                g.dma_start(out=gdram[1:9, tt * 128:(tt + 1) * 128],
                            in_=gstg[0:1, (tt % 2) * 1024:(tt % 2) * 1024 + 1024]
                            ).then_inc(s_gsd, 16)
            g.wait_ge(s_ev2, N_EV2)
            for oc in range(NCH):
                if oc >= 1:
                    g.wait_ge(s_st, 16 * oc)
                g.dma_start(out=y_out.rearrange("(j p) c -> p j c", p=128)[:, :, oc * CH:(oc + 1) * CH],
                            in_=X.rearrange("p (j c) -> p j c", j=NT)[:, :, oc * CH:(oc + 1) * CH]
                            ).then_inc(s_st, 16)
            g.wait_ge(s_st, 16 * NCH)

        @block.gpsimd
        def _(g):
            dma_wtiles(g, rwT_sb[:, :], rw_in, NT, s_ww)
            g.dma_start(out=b2_bf[:, :], in_=b2_in).then_inc(s_ww, 16)
            g.dma_start(out=rb_sb[:, :], in_=rb_in).then_inc(s_ww, 16)
            for j in range(NT):
                g.dma_start(out=tile(X, j, T),
                            in_=x_in[128 * j:128 * (j + 1), :]).then_inc(s_ld, 16)
            g.wait_ge(s_gsd, 16 * NTT)
            g.dma_start(out=gate9[1:9, :], in_=gdram[1:9, :]).then_inc(s_g9, 16)
            wi = 0; we = 0; wd = 0
            for half in range(2):
                for e in range(NE):
                    for q16 in range(16):
                        wi += 1; we += 1
                        if wi >= 2:
                            g.wait_ge(s_wst, 16 * (wi - 1))
                        if we >= 3:
                            g.wait_ge(s_hm, 4 * (we - 2))
                        g.dma_start(out=w1b.rearrange("p (s j c) -> p s j c", s=2, j=NT)
                                    [:, (we - 1) % 2, :, :],
                                    in_=w1_in[e].rearrange("(j p) c -> p j c", p=128)
                                    [:, :, q16 * 256:(q16 + 1) * 256]).then_inc(s_wst, 16)
                    for d in range(8):
                        wi += 1; wd += 1
                        g.wait_ge(s_wst, 16 * (wi - 1))
                        if wd >= 3:
                            g.wait_ge(s_p2, 2 * (wd - 2))
                        g.dma_start(out=w2b.rearrange("p (s f c) -> p s f c", s=2, f=32)
                                    [:, (wd - 1) % 2, :, :],
                                    in_=w2_in[e, d].rearrange("(f p) c -> p f c", p=128)
                                    ).then_inc(s_wst, 16)

        @block.tensor
        def _(t):
            t.wait_ge(s_lw, 16 * n_lw)
            t.wait_ge(s_ww, 48)
            t.wait_ge(s_ld, 16 * NT)
            for ci in range(NCH):
                t.wait_ge(s_sq, ci + 1)
                if ci >= 1:
                    t.wait_ge(s_mu, ci)
                for j in range(NT):
                    t.matmul(pS[0:1, 0:CH], cb["ones_col_bf"][:, :],
                             xbf[:, j * CH:(j + 1) * CH], start=(j == 0), stop=(j == NT - 1))
                for j in range(NT):
                    ins = t.matmul(pS[0:1, CH:2 * CH], cb["ones_col_bf"][:, :],
                                   xbf[:, NT * CH + j * CH: NT * CH + (j + 1) * CH],
                                   start=(j == 0), stop=(j == NT - 1))
                ins.then_inc(s_stats, 1)
                t.wait_ge(s_pr, ci + 1)
                if ci >= 1:
                    t.wait_ge(s_nm, ci)
                t.matmul(pB[:, 0:CH], cb["ones_row_bf"][0:1, :], pair_bf[0:1, 0:CH],
                         start=True, stop=True)
                t.matmul(pB[:, CH:2 * CH], cb["ones_row_bf"][0:1, :], pair_bf[0:1, CH:2 * CH],
                         start=True, stop=True).then_inc(s_bc, 1)
            t.wait_ge(s_nm, NCH)
            for tt in range(NTT):
                if tt >= 1:
                    t.wait_ge(s_gv, tt)
                for j in range(NT):
                    t.matmul(pB[:, 0:8], xln[:, j * T + tt * 128: j * T + tt * 128 + 128],
                             rwT_sb[:, j * 8:(j + 1) * 8], start=(j == 0), stop=False)
                t.matmul(pB[:, 0:8], cb["ones_row_bf"][0:1, :],
                         rb_sb[0:1, :], start=False, stop=True).then_inc(s_rt, 1)
                t.wait_ge(s_gv, tt + 1)
                t.wait_ge(s_gtf, 8 * tt)
                for e in range(8):
                    ins = t.matmul(pO2[0:1, e * 128:(e + 1) * 128],
                                   gates_bf[:, e:e + 1], cb["ident_bf"][:, :],
                                   start=True, stop=True)
                ins.then_inc(s_gmm, 1)
            t.wait_ge(s_g9, 16)
            t.wait_ge(s_g9m, 1)
            hk = 0; we = 0; wd = 0; p2g = 0; gbn = 0
            for half in range(2):
                for e in range(NE):
                    eg = half * NE + e
                    if e > 0:
                        for sub in range(2):
                            gbn += 1
                            if gbn > 2:
                                t.wait_ge(s_gbe, gbn - 2)
                            t.matmul(pB[:, sub * CH:(sub + 1) * CH],
                                     sel9[:, (e - 1) * 128:e * 128],
                                     gate9[:, half * 1024 + sub * CH:
                                           half * 1024 + (sub + 1) * CH],
                                     start=True, stop=True).then_inc(s_gb, 1)
                    for q16 in range(16):
                        we += 1
                        t.wait_ge(s_wst, 16 * (we + wd))
                        for fft in range(2):
                            for sub in range(2):
                                hk += 1
                                if hk > 2:
                                    t.wait_ge(s_ge, hk - 2)
                                slot = pH[:, ((hk - 1) % 2) * CH:((hk - 1) % 2) * CH + CH]
                                for j in range(NT):
                                    ins = t.matmul(
                                        slot,
                                        w1b[:, ((we - 1) % 2) * NT * 256 + j * 256 + fft * 128:
                                            ((we - 1) % 2) * NT * 256 + j * 256 + fft * 128 + 128],
                                        xln[:, j * T + half * 1024 + sub * CH:
                                            j * T + half * 1024 + (sub + 1) * CH],
                                        start=(j == 0), stop=(j == NT - 1))
                                ins.then_inc(s_hm, 1)
                    for d in range(8):
                        wd += 1
                        t.wait_ge(s_wst, 16 * (we + wd))
                        for sub in range(2):
                            p2g += 1
                            if p2g > 2:
                                t.wait_ge(s_ev2, p2g - 2)
                            if e > 0:
                                t.wait_ge(s_gt, 64 * (half * 8 + e - 1) + 64)
                            else:
                                t.wait_ge(s_ge, 64 * eg + 64)
                            slot = pO2[:, ((p2g - 1) % 2) * CH:((p2g - 1) % 2) * CH + CH]
                            for fft in range(32):
                                last = (fft == 31) and (e != 0)
                                ins = t.matmul(
                                    slot,
                                    w2b[:, ((wd - 1) % 2) * 32 * 128 + fft * 128:
                                        ((wd - 1) % 2) * 32 * 128 + fft * 128 + 128],
                                    h_sb[:, fft * 1024 + sub * CH: fft * 1024 + (sub + 1) * CH],
                                    start=(fft == 0), stop=last)
                            if e == 0:
                                ins = t.matmul(slot, b2_bf[:, d * 128:(d + 1) * 128],
                                               gate9[:, half * 1024 + sub * CH:
                                                     half * 1024 + (sub + 1) * CH],
                                               start=False, stop=True)
                            ins.then_inc(s_p2, 1)

        @block.scalar
        def _(s):
            ssn = [0]
            def sbump(ins):
                ins.then_inc(ss_self, 1); ssn[0] += 1
            def sfence():
                s.wait_ge(ss_self, ssn[0])
            s.wait_ge(s_lw, 16 * n_lw)
            s.wait_ge(s_ld, 16 * NT)
            for ci in range(NCH):
                s.wait_ge(s_vc, ci + 1)
                for j in range(NT):
                    ins = s.activation(xbf[:, NT * CH + j * CH: NT * CH + (j + 1) * CH],
                                       xbf[:, j * CH:(j + 1) * CH], AF.Square)
                ins.then_inc(s_sq, 1)
                s.wait_ge(s_mu, ci + 1)
                if ci >= 1:
                    s.wait_ge(s_pr, ci)
                var = lnsm[0:1, CH:2 * CH]
                rstd = pH[0:1, 0:CH]
                sbump(s.activation(rstd, var, AF.Ln, bias=eps1[0:1, 0:1]))
                sfence()
                s.activation(rstd, rstd, AF.Exp, scale=-0.5).then_inc(s_rs, 1)
            for tt in range(NTT):
                s.wait_ge(s_rv1, tt + 1)
                s.activation(lg[:, 8:16], lg[:, 0:8], AF.Exp).then_inc(s_se, 1)
            hk = 0
            for half in range(2):
                for e in range(NE):
                    eg = half * NE + e
                    if eg >= 1:
                        s.wait_ge(s_p2, 16 * eg)
                    for q16 in range(16):
                        for fft in range(2):
                            for sub in range(2):
                                hk += 1
                                s.wait_ge(s_hm, hk)
                                fg = q16 * 2 + fft
                                slot = pH[:, ((hk - 1) % 2) * CH:((hk - 1) % 2) * CH + CH]
                                s.activation(h_sb[:, fg * 1024 + sub * CH:
                                                  fg * 1024 + (sub + 1) * CH],
                                             slot, AF.Gelu_apprx_tanh,
                                             bias=b1_sb[:, e * 32 + fg: e * 32 + fg + 1]
                                             ).then_inc(s_ge, 1)

        @block.vector
        def _(v):
            svn = [0]
            def vbump(ins):
                ins.then_inc(sv_self, 1); svn[0] += 1
            def vfence():
                v.wait_ge(sv_self, svn[0])
            v.wait_ge(s_lw, 16 * n_lw)
            v.wait_ge(s_ld, 16 * NT)
            v.memset(eps1[0:1, :], LN_EPS)
            for ci in range(NCH):
                if ci >= 1:
                    v.wait_ge(s_sq, ci)
                    v.wait_ge(s_stats, ci)
                for j in range(NT):
                    ins = v.tensor_copy(xbf[:, j * CH:(j + 1) * CH],
                                        X[:, j * T + ci * CH: j * T + (ci + 1) * CH])
                ins.then_inc(s_vc, 1)
                v.wait_ge(s_stats, ci + 1)
                if ci >= 1:
                    v.wait_ge(s_rs, ci)
                mu = lnsm[0:1, 0:CH]
                ex2 = lnsm[0:1, CH:2 * CH]
                mumu = pH[0:1, 0:CH]
                vbump(v.tensor_scalar(mu, pS[0:1, 0:CH], 1.0 / D, None, ALU.mult))
                vbump(v.tensor_scalar(ex2, pS[0:1, CH:2 * CH], 1.0 / D, None, ALU.mult))
                vfence()
                vbump(v.tensor_tensor(mumu, mu, mu, ALU.mult))
                vfence()
                v.tensor_tensor(ex2, ex2, mumu, ALU.subtract).then_inc(s_mu, 1)
                v.wait_ge(s_rs, ci + 1)
                rstd = pH[0:1, 0:CH]
                nu = lnsm[0:1, CH:2 * CH]
                vbump(v.tensor_copy(pair_bf[0:1, 0:CH], rstd))
                vfence()
                vbump(v.tensor_tensor(nu, mu, rstd, ALU.mult))
                vfence()
                v.tensor_copy(pair_bf[0:1, CH:2 * CH], nu).then_inc(s_pr, 1)
                v.wait_ge(s_bc, ci + 1)
                vbump(v.tensor_copy(rbuf[:, :], pB[:, :]))
                vfence()
                for j in range(NT):
                    vbump(v.tensor_tensor(xln[:, j * T + ci * CH: j * T + (ci + 1) * CH],
                                          xbf[:, j * CH:(j + 1) * CH],
                                          rbuf[:, 0:CH], ALU.mult))
                vfence()
                for j in range(NT):
                    o = xln[:, j * T + ci * CH: j * T + (ci + 1) * CH]
                    ins = v.tensor_tensor(o, o, rbuf[:, CH:2 * CH], ALU.subtract)
                ins.then_inc(s_nm, 1)
            for tt in range(NTT):
                v.wait_ge(s_rt, tt + 1)
                vbump(v.tensor_copy(lg[:, 0:8], pB[:, 0:8]))
                vfence()
                v.tensor_copy(lg[:, 32:33], lg[:, 0:1]).then_inc(s_rv1, 1)
                v.wait_ge(s_se, tt + 1)
                pe = lg[:, 8:16]
                s1 = lg[:, 16:17]; rc1 = lg[:, 17:18]
                p = lg[:, 18:26]
                m1 = lg[:, 26:27]; cnt = lg[:, 27:28]; m2r = lg[:, 28:29]
                c2 = lg[:, 29:30]; m2 = lg[:, 30:31]; den = lg[:, 31:32]
                eq = lg[:, 32:40]
                vbump(v.reduce_sum(s1, pe, axis=AX))
                vfence()
                vbump(v.reciprocal(rc1, s1))
                vfence()
                vbump(v.tensor_scalar(p, pe, rc1, None, ALU.mult))
                vfence()
                vbump(v.reduce_max(m1, p, axis=AX))
                vfence()
                vbump(v.tensor_scalar(eq, p, m1, None, ALU.is_ge))
                vfence()
                vbump(v.reduce_sum(cnt, eq, axis=AX))
                vbump(v.tensor_scalar(eq, eq, -1e30, None, ALU.mult))
                vfence()
                vbump(v.tensor_tensor(eq, p, eq, ALU.add))
                vfence()
                vbump(v.reduce_max(m2r, eq, axis=AX))
                vfence()
                vbump(v.tensor_scalar(c2, cnt, 2.0, None, ALU.is_ge))
                vfence()
                vbump(v.tensor_tensor(m2, c2, m1, ALU.mult))
                vbump(v.tensor_scalar(c2, c2, -1.0, 1.0, ALU.mult, ALU.add))
                vfence()
                vbump(v.tensor_tensor(c2, c2, m2r, ALU.mult))
                vfence()
                vbump(v.tensor_tensor(m2, m2, c2, ALU.add))
                vfence()
                vbump(v.tensor_tensor(den, m1, m2, ALU.add))
                vbump(v.tensor_scalar(eq, p, m2, None, ALU.is_ge))
                vfence()
                vbump(v.reciprocal(den, den))
                vbump(v.tensor_tensor(eq, p, eq, ALU.mult))
                vfence()
                v.tensor_scalar(gates_bf[:, :], eq, den, None, ALU.mult).then_inc(s_gv, 1)
                v.wait_ge(s_gmm, tt + 1)
                if tt >= 2:
                    v.wait_ge(s_gsd, 16 * (tt - 1))
                for e in range(8):
                    ins = v.tensor_copy(gstg[0:1, (tt % 2) * 1024 + e * 128:
                                            (tt % 2) * 1024 + (e + 1) * 128],
                                        pO2[0:1, e * 128:(e + 1) * 128])
                    ins.then_inc(s_gtf, 1)
            v.wait_ge(s_g9, 16)
            v.memset(gate9[0:1, :], 1.0).then_inc(s_g9m, 1)
            hk = 0; p2g = 0; gbn = 0; gtn = 0
            for half in range(2):
                for e in range(NE):
                    eg = half * NE + e
                    if e > 0:
                        for sub in range(2):
                            gbn += 1
                            v.wait_ge(s_gb, gbn)
                            v.tensor_copy(gb_sb[:, sub * CH:(sub + 1) * CH],
                                          pB[:, sub * CH:(sub + 1) * CH]).then_inc(s_gbe, 1)
                        v.wait_ge(s_gbe, gbn)
                    for q16 in range(16):
                        for fft in range(2):
                            for sub in range(2):
                                hk += 1
                                if e > 0:
                                    v.wait_ge(s_ge, hk)
                                    fg = q16 * 2 + fft
                                    hsl = h_sb[:, fg * 1024 + sub * CH: fg * 1024 + (sub + 1) * CH]
                                    gtn += 1
                                    v.tensor_tensor(hsl, hsl, gb_sb[:, sub * CH:(sub + 1) * CH],
                                                    ALU.mult).then_inc(s_gt, 1)
                    for d in range(8):
                        for sub in range(2):
                            p2g += 1
                            v.wait_ge(s_p2, p2g)
                            slot = pO2[:, ((p2g - 1) % 2) * CH:((p2g - 1) % 2) * CH + CH]
                            dst = X[:, d * T + half * 1024 + sub * CH:
                                    d * T + half * 1024 + (sub + 1) * CH]
                            v.tensor_tensor(dst, dst, slot, ALU.add).then_inc(s_ev2, 1)



DBG_C = 0
C_CONSTS = ("ones_col_bf", "ones_row_bf", "ident_bf", "ident2_bf", "ones_row_f32",
            "ident_f32")

PROJ_COLS = 4384
NCVT = 18
NH = 32
NSC = 16


def build_C(n_cores=8, upto=9):
    nc = bass.Bass("TRN2", target_bir_lowering=False)
    caps = declare_constants(nc, C_CONSTS)
    x_in = nc.dram_tensor("xTc", [D, 2051], F32, kind="ExternalInput").ap()
    win_in = nc.dram_tensor("W_in_p", [D, PROJ_COLS], F32, kind="ExternalInput").ap()
    wout_in = nc.dram_tensor("W_out", [2048, D], F32, kind="ExternalInput").ap()
    convw_in = nc.dram_tensor("convw", [128, NCVT * 4], F32, kind="ExternalInput").ap()
    convb_in = nc.dram_tensor("convb", [128, NCVT], F32, kind="ExternalInput").ap()
    alog_r = nc.dram_tensor("alog_r", [1, 32], F32, kind="ExternalInput").ap()
    dtb_in = nc.dram_tensor("dtb", [32, 1], F32, kind="ExternalInput").ap()
    dn_in = nc.dram_tensor("Dn", [128, 32], F32, kind="ExternalInput").ap()
    nw_in = nc.dram_tensor("nw_row", [1, 2048], F32, kind="ExternalInput").ap()
    prev_in = nc.dram_tensor("prev_col", [64, 1], F32, kind="ExternalInput").ap()
    ut_in = nc.dram_tensor("UTones", [128, 128], F32, kind="ExternalInput").ap()
    id32_in = nc.dram_tensor("ident32", [32, 32], F32, kind="ExternalInput").ap()
    mst_in = nc.dram_tensor("maskST", [128, 128], F32, kind="ExternalInput").ap()
    sel32_in = nc.dram_tensor("sel32", [32, 32 * 128], F32, kind="ExternalInput").ap()
    y_out = nc.dram_tensor("yT", [D, T], F32, kind="ExternalOutput").ap()
    emit_C(nc, caps, x_in, win_in, wout_in, convw_in, convb_in, alog_r, dtb_in,
           dn_in, nw_in, prev_in, ut_in, id32_in, mst_in, sel32_in, y_out,
           n_cores=n_cores, upto=upto)
    return nc


def emit_C(nc, caps, x_in, win_in, wout_in, convw_in, convb_in, alog_r, dtb_in,
           dn_in, nw_in, prev_in, ut_in, id32_in, mst_in, sel32_in, y_out,
           n_cores=8, upto=9, P="C_"):
    xbc_d = nc.dram_tensor(P + "xbc_scr", [NCVT * 128, 2051], BF16).ap()
    z_d = nc.dram_tensor(P + "z_scr", [T, 2048], BF16).ap()
    yn_d = nc.dram_tensor(P + "yn_scr", [T, 2048], BF16).ap()
    stin_d = nc.dram_tensor(P + "stin_scr", [64, 2048], F32)
    stout_d = nc.dram_tensor(P + "stout_scr", [128, 2048], F32)

    LNCH = [(0, 3), (3, 512), (515, 512), (1027, 512), (1539, 512)]
    P2_BLOCKS = [(2048 + 128 * i, 128) for i in range(NCVT)] + [(4352, 32)]
    rg = [[2 * i, 2 * i + 1] for i in range(max(1, n_cores // 2))]

    with ExitStack() as ctx:
        def sb(name, shape, dt):
            return ctx.enter_context(nc.sbuf_tensor(P + name, shape, dt))
        def sem(name):
            return ctx.enter_context(nc.semaphore(P + name))

        s_T = sem("s_T"); s_V = sem("s_V"); s_S = sem("s_S")
        s_G = sem("s_G"); s_Y = sem("s_Y"); s_lw = sem("s_lw"); s_AG = sem("s_AG")
        s_hT = sem("s_hT"); s_hV = sem("s_hV"); s_hS = sem("s_hS")

        xln = sb("xlnC", [128, NT * 2051], BF16)
        dtT = sb("dtTC", [32, T], F32)
        winb = sb("winbC", [128, 2 * NT * 512], BF16)
        woutb = sb("woutbC", [128, 8 * 16 * 128], BF16)
        xbcs = sb("xbcsC", [128, 2 * 2051], BF16)
        xbcc = sb("xbccC", [128, NCVT * 512], BF16)
        xbcr = sb("xbcrC", [128, NCVT * 520], BF16)
        cvA = sb("cvAC", [128, 512], F32)
        cvB = sb("cvBC", [128, 512], F32)
        ctl = sb("ctileC", [128, 128], BF16)
        xf32 = sb("xf32C", [128, NT * CH], F32)
        Prun = sb("PrunC", [64, 2048], F32)
        Prun_bf = sb("PrunbfC", [64, 2048], BF16)

        dadj = sb("dadjC", [128, NSC * 32], F32)
        convw = sb("convwC", [128, NCVT * 4], F32)
        convb = sb("convbC", [128, NCVT], F32)
        lnsm = sb("lnsmC", [1, 2 * CH], F32)
        pair_bf = sb("pairC", [1, 2 * CH], BF16)
        rbuf = sb("rbufC", [128, 2 * CH], BF16)
        UT = sb("UTC", [128, 128], F32)
        id32 = sb("id32C", [32, 32], F32)
        mst = sb("mstC", [128, 128], F32)
        sel32 = sb("sel32C", [32, 32 * 128], F32)
        alogr = sb("alogrC", [1, 32], F32)
        dtb = sb("dtbC", [32, 1], F32)
        dn = sb("dnC", [128, 32], F32)
        nwb = sb("nwbC", [128, 2048], BF16)
        prevc = sb("prevcC", [64, 1], F32)
        Abc = sb("AbcC", [128, 32], F32)
        Arow = sb("ArowC", [1, 32], F32)
        acolb = sb("acolbC", [32, 1], F32)

        dtn = sb("dtnC", [128, 32], F32)
        ldt = sb("ldtC", [128, 32], F32)
        csb = sb("csbC", [128, 32], F32)
        acn = sb("acnC", [128, 32], F32)
        asl = sb("aslC", [128, 32], F32)
        acT = sb("acTC", [32, 128], F32)
        wdtn = sb("wdtnC", [128, 32], F32)
        wsc = sb("wscC", [128, 32], F32)
        fcols = sb("fcolsC", [64, 32], F32)
        bacr = sb("bacrC", [1, 32], F32)
        rowsc = sb("rowscC", [1, 64], F32)
        cendc = sb("cendC", [32, 1], F32)
        Gm = sb("GmC", [128, 128], BF16)
        pre2 = sb("pre2C", [128, 2 * 128], F32)
        E2 = sb("E2C", [128, 2 * 128], BF16)
        MT = sb("MTC", [128, 2 * 128], BF16)
        Xn = sb("XnC", [128, 2 * 64], BF16)
        Dx = sb("DxC", [128, 2 * 64], BF16)
        Bn = sb("BnC", [128, 64], BF16)
        Bw = sb("BwC", [128, 2 * 64], BF16)
        Pts = sb("PtsC", [64, 2 * 64], F32)
        ubuf = sb("ubufC", [128, 2 * 64], F32)
        tmpb = sb("tmpbC", [128, 2 * 64], F32)
        nasl = sb("naslC", [128, 32], F32)
        dexp = sb("dexpC", [128, 32], F32)
        rstd1 = sb("rstd1C", [128, 1], F32)
        eps1 = sb("eps1C", [1, 1], F32)
        eps128 = sb("eps128C", [128, 1], F32)
        one32 = sb("one32C", [32, 1], F32)
        cb = {n: sb("cC_" + n, shp, dt) for n, shp, dt in CONST_SPEC if n in C_CONSTS}

        nwrow = xf32[0:1, 0:2048].bitcast(F32) if False else xf32[0:1, 0:2048]
        nwrb = xbcs[0:1, 0:2048]
        dtsp = xf32[0:32, 2048:3072]
        Sin = xbcr[0:64, 2048:4096]
        xbf = xbcc[:, 0:4096]
        sqb = xbcc[:, 4096:8192]
        ynat = xbcs[:, 0:4096]
        zrl = xbcc[:, 0:4096]
        yrl = xbcc[:, 4096:8192]
        zsil = winb[:, 0:2048]
        ynf = winb[:, 2048:4096]
        ynT = xln[:, 0:8192]

        p = [ctx.enter_context(nc.psum_tensor(P + "pp%d" % i, [128, CH], F32))
             for i in range(8)]

        block = ctx.enter_context(nc.Block())

        cnt = {"T": 0, "V": 0, "S": 0, "G": 0, "Y": 0}
        SEMS = {"T": s_T, "V": s_V, "S": s_S, "G": s_G, "Y": s_Y}

        def _sec(k, body):
            if k == "T":
                @block.tensor
                def _(e): body(e)
            elif k == "V":
                @block.vector
                def _(e): body(e)
            elif k == "S":
                @block.scalar
                def _(e): body(e)
            elif k == "G":
                @block.gpsimd
                def _(e): body(e)
            else:
                @block.sync
                def _(e): body(e)

        def _step(k, f, ndma=1):
            snap = {k2: cnt[k2] for k2 in cnt}
            def body(e):
                for k2, v in snap.items():
                    if v == 0:
                        continue
                    e.wait_ge(SEMS[k2], v * (16 if k2 in ("G", "Y") else 1))
                f(e)
            _sec(k, body)
            cnt[k] += ndma if k in ("G", "Y") else 1

        def stepT(f): _step("T", lambda e: f(e).then_inc(s_T, 1))
        def stepV(f): _step("V", lambda e: f(e).then_inc(s_V, 1))
        def stepS(f): _step("S", lambda e: f(e).then_inc(s_S, 1))
        def stepG(f): _step("G", lambda e: f(e).then_inc(s_G, 16))
        def stepY(f): _step("Y", lambda e: f(e).then_inc(s_Y, 16))

        def stepG_multi(fs):
            def body(e):
                for f in fs:
                    f(e).then_inc(s_G, 16)
            _step("G", body, ndma=len(fs))

        s_vself = sem("s_vself"); s_sself = sem("s_sself")
        _mf = {"V": [0, s_vself], "S": [0, s_sself]}

        def mk_chain(k):
            st = _mf[k]
            def bump(ins):
                ins.then_inc(st[1], 1)
                st[0] += 1
            def fence(e):
                e.wait_ge(st[1], st[0])
            return bump, fence
        vbump, vfence = mk_chain("V")
        sbump, sfence = mk_chain("S")

        NLW = len(C_CONSTS) + 11

        @block.sync
        def _(g):
            def L(dst, src):
                g.dma_start(out=dst, in_=src).then_inc(s_lw, 16)
            for name in C_CONSTS:
                L(cb[name][:, :], caps[name])
            L(convw[:, :], convw_in); L(convb[:, :], convb_in)
            L(alogr[0:1, 0:32], alog_r); L(dtb[:, :], dtb_in); L(dn[:, :], dn_in)
            L(prevc[:, :], prev_in); L(UT[:, :], ut_in); L(id32[:, :], id32_in)
            L(mst[:, :], mst_in); L(sel32[:, :], sel32_in); L(nwrow[0:1, :], nw_in)

        for k in ("T", "V", "S", "G", "Y"):
            _sec(k, lambda e: e.wait_ge(s_lw, 16 * NLW))

        stepV(lambda v: v.memset(eps1[0:1, :], LN_EPS))
        stepV(lambda v: v.memset(one32[:, :], 1.0))
        stepV(lambda v: v.memset(eps128[:, :], LN_EPS))
        stepV(lambda v: v.memset(Prun[:, :], 0.0))
        stepV(lambda v: v.memset(bacr[0:1, :], 0.0))
        stepS(lambda s: s.activation(Arow[0:1, :], alogr[0:1, 0:32], AF.Exp))
        stepV(lambda v: v.tensor_scalar(Arow[0:1, :], Arow[0:1, :], -1.0, None, ALU.mult))
        stepT(lambda t: t.matmul(p[0][:, 0:32], cb["ones_row_f32"][0:1, :],
                                 Arow[0:1, :], start=True, stop=True))
        stepV(lambda v: v.tensor_copy(Abc[:, :], p[0][:, 0:32]))
        stepT(lambda t: t.matmul(p[0][0:32, 0:1], Arow[0:1, :],
                                 cb["ones_row_f32"][0:1, 0:1], start=True, stop=True))
        stepV(lambda v: v.tensor_copy(acolb[:, :], p[0][0:32, 0:1]))
        stepV(lambda v: v.tensor_copy(nwrb[0:1, :], nwrow[0:1, :]))
        for i in range(4):
            stepT(lambda t, i=i: t.matmul(p[1][:, :], cb["ones_row_bf"][0:1, :],
                                          nwrb[0:1, i * 512:(i + 1) * 512],
                                          start=True, stop=True))
            stepV(lambda v, i=i: v.tensor_copy(nwb[:, i * 512:(i + 1) * 512], p[1][:, :]))

        for ci, (t0, tw) in enumerate(LNCH):
            stepG_multi([
                (lambda g, j=j, t0=t0, tw=tw: g.dma_start(
                    out=xbf[:, j * 512:j * 512 + tw],
                    in_=x_in[128 * j:128 * (j + 1), t0:t0 + tw]))
                for j in range(NT)])
            def sqs(s, tw=tw):
                for j in range(NT):
                    ins = s.activation(sqb[:, j * 512:j * 512 + tw],
                                       xbf[:, j * 512:j * 512 + tw], AF.Square)
                return ins
            stepS(sqs)
            def stats(t, tw=tw):
                for j in range(NT):
                    t.matmul(p[2][0:1, 0:tw], cb["ones_col_bf"][:, :],
                             xbf[:, j * 512:j * 512 + tw], start=(j == 0), stop=(j == NT - 1))
                for j in range(NT):
                    ins = t.matmul(p[3][0:1, 0:tw], cb["ones_col_bf"][:, :],
                                   sqb[:, j * 512:j * 512 + tw], start=(j == 0),
                                   stop=(j == NT - 1))
                return ins
            stepT(stats)
            def lnv1(v, tw=tw):
                vbump(v.tensor_scalar(lnsm[0:1, 0:tw], p[2][0:1, 0:tw],
                                      1.0 / D, None, ALU.mult))
                vbump(v.tensor_scalar(lnsm[0:1, CH:CH + tw], p[3][0:1, 0:tw],
                                      1.0 / D, None, ALU.mult))
                vfence(v)
                vbump(v.tensor_tensor(cvA[0:1, 0:tw], lnsm[0:1, 0:tw],
                                      lnsm[0:1, 0:tw], ALU.mult))
                vfence(v)
                return v.tensor_tensor(lnsm[0:1, CH:CH + tw], lnsm[0:1, CH:CH + tw],
                                       cvA[0:1, 0:tw], ALU.subtract)
            stepV(lnv1)
            def lns1(s, tw=tw):
                sbump(s.activation(lnsm[0:1, CH:CH + tw], lnsm[0:1, CH:CH + tw],
                                   AF.Ln, bias=eps1[0:1, 0:1]))
                sfence(s)
                return s.activation(lnsm[0:1, CH:CH + tw], lnsm[0:1, CH:CH + tw],
                                    AF.Exp, scale=-0.5)
            stepS(lns1)
            def lnv2(v, tw=tw):
                vbump(v.tensor_copy(pair_bf[0:1, 0:tw], lnsm[0:1, CH:CH + tw]))
                vbump(v.tensor_tensor(lnsm[0:1, 0:tw], lnsm[0:1, 0:tw],
                                      lnsm[0:1, CH:CH + tw], ALU.mult))
                vfence(v)
                return v.tensor_copy(pair_bf[0:1, CH:CH + tw], lnsm[0:1, 0:tw])
            stepV(lnv2)
            def bcasts(t, tw=tw):
                t.matmul(p[4][:, 0:tw], cb["ones_row_bf"][0:1, :], pair_bf[0:1, 0:tw],
                         start=True, stop=True)
                return t.matmul(p[5][:, 0:tw], cb["ones_row_bf"][0:1, :],
                                pair_bf[0:1, CH:CH + tw], start=True, stop=True)
            stepT(bcasts)
            def lnv3(v, t0=t0, tw=tw):
                vbump(v.tensor_copy(rbuf[:, 0:tw], p[4][:, 0:tw]))
                vbump(v.tensor_copy(rbuf[:, CH:CH + tw], p[5][:, 0:tw]))
                vfence(v)
                for j in range(NT):
                    vbump(v.tensor_tensor(xln[:, j * 2051 + t0: j * 2051 + t0 + tw],
                                          xbf[:, j * 512:j * 512 + tw], rbuf[:, 0:tw],
                                          ALU.mult))
                vfence(v)
                for j in range(NT):
                    ins = v.tensor_tensor(xln[:, j * 2051 + t0: j * 2051 + t0 + tw],
                                          xln[:, j * 2051 + t0: j * 2051 + t0 + tw],
                                          rbuf[:, CH:CH + tw], ALU.subtract)
                    if j < NT - 1:
                        vbump(ins)
                return ins
            stepV(lnv3)

        def p2wload(g, bi):
            c0, w = P2_BLOCKS[bi]
            g.dma_start(
                out=winb.rearrange("p (s j c) -> p s j c", s=2, j=NT)[:, bi % 2, :, 0:w],
                in_=win_in.rearrange("(j p) c -> p j c", p=128)[:, :, c0:c0 + w]
            ).then_inc(s_G, 16)
        _step("G", lambda g: p2wload(g, 0), ndma=1)
        for bi, (c0, w) in enumerate(P2_BLOCKS):
            snap_p2 = {k2: cnt[k2] for k2 in cnt}
            cis = [(ci, t0, tw) for ci, (t0, tw) in enumerate(LNCH)
                   if not (bi == len(P2_BLOCKS) - 1 and ci == 0)]
            for g0 in range(0, len(cis), 2):
                grp = cis[g0:g0 + 2]
                def pjp(t, bi=bi, w=w, grp=grp):
                    for ci, t0, tw in grp:
                        for j in range(NT):
                            ins = t.matmul(p[ci % 2][0:w, 0:tw],
                                           winb[:, (bi % 2) * NT * 512 + j * 512:
                                                (bi % 2) * NT * 512 + j * 512 + w],
                                           xln[:, j * 2051 + t0: j * 2051 + t0 + tw],
                                           start=(j == 0), stop=(j == NT - 1))
                    return ins
                stepT(pjp)
                def pcp(s, bi=bi, grp=grp):
                    for ci, t0, tw in grp:
                        if bi < NCVT:
                            ins = s.activation(
                                xbcs[:, (bi % 2) * 2051 + t0:(bi % 2) * 2051 + t0 + tw],
                                p[ci % 2][:, 0:tw], AF.Copy)
                        else:
                            ins = s.activation(
                                dtT[0:32, t0 - 3:t0 - 3 + tw], p[ci % 2][0:32, 0:tw],
                                AF.Identity, bias=dtb[:, 0:1])
                    return ins
                stepS(pcp)
            if bi < NCVT:
                stepY(lambda g, bi=bi: g.dma_start(
                    out=xbc_d[bi * 128:(bi + 1) * 128, :],
                    in_=xbcs[:, (bi % 2) * 2051:(bi % 2) * 2051 + 2051]))
            if bi + 1 < len(P2_BLOCKS):
                def p2pf(g, bi=bi, snap_p2=snap_p2):
                    for k2, v2 in snap_p2.items():
                        if v2:
                            g.wait_ge(SEMS[k2], v2 * (16 if k2 in ("G", "Y") else 1))
                    p2wload(g, bi + 1)
                _sec("G", p2pf)
                cnt["G"] += 1
        for i in range(4):
            sl = dtT[0:32, i * 512:(i + 1) * 512]
            a1 = dtsp[0:32, 0:512]
            a2 = dtsp[0:32, 512:1024]
            def spchain(s, sl=sl, a1=a1, a2=a2):
                sbump(s.activation(a1, sl, AF.Abs))
                sfence(s)
                sbump(s.activation(a1, a1, AF.Exp, scale=-1.0))
                sfence(s)
                sbump(s.activation(a1, a1, AF.Ln, bias=one32[0:32, 0:1]))
                return s.activation(a2, sl, AF.Relu)
            stepS(spchain)
            stepV(lambda v, sl=sl, a1=a1, a2=a2: v.tensor_tensor(sl, a1, a2, ALU.add))

        def p3wload(g, zc):
            wslot = (len(P2_BLOCKS) + zc) % 2
            g.dma_start(
                out=winb.rearrange("p (s j c) -> p s j c", s=2, j=NT)[:, wslot, :, :],
                in_=win_in.rearrange("(j p) c -> p j c", p=128)
                [:, :, zc * 512:(zc + 1) * 512]).then_inc(s_G, 16)
        _step("G", lambda g: p3wload(g, 0), ndma=1)
        for zc in range(4):
            wslot = (len(P2_BLOCKS) + zc) % 2
            snap_p3 = {k2: cnt[k2] for k2 in cnt}
            for qd in range(4):
                for k2 in range(2):
                    def zmm(t, qd=qd, k2=k2, wslot=wslot):
                        for k in (2 * k2, 2 * k2 + 1):
                            tt = 4 * qd + k
                            for j in range(NT):
                                ins = t.matmul(p[k % 2][:, :],
                                               xln[:, j * 2051 + 3 + tt * 128:
                                                   j * 2051 + 3 + tt * 128 + 128],
                                               winb[:, wslot * NT * 512 + j * 512:
                                                    wslot * NT * 512 + (j + 1) * 512],
                                               start=(j == 0), stop=(j == NT - 1))
                        return ins
                    stepT(zmm)
                    def zcp(s, k2=k2):
                        for k in (2 * k2, 2 * k2 + 1):
                            ins = s.activation(
                                xbcs[:, k * 512:(k + 1) * 512], p[k % 2][:, :], AF.Copy)
                        return ins
                    stepS(zcp)
                stepY(lambda g, zc=zc, qd=qd: g.dma_start(
                    out=z_d.rearrange("(q p) c -> p q c", p=128)
                    [:, 4 * qd:4 * qd + 4, zc * 512:(zc + 1) * 512],
                    in_=xbcs[:, 0:2048].rearrange("p (s c) -> p s c", s=4)))
            if zc + 1 < 4:
                def p3pf(g, zc=zc, snap_p3=snap_p3):
                    for k2, v2 in snap_p3.items():
                        if v2:
                            g.wait_ge(SEMS[k2], v2 * (16 if k2 in ("G", "Y") else 1))
                    p3wload(g, zc + 1)
                _sec("G", p3pf)
                cnt["G"] += 1

        TCALL = {}
        _tc = [0]
        for c2 in range(NSC):
            Hb = c2 * NH
            seq = [("T1", Hb + 1), ("T1", Hb + 2)]
            for P2 in range(NH // 2):
                if P2 < NH // 2 - 1:
                    seq += [("T1", Hb + 2 * P2 + 3), ("T1", Hb + 2 * P2 + 4)]
                seq += [("T2", Hb + 2 * P2 + 1), ("T3", Hb + 2 * P2 + 1),
                        ("T2", Hb + 2 * P2 + 2), ("T3", Hb + 2 * P2 + 2)]
            for st2, Hx in seq:
                _tc[0] += 1
                TCALL[(st2, Hx)] = _tc[0]

        def tT1(H): return TCALL[("T1", H)]
        def tT2(H): return TCALL[("T2", H)]
        def tT3(H): return TCALL[("T3", H)]

        VPC = NH * 10 + 1

        def vccH(H, k):
            c2, h2 = divmod(H - 1, NH)
            return c2 * VPC + 1 + h2 * 10 + k

        def vpreH(H):
            c2, h2 = divmod(H - 1, NH)
            return c2 * VPC + (1 if h2 == 0 else 1 + (h2 - 1) * 10 + 2)

        for c in range(NSC if upto >= 2 else 0):
            cg, ck = c // 4, c % 4
            if ck == 0:
                stepG(lambda g, cg=cg: g.dma_start(
                    out=xbcr.rearrange("p (k c) -> p k c", c=520)[:, :, 0:515],
                    in_=xbc_d.rearrange("(k p) c -> p k c", p=128)
                    [:, :, cg * 512:cg * 512 + 515]))
                for k in range(NCVT):
                    base = k * 520
                    def convk(v, base=base, k=k):
                        vbump(v.tensor_scalar(
                            cvA[:, :], xbcr[:, base:base + 512],
                            convw[:, 4 * k:4 * k + 1], convb[:, k:k + 1],
                            ALU.mult, ALU.add))
                        for jj in range(1, 4):
                            vbump(v.tensor_scalar(
                                cvB[:, :], xbcr[:, base + jj:base + jj + 512],
                                convw[:, 4 * k + jj:4 * k + jj + 1], None, ALU.mult))
                            vfence(v)
                            ins = v.tensor_tensor(cvA[:, :], cvA[:, :], cvB[:, :],
                                                  ALU.add)
                            if jj < 3:
                                vbump(ins)
                                vfence(v)
                        return ins
                    stepV(convk)
                    stepS(lambda s, k=k: s.activation(xbcc[:, k * 512:(k + 1) * 512],
                                                      cvA[:, :], AF.Silu))
            t0c = c * 128
            stepT(lambda t, t0c=t0c: t.matmul(p[0][0:128, 0:32], dtT[:, t0c:t0c + 128],
                                              id32[:, :], start=True, stop=True))
            stepV(lambda v: v.tensor_copy(dtn[:, :], p[0][0:128, 0:32]))
            stepT(lambda t: t.matmul(p[1][0:128, 0:32], UT[:, :], dtn[:, :],
                                     start=True, stop=True))
            stepS(lambda s: s.activation(ldt[:, :], dtn[:, :], AF.Ln))
            def prepV1(v, t0c=t0c):
                vbump(v.reduce_sum(cendc[:, :], dtT[:, t0c:t0c + 128], axis=AX))
                vfence(v)
                vbump(v.tensor_tensor(cendc[:, :], cendc[:, :], acolb[:, :], ALU.mult))
                vbump(v.tensor_copy(csb[:, :], p[1][0:128, 0:32]))
                vfence(v)
                return v.tensor_tensor(acn[:, :], csb[:, :], Abc[:, :], ALU.mult)
            stepV(prepV1)
            stepT(lambda t: t.matmul(p[2][0:32, 0:128], acn[:, :],
                                     cb["ident_f32"][:, :], start=True, stop=True))
            stepT(lambda t: t.matmul(p[3][:, 0:32], cb["ones_row_f32"][0:1, :],
                                     bacr[0:1, :], start=True, stop=True))
            def prepV2(v):
                vbump(v.tensor_copy(acT[:, :], p[2][0:32, 0:128]))
                vbump(v.tensor_tensor(wsc[:, :], acn[:, :], p[3][:, 0:32], ALU.add))
                vbump(v.tensor_tensor(asl[:, :], acn[:, :], ldt[:, :], ALU.subtract))
                vfence(v)
                return v.tensor_scalar(nasl[:, :], asl[:, :], -1.0, None, ALU.mult)
            stepV(prepV2)
            stepT(lambda t: t.matmul(p[3][0:1, 0:32], cendc[:, :], id32[:, :],
                                     start=True, stop=True))
            def prepS1(s, c=c):
                sbump(s.activation(dadj[:, c * 32:(c + 1) * 32], wsc[:, :], AF.Exp))
                return s.activation(dexp[:, :], acn[:, :], AF.Exp)
            stepS(prepS1)
            def prepV3(v):
                vbump(v.tensor_copy(rowsc[0:1, 0:32], p[3][0:1, 0:32]))
                vfence(v)
                return v.tensor_tensor(bacr[0:1, :], bacr[0:1, :], rowsc[0:1, 0:32],
                                       ALU.add)
            stepV(prepV3)
            stepT(lambda t: t.matmul(p[3][:, 0:32], cb["ones_row_f32"][0:1, :],
                                     rowsc[0:1, 0:32], start=True, stop=True))
            stepV(lambda v: v.tensor_tensor(wsc[:, :], p[3][:, 0:32], asl[:, :], ALU.subtract))
            def prepS2(s):
                sbump(s.activation(wdtn[:, :], wsc[:, :], AF.Exp))
                return s.activation(rowsc[0:1, 32:64], rowsc[0:1, 0:32], AF.Exp)
            stepS(prepS2)
            bsl = xbcc[0:64, 16 * 512 + ck * 128: 16 * 512 + ck * 128 + 128]
            csl = xbcc[0:64, 17 * 512 + ck * 128: 17 * 512 + ck * 128 + 128]
            stepT(lambda t: t.matmul(p[3][0:64, 0:32], cb["ones_row_f32"][0:1, 0:64],
                                     rowsc[0:1, 32:64], start=True, stop=True))
            def prepT2(t, bsl=bsl, csl=csl):
                t.matmul(p[4][:, 0:128], bsl, csl, start=True, stop=True)
                return t.matmul(p[5][0:128, 0:64], bsl,
                                cb["ident2_bf"][0:64, :], start=True, stop=True)
            stepT(prepT2)
            def prepV4(v):
                vbump(v.tensor_copy(fcols[:, :], p[3][0:64, 0:32]))
                vbump(v.tensor_copy(Gm[:, :], p[4][:, 0:128]))
                vbump(v.tensor_copy(Bn[:, :], p[5][0:128, 0:64]))
                return v.tensor_copy(Prun_bf[:, :], Prun[:, :])
            stepV(prepV4)
            H0 = c * NH

            snapc = {k2: cnt[k2] for k2 in cnt}

            def _hw(e, snapc=snapc):
                for k2, v in snapc.items():
                    if v:
                        e.wait_ge(SEMS[k2], v * (16 if k2 in ("G", "Y") else 1))

            def prefirst(v, c=c, H0=H0):
                H1 = H0 + 1
                v.wait_ge(s_hT, tT1(H1))
                if H1 >= 3:
                    v.wait_ge(s_hS, H1 - 2)
                v.tensor_tensor(pre2[:, 0:128], p[0][:, 0:128], mst[:, :],
                                ALU.add).then_inc(s_hV, 1)
            _sec("V", prefirst)

            def heads_T(t, c=c, ck=ck, csl=csl, snapc=snapc, H0=H0):
                def t1pair(P):
                    HL = H0 + 2 * P + 1
                    HF = HL + 1
                    pb = p[2 + ((HL - 1) // 2) % 2]
                    if HL >= 4:
                        t.wait_ge(s_hV, vccH(HL - 3, 6))
                    if HL >= 3:
                        t.wait_ge(s_hV, vpreH(HL - 2))
                    t.matmul(p[0][:, 0:128], sel32[:, (2 * P) * 128:(2 * P) * 128 + 128],
                             acT[:, :], start=True, stop=True)
                    t.matmul(pb[0:128, 128:192], csl,
                             Prun_bf[:, (2 * P) * 64:(2 * P) * 64 + 64],
                             start=True, stop=True)
                    xp = xbcc[0:128, P * 512 + ck * 128: P * 512 + ck * 128 + 128]
                    t.matmul(pb[0:128, 0:128], xp, cb["ident_bf"][:, :],
                             start=True, stop=True).then_inc(s_hT, 1)
                    if HF >= 4:
                        t.wait_ge(s_hV, vccH(HF - 3, 2))
                    t.matmul(p[1][:, 0:128],
                             sel32[:, (2 * P + 1) * 128:(2 * P + 1) * 128 + 128],
                             acT[:, :], start=True, stop=True)
                    t.matmul(pb[0:128, 192:256], csl,
                             Prun_bf[:, (2 * P + 1) * 64:(2 * P + 1) * 64 + 64],
                             start=True, stop=True).then_inc(s_hT, 1)

                _hw(t, snapc)
                t1pair(0)
                for P in range(NH // 2):
                    if P < NH // 2 - 1:
                        t1pair(P + 1)
                    for H in (H0 + 2 * P + 1, H0 + 2 * P + 2):
                        par = (H - 1) % 2
                        pY = p[4 + par]; pSl = p[6 + par]
                        t.wait_ge(s_hV, vccH(H, 3))
                        t.matmul(pY[0:128, 0:64], MT[:, par * 128:par * 128 + 128],
                                 Xn[:, par * 64:par * 64 + 64], start=True,
                                 stop=True).then_inc(s_hT, 1)
                        t.wait_ge(s_hV, vccH(H, 4))
                        t.matmul(pSl[0:64, 0:64], Bw[:, par * 64:par * 64 + 64],
                                 Xn[:, par * 64:par * 64 + 64], start=True,
                                 stop=True).then_inc(s_hT, 1)

            def heads_S(s, snapc=snapc, H0=H0):
                for h in range(NH):
                    H = H0 + h + 1
                    par = (H - 1) % 2
                    if h == 0:
                        _hw(s, snapc)
                    if H >= 2:
                        s.wait_ge(s_hS, H - 1)
                    s.wait_ge(s_hV, vpreH(H))
                    s.activation(E2[:, par * 128:par * 128 + 128],
                                 pre2[:, par * 128:par * 128 + 128], AF.Exp,
                                 bias=nasl[:, h:h + 1]).then_inc(s_hS, 1)

            def heads_V(v, c=c, snapc=snapc, H0=H0):
                for h in range(NH):
                    H = H0 + h + 1
                    par = (H - 1) % 2
                    pb = p[2 + ((H - 1) // 2) % 2]
                    pSl = p[6 + par]; pY = p[4 + par]
                    if h == 0:
                        _hw(v, snapc)
                    if H >= 2:
                        v.wait_ge(s_hV, vccH(H - 1, 10))
                    v.wait_ge(s_hS, H)
                    if H >= 3:
                        v.wait_ge(s_hT, tT2(H - 2))
                    v.tensor_tensor(MT[:, par * 128:par * 128 + 128],
                                    E2[:, par * 128:par * 128 + 128], Gm[:, :],
                                    ALU.mult).then_inc(s_hV, 1)
                    if h < NH - 1:
                        par1 = H % 2
                        v.wait_ge(s_hT, tT1(H + 1))
                        if H >= 2:
                            v.wait_ge(s_hS, H - 1)
                        v.tensor_tensor(pre2[:, par1 * 128:par1 * 128 + 128],
                                        p[0 + par1][:, 0:128], mst[:, :],
                                        ALU.add).then_inc(s_hV, 1)
                    else:
                        v.memset(wsc[0:1, 0:1], 0.0).then_inc(s_hV, 1)
                    v.wait_ge(s_hT, tT1(H - par))
                    v.tensor_copy(Xn[:, par * 64:par * 64 + 64],
                                  pb[0:128, par * 64:par * 64 + 64]).then_inc(s_hV, 1)
                    v.tensor_scalar(Bw[:, par * 64:par * 64 + 64], Bn[:, :],
                                    wdtn[:, h:h + 1], None, ALU.mult).then_inc(s_hV, 1)
                    v.wait_ge(s_hV, vccH(H, 3))
                    v.tensor_scalar(Dx[:, par * 64:par * 64 + 64],
                                    Xn[:, par * 64:par * 64 + 64], dn[:, h:h + 1],
                                    None, ALU.mult).then_inc(s_hV, 1)
                    v.wait_ge(s_hT, tT1(H))
                    v.tensor_scalar(tmpb[:, par * 64:par * 64 + 64],
                                    pb[0:128, 128 + par * 64:192 + par * 64],
                                    dexp[:, h:h + 1],
                                    None, ALU.mult).then_inc(s_hV, 1)
                    v.tensor_scalar(Pts[:, par * 64:par * 64 + 64],
                                    Prun[:, h * 64:(h + 1) * 64], fcols[:, h:h + 1],
                                    None, ALU.mult).then_inc(s_hV, 1)
                    v.wait_ge(s_hT, tT2(H))
                    v.wait_ge(s_hV, vccH(H, 5))
                    v.tensor_tensor(ubuf[:, par * 64:par * 64 + 64],
                                    pY[0:128, 0:64], Dx[:, par * 64:par * 64 + 64],
                                    ALU.add).then_inc(s_hV, 1)
                    v.wait_ge(s_hV, vccH(H, 8))
                    v.tensor_tensor(
                        ynat[:, (c % 2) * 2048 + h * 64:(c % 2) * 2048 + (h + 1) * 64],
                        ubuf[:, par * 64:par * 64 + 64],
                        tmpb[:, par * 64:par * 64 + 64], ALU.add).then_inc(s_hV, 1)
                    v.wait_ge(s_hT, tT3(H))
                    v.wait_ge(s_hV, vccH(H, 7))
                    v.tensor_tensor(Prun[:, h * 64:(h + 1) * 64],
                                    Pts[:, par * 64:par * 64 + 64], pSl[0:64, 0:64],
                                    ALU.add).then_inc(s_hV, 1)

            _sec("T", heads_T)
            _sec("S", heads_S)
            _sec("V", heads_V)
            def mkT(t, c=c):
                t.wait_ge(s_hV, VPC * (c + 1))
                t.wait_ge(s_hS, NH * (c + 1))
                t.wait_ge(s_hT, 3 * NH * (c + 1))
                return t.matmul(p[0][0:1, 0:1], cb["ones_row_f32"][0:1, 0:1],
                                cb["ones_row_f32"][0:1, 0:1], start=True, stop=True)
            stepT(mkT)
            def mkV(v, c=c):
                v.wait_ge(s_hV, VPC * (c + 1))
                return v.tensor_copy(lnsm[0:1, 0:1], lnsm[0:1, 0:1])
            stepV(mkV)
            def mkS(s, c=c):
                s.wait_ge(s_hS, NH * (c + 1))
                return s.activation(lnsm[0:1, 4:5], lnsm[0:1, 4:5], AF.Copy)
            stepS(mkS)
            stepY(lambda g, c=c: g.dma_start(
                out=yn_d[c * 128:(c + 1) * 128, :],
                in_=ynat[:, (c % 2) * 2048:(c % 2) * 2048 + 2048]))

        if upto < 3:
            @block.sync
            def _(g):
                g.wait_ge(s_Y, 16 * cnt["Y"])
            return
        stepG(lambda g: g.dma_start(out=stin_d.ap(), in_=Prun[:, :]))
        def coll(g):
            return g.collective_compute(
                "AllGather", mybir.AluOpType.bypass,
                replica_groups=rg,
                ins=[stin_d.ap().opt()],
                outs=[stout_d.ap().opt()],
            )
        _step("G", lambda e: coll(e).then_inc(s_AG), ndma=0)

        @block.gpsimd
        def _(g):
            g.wait_ge(s_AG, 1)
        stepG(lambda g: g.dma_start(out=Prun[:, :], in_=stout_d.ap()[0:64, :]))
        stepV(lambda v: v.tensor_scalar(Sin[:, :], Prun[:, :], prevc[:, 0:1], None, ALU.mult))

        if upto < 4:
            @block.sync
            def _(g):
                g.wait_ge(s_Y, 16 * cnt["Y"])
            return
        stepG_multi([
            (lambda g, dm=dm: g.dma_start(
                out=woutb.rearrange("p (d2 j c) -> p d2 j c", d2=8, j=16)[:, dm, :, :],
                in_=wout_in.rearrange("(j p) c -> p j c", p=128)
                [:, :, dm * 128:(dm + 1) * 128]))
            for dm in range(8)])
        def p2loads(c2):
            return [
                (lambda g, c2=c2: g.dma_start(
                    out=yrl[:, (c2 % 2) * 2048:(c2 % 2) * 2048 + 2048],
                    in_=yn_d[c2 * 128:(c2 + 1) * 128, :])),
                (lambda g, c2=c2: g.dma_start(
                    out=zrl[:, (c2 % 2) * 2048:(c2 % 2) * 2048 + 2048],
                    in_=z_d[c2 * 128:(c2 + 1) * 128, :])),
                (lambda g, c2=c2: g.dma_start(
                    out=xbcr[:, (c2 % 2) * 131:(c2 % 2) * 131 + 131],
                    in_=xbc_d[17 * 128:18 * 128, c2 * 128:c2 * 128 + 131])),
            ]
        stepG_multi(p2loads(0))
        for c in range(NSC):
            snap_pre = {k2: cnt[k2] for k2 in cnt}
            def convC(v, c=c):
                xb0 = (c % 2) * 131
                vbump(v.tensor_scalar(cvA[:, 0:128], xbcr[:, xb0:xb0 + 128],
                                      convw[:, 4 * 17:4 * 17 + 1],
                                      convb[:, 17:18], ALU.mult, ALU.add))
                for jj in range(1, 4):
                    vbump(v.tensor_scalar(cvB[:, 0:128], xbcr[:, xb0 + jj:xb0 + jj + 128],
                                          convw[:, 4 * 17 + jj:4 * 17 + jj + 1],
                                          None, ALU.mult))
                    vfence(v)
                    ins = v.tensor_tensor(cvA[:, 0:128], cvA[:, 0:128],
                                          cvB[:, 0:128], ALU.add)
                    if jj < 3:
                        vbump(ins)
                        vfence(v)
                return ins
            stepV(convC)
            stepS(lambda s: s.activation(ctl[:, :], cvA[:, 0:128], AF.Silu))
            def umms(t):
                for i in range(4):
                    ins = t.matmul(p[i][:, :], ctl[0:64, :],
                                   Sin[:, i * 512:(i + 1) * 512],
                                   start=True, stop=True)
                return ins
            stepT(umms)
            def corr(v, c=c):
                for h in range(NH):
                    dsl = Dx[:, (h % 2) * 64:(h % 2) * 64 + 64]
                    vbump(v.tensor_scalar(
                        dsl, p[h // 8][:, (h % 8) * 64:(h % 8) * 64 + 64],
                        dadj[:, c * 32 + h: c * 32 + h + 1], None, ALU.mult))
                    vfence(v)
                    ins = v.tensor_tensor(
                        yrl[:, (c % 2) * 2048 + h * 64:(c % 2) * 2048 + (h + 1) * 64],
                        yrl[:, (c % 2) * 2048 + h * 64:(c % 2) * 2048 + (h + 1) * 64],
                        dsl, ALU.add)
                    if h < NH - 1:
                        vbump(ins)
                return ins
            stepV(corr)
            stepS(lambda s, c=c: s.activation(zsil[:, :],
                                              zrl[:, (c % 2) * 2048:(c % 2) * 2048 + 2048],
                                              AF.Silu))
            def gat1(v, c=c):
                vbump(v.tensor_tensor(ynf[:, :],
                                      yrl[:, (c % 2) * 2048:(c % 2) * 2048 + 2048],
                                      zsil[:, :], ALU.mult))
                vfence(v)
                vbump(v.tensor_tensor(zsil[:, :], ynf[:, :], ynf[:, :], ALU.mult))
                vfence(v)
                vbump(v.reduce_sum(rstd1[:, :], zsil[:, :], axis=AX))
                vfence(v)
                return v.tensor_scalar(rstd1[:, :], rstd1[:, :], 1.0 / 2048.0,
                                       None, ALU.mult)
            stepV(gat1)
            def gat2(s):
                sbump(s.activation(rstd1[:, :], rstd1[:, :], AF.Ln,
                                   bias=eps128[:, 0:1]))
                sfence(s)
                return s.activation(rstd1[:, :], rstd1[:, :], AF.Exp, scale=-0.5)
            stepS(gat2)
            def gat3(v):
                vbump(v.tensor_scalar(ynf[:, :], ynf[:, :], rstd1[:, 0:1],
                                      None, ALU.mult))
                vfence(v)
                return v.tensor_tensor(ynf[:, :], ynf[:, :], nwb[:, :], ALU.mult)
            stepV(gat3)
            for dq in range(4):
                def tmm(t, dq=dq):
                    for k in range(4):
                        di = 4 * dq + k
                        ins = t.matmul(p[k][0:128, 0:128],
                                       ynf[:, di * 128:(di + 1) * 128],
                                       cb["ident_bf"][:, :], start=True, stop=True)
                    return ins
                stepT(tmm)
                def tcp(v, dq=dq, c=c):
                    for k in range(4):
                        di = 4 * dq + k
                        ins = v.tensor_copy(
                            ynT[:, di * 512 + (c % 4) * 128:
                                di * 512 + (c % 4) * 128 + 128],
                            p[k][0:128, 0:128])
                        if k < 3:
                            vbump(ins)
                    return ins
                stepV(tcp)
            if c + 1 < NSC:
                def pf(e, c=c, snap_pre=snap_pre):
                    for k2, v2 in snap_pre.items():
                        if v2:
                            e.wait_ge(SEMS[k2], v2 * (16 if k2 in ("G", "Y") else 1))
                    for f in p2loads(c + 1):
                        f(e).then_inc(s_G, 16)
                _sec("G", pf)
                cnt["G"] += 3
            if c % 4 == 3:
                oc = c // 4
                stepG_multi([
                    (lambda g, j=j, oc=oc: g.dma_start(
                        out=tile(xf32, j, CH),
                        in_=x_in[128 * j:128 * (j + 1), 3 + oc * CH: 3 + (oc + 1) * CH]))
                    for j in range(NT)])
                for dp2 in range(4):
                    def womm(t, dp2=dp2):
                        for k in range(2):
                            dm = 2 * dp2 + k
                            for di in range(16):
                                ins = t.matmul(p[6 + k][:, :],
                                               woutb[:, dm * 16 * 128 + di * 128:
                                                     dm * 16 * 128 + (di + 1) * 128],
                                               ynT[:, di * 512:(di + 1) * 512],
                                               start=(di == 0), stop=(di == 15))
                        return ins
                    stepT(womm)
                    def wadd(v, dp2=dp2):
                        for k in range(2):
                            dm = 2 * dp2 + k
                            ins = v.tensor_tensor(
                                tile(xf32, dm, CH), tile(xf32, dm, CH), p[6 + k][:, :],
                                ALU.add)
                            if k == 0:
                                vbump(ins)
                        return ins
                    stepV(wadd)
                stepY(lambda g, oc=oc: g.dma_start(
                    out=y_out.rearrange("(j p) c -> p j c", p=128)
                    [:, :, oc * CH:(oc + 1) * CH],
                    in_=xf32.rearrange("p (j c) -> p j c", j=NT)))

        @block.sync
        def _(g):
            g.wait_ge(s_Y, 16 * cnt["Y"])



D_CONSTS = ("ones_col_bf", "ones_row_bf")


def build_D():
    nc = bass.Bass("TRN2", target_bir_lowering=False)
    caps = declare_constants(nc, D_CONSTS)
    x_in = nc.dram_tensor("xTd", [D, 2049], F32, kind="ExternalInput").ap()
    wr_in = nc.dram_tensor("WrT", [D, D], F32, kind="ExternalInput").ap()
    wk_in = nc.dram_tensor("WkT", [D, D], F32, kind="ExternalInput").ap()
    wv_in = nc.dram_tensor("WvT", [D, D], F32, kind="ExternalInput").ap()
    mur_in = nc.dram_tensor("mur", [128, 8], F32, kind="ExternalInput").ap()
    muk_in = nc.dram_tensor("muk", [128, 8], F32, kind="ExternalInput").ap()
    y_out = nc.dram_tensor("yT", [D, T], F32, kind="ExternalOutput").ap()
    emit_D(nc, caps, x_in, wr_in, wk_in, wv_in, mur_in, muk_in, y_out)
    return nc


def emit_D(nc, caps, x_in, wr_in, wk_in, wv_in, mur_in, muk_in, y_out, P="D_"):

    LNCH = [(0, 1), (1, 512), (513, 512), (1025, 512), (1537, 512)]

    with ExitStack() as ctx:
        def sb(name, shape, dt):
            return ctx.enter_context(nc.sbuf_tensor(P + name, shape, dt))
        def sem(name):
            return ctx.enter_context(nc.semaphore(P + name))

        s_T = sem("s_T"); s_V = sem("s_V"); s_S = sem("s_S")
        s_G = sem("s_G"); s_Y = sem("s_Y"); s_lw = sem("s_lw")

        xln = sb("xlnD", [128, NT * 2049], BF16)
        wrb = sb("wrbD", [128, NT * 1024], BF16)
        wkb = sb("wkbD", [128, NT * 1024], BF16)
        wvb = sb("wvbD", [128, NT * 1024], BF16)
        rT = sb("rTD", [128, NT * T], BF16)
        k2T = sb("k2TD", [128, NT * CH], BF16)
        xsh = sb("xshD", [128, NT * CH], BF16)
        xf32 = sb("xf32D", [128, NT * CH], F32)
        mur = sb("murD", [128, 8], F32)
        muk = sb("mukD", [128, 8], F32)
        omr = sb("omrD", [128, 8], F32)
        omk = sb("omkD", [128, 8], F32)
        lnsm = sb("lnsmD", [1, 2 * CH], F32)
        pair_bf = sb("pairD", [1, 2 * CH], BF16)
        rbuf = sb("rbufD", [128, 2 * CH], BF16)
        kst = sb("kstD", [128, CH], BF16)
        kst2 = sb("kst2D", [128, CH], BF16)
        eps1 = sb("eps1D", [1, 1], F32)
        cb = {n: sb("cD_" + n, shp, dt) for n, shp, dt in CONST_SPEC if n in D_CONSTS}
        xbf = rT[:, 0:4096]
        sqb = rT[:, 4096:8192]

        p = [ctx.enter_context(nc.psum_tensor(P + "pd%d" % i, [128, CH], F32))
             for i in range(6)]

        block = ctx.enter_context(nc.Block())

        cnt = {"T": 0, "V": 0, "S": 0, "G": 0, "Y": 0}
        SEMS = {"T": s_T, "V": s_V, "S": s_S, "G": s_G, "Y": s_Y}

        def _sec(k, body):
            if k == "T":
                @block.tensor
                def _(e): body(e)
            elif k == "V":
                @block.vector
                def _(e): body(e)
            elif k == "S":
                @block.scalar
                def _(e): body(e)
            elif k == "G":
                @block.gpsimd
                def _(e): body(e)
            else:
                @block.sync
                def _(e): body(e)

        def _step(k, f, ndma=1):
            snap = {k2: cnt[k2] for k2 in cnt}
            def body(e):
                for k2, v in snap.items():
                    if v == 0:
                        continue
                    e.wait_ge(SEMS[k2], v * (16 if k2 in ("G", "Y") else 1))
                f(e)
            _sec(k, body)
            cnt[k] += ndma if k in ("G", "Y") else 1

        def stepT(f): _step("T", lambda e: f(e).then_inc(s_T, 1))
        def stepV(f): _step("V", lambda e: f(e).then_inc(s_V, 1))
        def stepS(f): _step("S", lambda e: f(e).then_inc(s_S, 1))
        def stepG(f): _step("G", lambda e: f(e).then_inc(s_G, 16))
        def stepY(f): _step("Y", lambda e: f(e).then_inc(s_Y, 16))

        def stepG_multi(fs):
            def body(e):
                for f in fs:
                    f(e).then_inc(s_G, 16)
            _step("G", body, ndma=len(fs))

        s_vself = sem("s_vself"); s_sself = sem("s_sself")
        _mf = {"V": [0, s_vself], "S": [0, s_sself]}

        def mk_chain(k):
            st = _mf[k]
            def bump(ins):
                ins.then_inc(st[1], 1)
                st[0] += 1
            def fence(e):
                e.wait_ge(st[1], st[0])
            return bump, fence
        vbump, vfence = mk_chain("V")
        sbump, sfence = mk_chain("S")

        NLW = len(D_CONSTS) + 2

        @block.sync
        def _(g):
            for name in D_CONSTS:
                g.dma_start(out=cb[name][:, :], in_=caps[name]).then_inc(s_lw, 16)
            g.dma_start(out=mur[:, :], in_=mur_in).then_inc(s_lw, 16)
            g.dma_start(out=muk[:, :], in_=muk_in).then_inc(s_lw, 16)

        for k in ("T", "V", "S", "G", "Y"):
            _sec(k, lambda e: e.wait_ge(s_lw, 16 * NLW))

        stepG_multi([
            (lambda g: dma_wtiles(g, wrb[:, :], wr_in, NT)),
            (lambda g: dma_wtiles(g, wkb[:, :], wk_in, NT)),
            (lambda g: dma_wtiles(g, wvb[:, :], wv_in, NT)),
        ])
        stepV(lambda v: v.memset(eps1[0:1, :], LN_EPS))
        stepV(lambda v: v.tensor_scalar(omr[:, :], mur[:, :], -1.0, 1.0, ALU.mult, ALU.add))
        stepV(lambda v: v.tensor_scalar(omk[:, :], muk[:, :], -1.0, 1.0, ALU.mult, ALU.add))

        for ci, (t0, tw) in enumerate(LNCH):
            stepG_multi([
                (lambda g, j=j, t0=t0, tw=tw: g.dma_start(
                    out=xbf[:, j * 512:j * 512 + tw],
                    in_=x_in[128 * j:128 * (j + 1), t0:t0 + tw],
                    allow_slow_non_contiguous=(tw == 1)))
                for j in range(NT)])
            def sqs(s, tw=tw):
                for j in range(NT):
                    ins = s.activation(sqb[:, j * 512:j * 512 + tw],
                                       xbf[:, j * 512:j * 512 + tw], AF.Square)
                return ins
            stepS(sqs)
            def stats(t, tw=tw):
                for j in range(NT):
                    t.matmul(p[2][0:1, 0:tw], cb["ones_col_bf"][:, :],
                             xbf[:, j * 512:j * 512 + tw], start=(j == 0), stop=(j == NT - 1))
                for j in range(NT):
                    ins = t.matmul(p[3][0:1, 0:tw], cb["ones_col_bf"][:, :],
                                   sqb[:, j * 512:j * 512 + tw], start=(j == 0),
                                   stop=(j == NT - 1))
                return ins
            stepT(stats)
            def lnv1(v, tw=tw):
                vbump(v.tensor_scalar(lnsm[0:1, 0:tw], p[2][0:1, 0:tw],
                                      1.0 / D, None, ALU.mult))
                vbump(v.tensor_scalar(lnsm[0:1, CH:CH + tw], p[3][0:1, 0:tw],
                                      1.0 / D, None, ALU.mult))
                vfence(v)
                vbump(v.tensor_tensor(p[2][0:1, 0:tw], lnsm[0:1, 0:tw],
                                      lnsm[0:1, 0:tw], ALU.mult))
                vfence(v)
                return v.tensor_tensor(lnsm[0:1, CH:CH + tw], lnsm[0:1, CH:CH + tw],
                                       p[2][0:1, 0:tw], ALU.subtract)
            stepV(lnv1)
            def lns1(s, tw=tw):
                sbump(s.activation(lnsm[0:1, CH:CH + tw], lnsm[0:1, CH:CH + tw],
                                   AF.Ln, bias=eps1[0:1, 0:1]))
                sfence(s)
                return s.activation(lnsm[0:1, CH:CH + tw], lnsm[0:1, CH:CH + tw],
                                    AF.Exp, scale=-0.5)
            stepS(lns1)
            def lnv2(v, tw=tw):
                vbump(v.tensor_copy(pair_bf[0:1, 0:tw], lnsm[0:1, CH:CH + tw]))
                vbump(v.tensor_tensor(lnsm[0:1, 0:tw], lnsm[0:1, 0:tw],
                                      lnsm[0:1, CH:CH + tw], ALU.mult))
                vfence(v)
                return v.tensor_copy(pair_bf[0:1, CH:CH + tw], lnsm[0:1, 0:tw])
            stepV(lnv2)
            def bcasts(t, tw=tw):
                t.matmul(p[4][:, 0:tw], cb["ones_row_bf"][0:1, :], pair_bf[0:1, 0:tw],
                         start=True, stop=True)
                return t.matmul(p[5][:, 0:tw], cb["ones_row_bf"][0:1, :],
                                pair_bf[0:1, CH:CH + tw], start=True, stop=True)
            stepT(bcasts)
            def lnv3(v, t0=t0, tw=tw):
                vbump(v.tensor_copy(rbuf[:, 0:tw], p[4][:, 0:tw]))
                vbump(v.tensor_copy(rbuf[:, CH:CH + tw], p[5][:, 0:tw]))
                vfence(v)
                for j in range(NT):
                    vbump(v.tensor_tensor(xln[:, j * 2049 + t0: j * 2049 + t0 + tw],
                                          xbf[:, j * 512:j * 512 + tw], rbuf[:, 0:tw],
                                          ALU.mult))
                vfence(v)
                for j in range(NT):
                    ins = v.tensor_tensor(xln[:, j * 2049 + t0: j * 2049 + t0 + tw],
                                          xln[:, j * 2049 + t0: j * 2049 + t0 + tw],
                                          rbuf[:, CH:CH + tw], ALU.subtract)
                    if j < NT - 1:
                        vbump(ins)
                return ins
            stepV(lnv3)

        for ci in range(4):
            off = 1 + ci * 512
            def shmix(v, off=off, mu=mur, om=omr):
                for j in range(NT):
                    v.tensor_scalar(xsh[:, j * 512:(j + 1) * 512],
                                    xln[:, j * 2049 + off: j * 2049 + off + 512],
                                    mu[:, j:j + 1], None, ALU.mult)
                for j in range(NT):
                    ins = v.tensor_scalar(kst[:, 0:512] if False else
                                          xsh[:, j * 512:(j + 1) * 512],
                                          xln[:, j * 2049 + off - 1: j * 2049 + off - 1 + 512],
                                          om[:, j:j + 1], None, ALU.mult)
                return ins
            def sh1(v, off=off, mu=mur):
                for j in range(NT):
                    ins = v.tensor_scalar(xsh[:, j * 512:(j + 1) * 512],
                                          xln[:, j * 2049 + off: j * 2049 + off + 512],
                                          mu[:, j:j + 1], None, ALU.mult)
                return ins
            def sh2(v, off=off, om=omr):
                for j in range(NT):
                    ins = v.tensor_scalar(sqb[:, j * 512:(j + 1) * 512] if False else
                                          xf32.bitcast(BF16)[:, j * 512:(j + 1) * 512]
                                          if False else
                                          kst[:, 0:512],
                                          xln[:, j * 2049 + off - 1: j * 2049 + off - 1 + 512],
                                          om[:, j:j + 1], None, ALU.mult)
                return ins
            def shiftr(v, off=off):
                for j in range(NT):
                    tmp = kst[:, :] if j % 2 == 0 else kst2[:, 0:512]
                    vbump(v.tensor_scalar(
                        xsh[:, j * 512:(j + 1) * 512],
                        xln[:, j * 2049 + off: j * 2049 + off + 512],
                        mur[:, j:j + 1], None, ALU.mult))
                    vbump(v.tensor_scalar(
                        tmp, xln[:, j * 2049 + off - 1: j * 2049 + off - 1 + 512],
                        omr[:, j:j + 1], None, ALU.mult))
                    vfence(v)
                    ins = v.tensor_tensor(
                        xsh[:, j * 512:(j + 1) * 512], xsh[:, j * 512:(j + 1) * 512],
                        tmp, ALU.add)
                    if j < NT - 1:
                        vbump(ins)
                return ins
            stepV(shiftr)
            for op2 in range(NT // 2):
                def rmm(t, op2=op2):
                    for k in range(2):
                        ot = 2 * op2 + k
                        for j in range(NT):
                            ins = t.matmul(p[k][:, :],
                                           wrb[:, j * 1024 + ot * 128:
                                               j * 1024 + ot * 128 + 128],
                                           xsh[:, j * 512:(j + 1) * 512],
                                           start=(j == 0), stop=(j == NT - 1))
                    return ins
                stepT(rmm)
                def rsig(s, op2=op2, ci=ci):
                    for k in range(2):
                        ot = 2 * op2 + k
                        ins = s.activation(
                            rT[:, ot * T + ci * 512: ot * T + ci * 512 + 512],
                            p[k][:, :], AF.Sigmoid)
                    return ins
                stepS(rsig)

        for ci in range(4):
            off = 1 + ci * 512
            def shiftk(v, off=off):
                for j in range(NT):
                    tmp = kst[:, :] if j % 2 == 0 else kst2[:, 0:512]
                    vbump(v.tensor_scalar(
                        xsh[:, j * 512:(j + 1) * 512],
                        xln[:, j * 2049 + off: j * 2049 + off + 512],
                        muk[:, j:j + 1], None, ALU.mult))
                    vbump(v.tensor_scalar(
                        tmp, xln[:, j * 2049 + off - 1: j * 2049 + off - 1 + 512],
                        omk[:, j:j + 1], None, ALU.mult))
                    vfence(v)
                    ins = v.tensor_tensor(
                        xsh[:, j * 512:(j + 1) * 512], xsh[:, j * 512:(j + 1) * 512],
                        tmp, ALU.add)
                    if j < NT - 1:
                        vbump(ins)
                return ins
            stepV(shiftk)
            for op2 in range(NT // 2):
                def kmm(t, op2=op2):
                    for k in range(2):
                        ot = 2 * op2 + k
                        for j in range(NT):
                            ins = t.matmul(p[k][:, :],
                                           wkb[:, j * 1024 + ot * 128:
                                               j * 1024 + ot * 128 + 128],
                                           xsh[:, j * 512:(j + 1) * 512],
                                           start=(j == 0), stop=(j == NT - 1))
                    return ins
                stepT(kmm)
                def krel(s, op2=op2):
                    for k in range(2):
                        ot = 2 * op2 + k
                        ins = s.activation(k2T[:, ot * 512:(ot + 1) * 512],
                                           p[k][:, :], AF.Relu)
                    return ins
                stepS(krel)
                def ksq(v, op2=op2):
                    for k in range(2):
                        ot = 2 * op2 + k
                        ins = v.tensor_tensor(
                            k2T[:, ot * 512:(ot + 1) * 512],
                            k2T[:, ot * 512:(ot + 1) * 512],
                            k2T[:, ot * 512:(ot + 1) * 512], ALU.mult)
                        if k == 0:
                            vbump(ins)
                    return ins
                stepV(ksq)
            stepG_multi([
                (lambda g, j=j, ci=ci: g.dma_start(
                    out=tile(xf32, j, CH),
                    in_=x_in[128 * j:128 * (j + 1), 1 + ci * CH: 1 + (ci + 1) * CH]))
                for j in range(NT)])
            for op2 in range(NT // 2):
                def vmm(t, op2=op2):
                    for k in range(2):
                        ot = 2 * op2 + k
                        for j in range(NT):
                            ins = t.matmul(p[2 + k][:, :],
                                           wvb[:, j * 1024 + ot * 128:
                                               j * 1024 + ot * 128 + 128],
                                           k2T[:, j * 512:(j + 1) * 512],
                                           start=(j == 0), stop=(j == NT - 1))
                    return ins
                stepT(vmm)
                def vadd(v, op2=op2, ci=ci):
                    for k in range(2):
                        ot = 2 * op2 + k
                        tmp = kst[:, :] if k == 0 else kst2[:, 0:512]
                        vbump(v.tensor_tensor(
                            tmp, p[2 + k][:, :],
                            rT[:, ot * T + ci * 512: ot * T + ci * 512 + 512],
                            ALU.mult))
                        vfence(v)
                        ins = v.tensor_tensor(tile(xf32, ot, CH), tile(xf32, ot, CH),
                                              tmp, ALU.add)
                        if k == 0:
                            vbump(ins)
                    return ins
                stepV(vadd)
            stepY(lambda g, ci=ci: g.dma_start(
                out=y_out.rearrange("(j p) c -> p j c", p=128)[:, :, ci * CH:(ci + 1) * CH],
                in_=xf32.rearrange("p (j c) -> p j c", j=NT)))

        @block.sync
        def _(g):
            g.wait_ge(s_Y, 16 * cnt["Y"])



def emit_glue(nc, nb, src_ap, dst_ap, prev_ap, rg, P):
    bin_d = nc.dram_tensor(P + "bin", [128, NT * nb], F32)
    bout_d = nc.dram_tensor(P + "bout", [256, NT * nb], F32)
    with ExitStack() as ctx:
        bb = ctx.enter_context(nc.sbuf_tensor(P + "bb", [128, NT * nb], F32))
        pv = ctx.enter_context(nc.sbuf_tensor(P + "pv", [128, 1], F32))
        sg = ctx.enter_context(nc.semaphore(P + "sg"))
        sa = ctx.enter_context(nc.semaphore(P + "sa"))
        sv = ctx.enter_context(nc.semaphore(P + "sv"))
        block = ctx.enter_context(nc.Block())

        @block.gpsimd
        def _(g):
            g.dma_start(out=pv[:, :], in_=prev_ap).then_inc(sg, 16)
            g.dma_start(out=bb.rearrange("p (j c) -> p j c", j=NT),
                        in_=src_ap.rearrange("(j p) c -> p j c", p=128),
                        allow_slow_non_contiguous=(nb == 1)).then_inc(sg, 16)
            g.wait_ge(sg, 32)
            g.dma_start(out=bin_d.ap(), in_=bb[:, :]).then_inc(sg, 16)
            g.wait_ge(sg, 48)
            g.collective_compute(
                "AllGather", mybir.AluOpType.bypass,
                replica_groups=rg,
                ins=[bin_d.ap().opt()],
                outs=[bout_d.ap().opt()],
            ).then_inc(sa)
            g.wait_ge(sa, 1)
            g.dma_start(out=bb[:, :], in_=bout_d.ap()[0:128, :]).then_inc(sg, 16)
            g.wait_ge(sv, 1)
            g.dma_start(out=dst_ap.rearrange("(j p) c -> p j c", p=128),
                        in_=bb.rearrange("p (j c) -> p j c", j=NT),
                        allow_slow_non_contiguous=(nb == 1)).then_inc(sg, 16)
            g.wait_ge(sg, 80)

        @block.vector
        def _(v):
            v.wait_ge(sg, 64)
            v.tensor_scalar(bb[:, :], bb[:, :], pv[:, 0:1], None,
                            ALU.mult).then_inc(sv, 1)


def build_fused(n_cores=8):
    nc = bass.Bass("TRN2", target_bir_lowering=False)
    caps = declare_constants(nc)
    rg = [[2 * i, 2 * i + 1] for i in range(max(1, n_cores // 2))]
    x_in = nc.dram_tensor("xT_full", [D, TF], F32, kind="ExternalInput").ap()
    wq_in = nc.dram_tensor("wq", [D, D], F32, kind="ExternalInput").ap()
    wk_in = nc.dram_tensor("wk", [D, D], F32, kind="ExternalInput").ap()
    wo_in = nc.dram_tensor("wo", [D, D], F32, kind="ExternalInput").ap()
    bo_in = nc.dram_tensor("bo", [128, 8], F32, kind="ExternalInput").ap()
    w1_in = nc.dram_tensor("w1T_all", [9, D, 4096], F32, kind="ExternalInput").ap()
    w2_in = nc.dram_tensor("w2T_all", [9, 8, 4096, 128], F32, kind="ExternalInput").ap()
    b1_in = nc.dram_tensor("b1_all", [9, 128, 32], F32, kind="ExternalInput").ap()
    b2_in = nc.dram_tensor("b2_all", [9, 1024], F32, kind="ExternalInput").ap()
    rw_in = nc.dram_tensor("rwT", [D, 8], F32, kind="ExternalInput").ap()
    rb_in = nc.dram_tensor("rb", [1, 8], F32, kind="ExternalInput").ap()
    sel9_in = nc.dram_tensor("sel9", [9, 1024], BF16, kind="ExternalInput").ap()
    win_in = nc.dram_tensor("W_in_p", [D, PROJ_COLS], F32, kind="ExternalInput").ap()
    wout_in = nc.dram_tensor("W_out", [2048, D], F32, kind="ExternalInput").ap()
    convw_in = nc.dram_tensor("convw", [128, NCVT * 4], F32, kind="ExternalInput").ap()
    convb_in = nc.dram_tensor("convb", [128, NCVT], F32, kind="ExternalInput").ap()
    alog_r = nc.dram_tensor("alog_r", [1, 32], F32, kind="ExternalInput").ap()
    dtb_in = nc.dram_tensor("dtb", [32, 1], F32, kind="ExternalInput").ap()
    dn_in = nc.dram_tensor("Dn", [128, 32], F32, kind="ExternalInput").ap()
    nw_in = nc.dram_tensor("nw_row", [1, 2048], F32, kind="ExternalInput").ap()
    prev_in = nc.dram_tensor("prev_col", [64, 1], F32, kind="ExternalInput").ap()
    prev128_in = nc.dram_tensor("prev128", [128, 1], F32, kind="ExternalInput").ap()
    ut_in = nc.dram_tensor("UTones", [128, 128], F32, kind="ExternalInput").ap()
    id32_in = nc.dram_tensor("ident32", [32, 32], F32, kind="ExternalInput").ap()
    mst_in = nc.dram_tensor("maskST", [128, 128], F32, kind="ExternalInput").ap()
    sel32_in = nc.dram_tensor("sel32", [32, 32 * 128], F32, kind="ExternalInput").ap()
    wr_in = nc.dram_tensor("WrT", [D, D], F32, kind="ExternalInput").ap()
    wkd_in = nc.dram_tensor("WkT", [D, D], F32, kind="ExternalInput").ap()
    wv_in = nc.dram_tensor("WvT", [D, D], F32, kind="ExternalInput").ap()
    mur_in = nc.dram_tensor("mur", [128, 8], F32, kind="ExternalInput").ap()
    muk_in = nc.dram_tensor("muk", [128, 8], F32, kind="ExternalInput").ap()
    y_out = nc.dram_tensor("yT", [D, T], F32, kind="ExternalOutput").ap()
    xb_d = nc.dram_tensor("xb_d", [D, T], F32).ap()
    xc_d = nc.dram_tensor("xc_d", [D, 2051], F32).ap()
    xd_d = nc.dram_tensor("xd_d", [D, 2049], F32).ap()

    emit_A(nc, caps, x_in, wq_in, wk_in, wo_in, bo_in, xb_d, P="A_")
    emit_B(nc, caps, xb_d, w1_in, w2_in, b1_in, b2_in, rw_in, rb_in, sel9_in,
           xc_d[:, 3:2051], P="B_")
    emit_glue(nc, 3, xc_d[:, 2048:2051], xc_d[:, 0:3], prev128_in, rg, P="G1_")
    emit_C(nc, caps, xc_d, win_in, wout_in, convw_in, convb_in, alog_r, dtb_in,
           dn_in, nw_in, prev_in, ut_in, id32_in, mst_in, sel32_in,
           xd_d[:, 1:2049], n_cores=n_cores, P="C_")
    emit_glue(nc, 1, xd_d[:, 2048:2049], xd_d[:, 0:1], prev128_in, rg, P="G2_")
    emit_D(nc, caps, xd_d, wr_in, wkd_in, wv_in, mur_in, muk_in, y_out, P="D_")
    return nc


def _roll_xT(xb, half):
    if half == 0:
        arr = xb
    else:
        arr = np.concatenate([xb[2048:], xb[:2048]], axis=0)
    return np.ascontiguousarray(arr.T)


def _hostA(consts, ins, x, c):
    b, half = c // 2, c % 2
    m = dict(consts)
    m["xT_full"] = _roll_xT(x[b], half)
    m["wq"] = np.ascontiguousarray(ins["L0_attn_Wq"])
    m["wk"] = np.ascontiguousarray(ins["L0_attn_Wk"])
    m["wo"] = np.ascontiguousarray(ins["L0_attn_Wo"])
    m["bo"] = np.ascontiguousarray(ins["L0_attn_bo"].reshape(8, 128).T)
    return m


def _sharedB(ins):
    m = {}
    w1 = np.concatenate([ins["L0_ffn_shared_w1"][None], ins["L0_ffn_spec_w1"]], 0)
    w2 = np.concatenate([ins["L0_ffn_shared_w2"][None], ins["L0_ffn_spec_w2"]], 0)
    b1 = np.concatenate([ins["L0_ffn_shared_b1"][None], ins["L0_ffn_spec_b1"]], 0)
    b2 = np.concatenate([ins["L0_ffn_shared_b2"][None], ins["L0_ffn_spec_b2"]], 0)
    m["w1T_all"] = np.ascontiguousarray(np.transpose(w1, (0, 2, 1)))
    w2T = np.transpose(w2, (0, 2, 1))
    m["w2T_all"] = np.ascontiguousarray(w2T.reshape(9, 4096, 8, 128).transpose(0, 2, 1, 3))
    m["b1_all"] = np.ascontiguousarray(b1.reshape(9, 32, 128).transpose(0, 2, 1))
    m["b2_all"] = np.ascontiguousarray(b2)
    m["rwT"] = np.ascontiguousarray(ins["L0_ffn_router_w"].T)
    m["rb"] = np.ascontiguousarray(ins["L0_ffn_router_b"][None, :])
    return m


def _sharedC(ins):
    m = {}
    W_in = ins["L1_attn_W_in"]
    wp = np.zeros((1024, 4384), np.float32)
    wp[:, 0:2048] = W_in[:, 0:2048]
    wp[:, 2048:4096] = W_in[:, 2048:4096]
    wp[:, 4096:4160] = W_in[:, 4096:4160]
    wp[:, 4224:4288] = W_in[:, 4160:4224]
    wp[:, 4352:4384] = W_in[:, 4224:4256]
    m["W_in_p"] = wp
    m["W_out"] = np.ascontiguousarray(ins["L1_attn_W_out"])
    cw = ins["L1_attn_conv_w"][:, 0, :]
    cb_ = ins["L1_attn_conv_b"]
    cwp = np.zeros((18 * 128, 4), np.float32)
    cbp = np.zeros((18 * 128,), np.float32)
    cwp[0:2048] = cw[0:2048]; cbp[0:2048] = cb_[0:2048]
    cwp[16 * 128:16 * 128 + 64] = cw[2048:2112]
    cbp[16 * 128:16 * 128 + 64] = cb_[2048:2112]
    cwp[17 * 128:17 * 128 + 64] = cw[2112:2176]
    cbp[17 * 128:17 * 128 + 64] = cb_[2112:2176]
    m["convw"] = np.ascontiguousarray(cwp.reshape(18, 128, 4).transpose(1, 0, 2).reshape(128, 72))
    m["convb"] = np.ascontiguousarray(cbp.reshape(18, 128).T)
    m["alog_r"] = np.ascontiguousarray(ins["L1_attn_A_log"][None, :])
    m["dtb"] = np.ascontiguousarray(ins["L1_attn_dt_bias"][:, None])
    m["Dn"] = np.ascontiguousarray(np.tile(ins["L1_attn_D_param"][None, :], (128, 1)))
    m["nw_row"] = np.ascontiguousarray(ins["L1_attn_norm_w"][None, :])
    m["UTones"] = np.triu(np.ones((128, 128), np.float32))
    mstv = np.zeros((128, 128), np.float32)
    mstv[np.tril_indices(128, -1)] = -1e30
    m["maskST"] = mstv
    m["ident32"] = np.eye(32, dtype=np.float32)
    s32 = np.zeros((32, 32 * 128), np.float32)
    for h in range(32):
        s32[h, h * 128:(h + 1) * 128] = 1.0
    m["sel32"] = s32
    return m


def _sharedD(ins):
    m = {}
    m["WrT"] = np.ascontiguousarray(ins["L1_ffn_W_r"].T)
    m["WkT"] = np.ascontiguousarray(ins["L1_ffn_W_k"].T)
    m["WvT"] = np.ascontiguousarray(ins["L1_ffn_W_v"].T)
    m["mur"] = np.ascontiguousarray(ins["L1_ffn_mu_r"].reshape(8, 128).T)
    m["muk"] = np.ascontiguousarray(ins["L1_ffn_mu_k"].reshape(8, 128).T)
    return m


def _gather(res, cores):
    out = np.empty((4, 4096, 1024), dtype=np.float32)
    for i, c in enumerate(cores):
        b, half = c // 2, c % 2
        out[b, half * 2048:(half + 1) * 2048] = np.asarray(res.results[i]["yT"]).T
    return out


_NC_CACHE = {}


def _get_nc(name, builder):
    if name not in _NC_CACHE:
        _NC_CACHE[name] = builder()
    return _NC_CACHE[name]


def _fused_maps(ins, x, cores):
    consts = host_constants()
    sB = _sharedB(ins)
    sC = _sharedC(ins)
    sD = _sharedD(ins)
    maps = []
    for c in cores:
        b, half = c // 2, c % 2
        m = dict(consts)
        m.update(sB); m.update(sC); m.update(sD)
        m["xT_full"] = _roll_xT(x[b], half)
        m["wq"] = np.ascontiguousarray(ins["L0_attn_Wq"])
        m["wk"] = np.ascontiguousarray(ins["L0_attn_Wk"])
        m["wo"] = np.ascontiguousarray(ins["L0_attn_Wo"])
        m["bo"] = np.ascontiguousarray(ins["L0_attn_bo"].reshape(8, 128).T)
        m["WrT"] = sD["WrT"]
        m["prev_col"] = np.full((64, 1), float(half), np.float32)
        m["prev128"] = np.full((128, 1), float(half), np.float32)
        maps.append(m)
    return maps


def kernel(**inputs):
    ins = inputs
    cores = list(range(8))
    x = np.ascontiguousarray(ins["L0_attn_x"], dtype=np.float32)
    nc = _get_nc("F", build_fused)
    res = run_bass_kernel_spmd(nc, _fused_maps(ins, x, cores), core_ids=cores)
    return _gather(res, cores)


def kernel_multilaunch(**inputs):

    ins = inputs
    consts = host_constants()
    cores = list(range(8))
    x = np.ascontiguousarray(ins["L0_attn_x"], dtype=np.float32)

    ncA = _get_nc("A", build_A)
    res = run_bass_kernel_spmd(ncA, [_hostA(consts, ins, x, c) for c in cores],
                               core_ids=cores)
    x = _gather(res, cores)

    ncB = _get_nc("B", build_B)
    sB = _sharedB(ins)
    maps = []
    for c in cores:
        b, half = c // 2, c % 2
        m = dict(consts); m.update(sB)
        m["xT"] = np.ascontiguousarray(x[b, half * 2048:(half + 1) * 2048].T)
        maps.append(m)
    res = run_bass_kernel_spmd(ncB, maps, core_ids=cores)
    x = _gather(res, cores)

    ncC = _get_nc("C", build_C)
    sC = _sharedC(ins)
    maps = []
    for c in cores:
        b, half = c // 2, c % 2
        own = x[b, half * 2048:(half + 1) * 2048]
        if half == 0:
            full = np.concatenate([np.zeros((3, 1024), np.float32), own], 0)
        else:
            full = np.concatenate([x[b, 2048 - 3:2048], own], 0)
        m = dict(consts); m.update(sC)
        m["xTc"] = np.ascontiguousarray(full.T)
        m["prev_col"] = np.full((64, 1), float(half), np.float32)
        maps.append(m)
    res = run_bass_kernel_spmd(ncC, maps, core_ids=cores)
    x = _gather(res, cores)

    ncD = _get_nc("D", build_D)
    sD = _sharedD(ins)
    maps = []
    for c in cores:
        b, half = c // 2, c % 2
        own = x[b, half * 2048:(half + 1) * 2048]
        if half == 0:
            full = np.concatenate([np.zeros((1, 1024), np.float32), own], 0)
        else:
            full = np.concatenate([x[b, 2048 - 1:2048], own], 0)
        m = dict(consts); m.update(sD)
        m["xTd"] = np.ascontiguousarray(full.T)
        maps.append(m)
    res = run_bass_kernel_spmd(ncD, maps, core_ids=cores)
    return _gather(res, cores)
